# version 14
# baseline (speedup 1.0000x reference)
import numpy as np
import ml_dtypes
from contextlib import ExitStack

import concourse.bass as bass
import concourse.mybir as mybir
from concourse.bass_utils import run_bass_kernel_spmd

F32 = mybir.dt.float32
BF16 = mybir.dt.bfloat16
AF = mybir.ActivationFunctionType
ALU = mybir.AluOpType
AX = mybir.AxisListType
BFD = ml_dtypes.bfloat16

NCORES = 8
S = 8192
D = 2048
T = 1024
HALO = 512
TH = T + HALO
NH = 16
DH = 128
NDC = D // 128
W_SWA = 512
W_OM = 64
DECAY = 0.9
GGH = 64
MEMF = 8192
FFH = 5461
FFHP = 5504
DFF = 5632
NEXP = 8
R = 16
SCALING = 2.0
EPS_RMS = 1e-6
EPS_LN = 1e-5
SCALE = DH ** -0.5
NEG = -1e30


ENGINE_OF = {"T": "tensor", "V": "vector", "S": "scalar", "Sd": "scalar",
             "G": "gpsimd", "Gd": "gpsimd", "Yd": "sync"}

NSLOT = 6

class Q:
    def __init__(self, name, step):
        self.name = name
        self.step = step
        self.is_dma = name.endswith("d")
        if self.is_dma:
            self.slot_cnt = [0] * NSLOT
            self.issue = 0
        else:
            self.cnt = 0

class Buf:
    __slots__ = ("t", "w", "readers", "name")
    def __init__(self, t, name=""):
        self.t = t
        self.name = name
        self.w = None
        self.readers = {}
    def __getitem__(self, idx):
        return self.t[idx]
    @property
    def ap(self):
        return self.t[:]

class Prog:
    def __init__(self, nc, es):
        self.nc = nc
        self.es = es
        self.queues = {n: Q(n, 16 if n.endswith("d") else 1)
                       for n in ("T", "V", "S", "G", "Yd", "Gd", "Sd")}
        self.sems = {}
        for n, q in self.queues.items():
            if q.is_dma:
                for s in range(NSLOT):
                    self.sems[f"{n}{s}"] = es.enter_context(nc.semaphore(f"sem_{n}{s}"))
            else:
                self.sems[n] = es.enter_context(nc.semaphore(f"sem_{n}"))
        self.streams = {e: [] for e in ("tensor", "vector", "scalar", "gpsimd", "sync")}
        self.hw = {e: {} for e in self.streams}

    def sb(self, name, shape, dt, es=None):
        return Buf((es or self.es).enter_context(
            self.nc.sbuf_tensor(name, list(shape), dt)), name)

    def ps(self, name, shape, dt=F32, es=None):
        return Buf((es or self.es).enter_context(
            self.nc.psum_tensor(name, list(shape), dt)), name)

    def dram(self, name, shape, dt):
        return Buf(self.nc.dram_tensor(name, list(shape), dt), name)

    def _edges(self, qname, reads, writes, skip_self_T=True):
        eng = ENGINE_OF[qname]
        need = {}
        def upd(e):
            if e is None:
                return
            sk, c = e
            if sk == "T" and qname == "T" and skip_self_T:
                return
            if need.get(sk, -1) < c:
                need[sk] = c
        for b in reads:
            upd(b.w)
        for b in writes:
            upd(b.w)
            for sk, c in b.readers.items():
                upd((sk, c))
        hw = self.hw[eng]
        out = []
        for sk in sorted(need):
            c = need[sk]
            if c <= 0 or hw.get(sk, 0) >= c:
                continue
            out.append((sk, c))
            hw[sk] = c
        return out

    def _push(self, qname, waits, emit, incsem):
        self.streams[ENGINE_OF[qname]].append((waits, emit, incsem))

    def _record(self, semkey, val, reads, writes):
        for b in writes:
            b.w = (semkey, val)
            b.readers = {}
        for b in reads:
            if b.readers.get(semkey, -1) < val:
                b.readers[semkey] = val

    def op(self, qname, emit, reads=(), writes=()):
        q = self.queues[qname]
        assert not q.is_dma
        waits = self._edges(qname, reads, writes)
        q.cnt += 1
        self._push(qname, waits, emit, (qname, 1))
        self._record(qname, q.cnt, reads, writes)

    def dma(self, qname, dst_ap, src_ap, reads=(), writes=()):
        q = self.queues[qname]
        assert q.is_dma
        slot = q.issue % NSLOT
        q.issue += 1
        semkey = f"{qname}{slot}"
        waits = []
        prior = q.slot_cnt[slot] * 16
        eng = ENGINE_OF[qname]
        if prior > 0 and self.hw[eng].get(semkey, 0) < prior:
            waits.append((semkey, prior))
            self.hw[eng][semkey] = prior
        waits += self._edges(qname, reads, writes)
        q.slot_cnt[slot] += 1
        self._push(qname, waits,
                   lambda e: e.dma_start(dst_ap, src_ap), (semkey, 16))
        self._record(semkey, q.slot_cnt[slot] * 16, reads, writes)

    def mm(self, out_ap, lhsT_ap, rhs_ap, psum_buf, reads=(), start=True, stop=True):
        self.op("T", lambda e: e.matmul(out_ap, lhsT_ap, rhs_ap, start=start, stop=stop),
                reads=list(reads), writes=[psum_buf])

    def mm_group(self, psum_buf, mms, reads=(), start=True, stop=True):
        waits = self._edges("T", list(reads), [psum_buf])
        q = self.queues["T"]
        q.cnt += 1
        n = len(mms)
        def emit(e, mms=tuple(mms), n=n, start=start, stop=stop):
            ins = None
            for i, (o, l, r) in enumerate(mms):
                ins = e.matmul(o, l, r, start=(start and i == 0),
                               stop=(stop and i == n - 1))
            return ins
        self._push("T", waits, emit, ("T", 1))
        self._record("T", q.cnt, list(reads), [psum_buf])

    def _cur_semvals(self):
        vals = {}
        for n, q in self.queues.items():
            if q.is_dma:
                for s in range(NSLOT):
                    if q.slot_cnt[s] > 0:
                        vals[f"{n}{s}"] = q.slot_cnt[s] * 16
            else:
                if q.cnt > 0:
                    vals[n] = q.cnt
        return vals

    def wait_all(self, qname):
        eng = ENGINE_OF[qname]
        hw = self.hw[eng]
        waits = []
        own = {qname} if not self.queues[qname].is_dma else \
            {f"{qname}{s}" for s in range(NSLOT)}
        for sk, c in sorted(self._cur_semvals().items()):
            if sk in own:
                continue
            if hw.get(sk, 0) < c:
                waits.append((sk, c))
                hw[sk] = c
        if waits:
            self._push(qname, waits, None, None)

    def barrier(self, qnames=("T", "V", "S", "Yd", "Gd", "Sd")):
        for qa in qnames:
            self.wait_all(qa)

    def finalize(self, block):
        def mk(engname):
            items = self.streams[engname]
            def body(eng):
                for waits, emit, incsem in items:
                    for sk, c in waits:
                        eng.wait_ge(self.sems[sk], c)
                    if emit is not None:
                        ins = emit(eng)
                        if incsem is not None:
                            ins.then_inc(self.sems[incsem[0]], incsem[1])
            return body
        block.tensor(mk("tensor"))
        block.vector(mk("vector"))
        block.scalar(mk("scalar"))
        block.gpsimd(mk("gpsimd"))
        block.sync(mk("sync"))



def _rope_tables():
    inv_freq = 1.0 / (10000.0 ** (np.arange(0, DH, 2, dtype=np.float64) / DH))
    t = np.arange(-HALO, S, dtype=np.float64)
    freqs = t[:, None] * inv_freq[None, :]
    cos = np.cos(freqs)
    sin = np.sin(freqs)
    cosT = np.concatenate([cos.T, cos.T], axis=0)
    sinT = np.concatenate([-sin.T, sin.T], axis=0)
    return cosT, sinT

def _masks(core):
    i = np.arange(128)[:, None]
    j = np.arange(128)[None, :]
    m_far = np.where(j > i, 0.0, NEG)
    m_causal = np.where(j <= i, 0.0, NEG)
    rel4 = i - j
    rel3 = i + 128 - j
    m_om4 = np.where((rel4 >= 0) & (rel4 < W_OM), 0.0, NEG)
    m_om3 = np.where((rel3 >= 0) & (rel3 < W_OM), 0.0, NEG)
    d_om4 = np.where((rel4 >= 0) & (rel4 < W_OM),
                     DECAY ** np.clip(rel4, 0, 255), 0.0)
    d_om3 = np.where((rel3 >= 0) & (rel3 < W_OM),
                     DECAY ** np.clip(rel3, 0, 255), 0.0)
    swa_cat = np.zeros((8, 128, 640))
    om_cat = np.zeros((8, 128, 256))
    for qi in range(8):
        for b in range(5):
            blk = np.zeros((128, 128))
            if b == 0:
                blk = blk + m_far
            elif b == 4:
                blk = blk + m_causal
            if core == 0 and (qi + b) < 4:
                blk = blk + NEG
            swa_cat[qi, :, 128*b:128*(b+1)] = blk
        om_cat[qi, :, 0:128] = m_om3 + (NEG if (core == 0 and qi == 0) else 0.0)
        om_cat[qi, :, 128:256] = m_om4
    np.clip(swa_cat, -3e38, 0, out=swa_cat)
    np.clip(om_cat, -3e38, 0, out=om_cat)
    d_cat = np.concatenate([d_om3, d_om4], axis=1)
    return dict(
        swa_cat=np.ascontiguousarray(
            swa_cat.transpose(1, 0, 2).reshape(128, 8 * 640)).astype(BFD),
        om_cat=np.ascontiguousarray(
            om_cat.transpose(1, 0, 2).reshape(128, 8 * 256)).astype(BFD),
        d_cat=d_cat.astype(np.float32))

def prep_inputs(inp):
    x = np.asarray(inp["L0_attn_x"])[0]
    shared = {}
    wqkv = np.asarray(inp["L0_attn_w_qkv"])
    cols = []
    for h in range(NH):
        cols.append(wqkv[:, h*DH:(h+1)*DH])
        cols.append(wqkv[:, D + h*DH:D + (h+1)*DH])
        cols.append(wqkv[:, 2*D + h*DH:2*D + (h+1)*DH])
    shared["wqkv_p"] = np.ascontiguousarray(np.concatenate(cols, axis=1))
    shared["qscale"] = np.ascontiguousarray(np.asarray(inp["L0_attn_q_scale"]).reshape(DH, 1))
    shared["kscale"] = np.ascontiguousarray(np.asarray(inp["L0_attn_k_scale"]).reshape(DH, 1))
    shared["n1w"] = np.ascontiguousarray(np.asarray(inp["L0_attn_norm1_w"]).reshape(NDC, 128).T)
    shared["n2w"] = np.ascontiguousarray(np.asarray(inp["L0_attn_norm2_w"]).reshape(NDC, 128).T)
    shared["gg1"] = np.asarray(inp["L0_attn_gg_w1"])
    shared["gg2"] = np.asarray(inp["L0_attn_gg_w2"])
    shared["wo"] = np.asarray(inp["L0_attn_w_o"])
    shared["mw1"] = np.asarray(inp["L0_attn_mem_w1"])
    shared["mw2"] = np.asarray(inp["L0_attn_mem_w2"])
    z = np.zeros((D, FFHP - FFH), np.float32)
    shared["fw1"] = np.ascontiguousarray(np.concatenate([np.asarray(inp["L0_attn_ffn_w1"]), z], axis=1))
    shared["fw3"] = np.ascontiguousarray(np.concatenate([np.asarray(inp["L0_attn_ffn_w3"]), z], axis=1))
    shared["fw2"] = np.ascontiguousarray(np.concatenate(
        [np.asarray(inp["L0_attn_ffn_w2"]), np.zeros((FFHP - FFH, D), np.float32)], axis=0))
    shared["wg"] = np.ascontiguousarray(np.asarray(inp["L0_ffn_Wg"]).T)
    shared["wu"] = np.ascontiguousarray(np.asarray(inp["L0_ffn_Wu"]).T)
    shared["wd"] = np.ascontiguousarray(np.asarray(inp["L0_ffn_Wd"]).T)
    shared["wr"] = np.ascontiguousarray(np.asarray(inp["L0_ffn_Wr"]).T)
    Ag = np.asarray(inp["L0_ffn_Ag"])
    Au = np.asarray(inp["L0_ffn_Au"])
    a_all = np.zeros((D, 8 * 128), np.float32)
    for t in range(4):
        a_all[:, 128*t + 0:128*t + 16] = Ag[2*t].T
        a_all[:, 128*t + 64:128*t + 80] = Ag[2*t + 1].T
        base = 512 + 128*t
        a_all[:, base + 0:base + 16] = Au[2*t].T
        a_all[:, base + 64:base + 80] = Au[2*t + 1].T
    shared["a_all"] = a_all
    shared["bg"] = np.ascontiguousarray(np.asarray(inp["L0_ffn_Bg"]).transpose(0, 2, 1))
    shared["bu"] = np.ascontiguousarray(np.asarray(inp["L0_ffn_Bu"]).transpose(0, 2, 1))
    shared["ad"] = np.ascontiguousarray(np.asarray(inp["L0_ffn_Ad"]).transpose(0, 2, 1))
    shared["bd"] = np.ascontiguousarray(
        np.asarray(inp["L0_ffn_Bd"]).transpose(0, 2, 1).reshape(NEXP * R, D))
    shared["ident"] = np.eye(128, dtype=BFD)
    shared["identf"] = np.eye(128, dtype=np.float32)
    shared["ones"] = np.ones((128, 128), np.float32)
    shared["onesb"] = np.ones((128, 128), dtype=BFD)
    epsb = np.zeros((128, 2), np.float32); epsb[:, 0] = EPS_RMS; epsb[:, 1] = EPS_LN
    shared['epsb'] = epsb

    cosT, sinT = _rope_tables()
    per_core = []
    for c in range(NCORES):
        gs = c * T
        m = {}
        lo = gs - HALO
        if lo < 0:
            xs = np.concatenate([np.zeros((HALO, D), np.float32),
                                 x[0:gs + T]], axis=0)
        else:
            xs = x[lo:gs + T]
        m["x_f"] = np.ascontiguousarray(xs.T)
        m["cosq"] = np.ascontiguousarray(cosT[:, HALO + gs: HALO + gs + T] * SCALE).astype(BFD)
        m["sinq"] = np.ascontiguousarray(sinT[:, HALO + gs: HALO + gs + T] * SCALE).astype(BFD)
        m["cosk"] = np.ascontiguousarray(cosT[:, gs: gs + TH]).astype(BFD)
        m["sink"] = np.ascontiguousarray(sinT[:, gs: gs + TH]).astype(BFD)
        m.update(_masks(c))
        per_core.append(m)
    return shared, per_core



IN_SPECS = [
    ("x_f", [D, TH], F32), ("wqkv_p", [D, 3 * D], F32),
    ("qscale", [DH, 1], F32), ("kscale", [DH, 1], F32),
    ("n1w", [128, NDC], F32), ("n2w", [128, NDC], F32),
    ("gg1", [D, GGH], F32), ("gg2", [GGH, 1], F32),
    ("wo", [D, D], F32), ("mw1", [D, MEMF], F32), ("mw2", [MEMF, D], F32),
    ("fw1", [D, FFHP], F32), ("fw3", [D, FFHP], F32), ("fw2", [FFHP, D], F32),
    ("wg", [D, DFF], F32), ("wu", [D, DFF], F32), ("wd", [DFF, D], F32),
    ("wr", [D, NEXP], F32), ("a_all", [D, 1024], F32),
    ("bg", [NEXP, R, DFF], F32), ("bu", [NEXP, R, DFF], F32),
    ("ad", [NEXP, DFF, R], F32), ("bd", [NEXP * R, D], F32),
    ("ident", [128, 128], BF16), ("identf", [128, 128], F32),
    ("onesb", [128, 128], BF16),
    ("ones", [128, 128], F32),
    ("swa_cat", [128, 8 * 640], BF16), ("om_cat", [128, 8 * 256], BF16),
    ("d_cat", [128, 256], F32),
    ("cosq", [128, T], BF16), ("sinq", [128, T], BF16),
    ("cosk", [128, TH], BF16), ("sink", [128, TH], BF16),
    ("epsb", [128, 2], F32),
]

CONST_NAMES = ["ident", "identf", "ones", "onesb", "n1w", "n2w", "qscale", "kscale", "epsb"]
P1_CONST_NAMES = ["swa_cat", "om_cat", "d_cat",
                  "cosq", "sinq", "cosk", "sink"]

def build(phases=("P0",), debug_taps=()):
    nc = bass.Bass("TRN2", target_bir_lowering=False)
    es = ExitStack()
    p = Prog(nc, es)
    dbg = {}

    I = {nm: Buf(nc.dram_tensor(nm, list(shape), dt, kind="ExternalInput"), nm)
         for nm, shape, dt in IN_SPECS}
    out = Buf(nc.dram_tensor("out", [NDC, 128, T], F32, kind="ExternalOutput"), "out")

    SC = {
        "attn": p.dram("sc_attn", [NH, 128, T], BF16),
        "xa": p.dram("sc_xa", [NDC, 128, T], F32),
        "r": p.dram("sc_r", [NDC, 128, T], F32),
        "x1b": p.dram("sc_x1b", [NDC, 128, T], BF16),
        "x1f": p.dram("sc_x1f", [NDC, 128, T], F32),
        "r2": p.dram("sc_r2", [NDC, 128, T], F32),
    }

    def tap(name, shape, dt=F32):
        if name in debug_taps:
            b = Buf(nc.dram_tensor("dbg_" + name, list(shape), dt,
                                   kind="ExternalOutput"), "dbg_" + name)
            dbg[name] = b
            return b
        return None

    with es:
        C = {}
        for nm in CONST_NAMES:
            shape = dict((n, s) for n, s, _ in IN_SPECS)[nm]
            dt = dict((n, d) for n, s, d in IN_SPECS)[nm]
            C[nm] = p.sb("c_" + nm, shape, dt)
            p.dma("Yd", C[nm].ap, I[nm].ap, writes=[C[nm]])

        es_p03 = es.enter_context(ExitStack())
        rms2ss = p.sb("rms2ss", [1, T], F32, es=es_p03)

        with ExitStack() as es_p02:
            qmem = [p.sb(f"qmem{h}", [128, T], BF16, es=es_p02) for h in range(NH)]
            hbuf = [p.sb(f"h{i}", [128, TH], BF16, es=es_p02) for i in range(NDC)]
            gammaT = p.sb("gammaT", [128, 12], F32, es=es_p02)

            if "P0" in phases:
                _p0(p, nc, I, C, hbuf, gammaT, tap)
            p1sel = [ph for ph in phases if ph.startswith("P1")]
            if p1sel:
                parts = p1sel[0].split(":")
                upto = parts[1] if len(parts) > 1 else "full"
                nh = int(parts[2]) if len(parts) > 2 else NH
                _p1(p, nc, I, C, hbuf, gammaT, qmem, SC, tap,
                    heads=range(nh), upto=upto)

            if "P2" in phases:
                _p2(p, nc, I, C, hbuf, qmem, SC, rms2ss, tap)

        if "P3" in phases:
            _p3(p, nc, I, C, SC, rms2ss, tap)
        es_p03.close()

        ln2s = p.sb("ln2s", [1, 2 * T], F32)
        if "P4" in phases:
            _p4(p, nc, I, C, SC, ln2s, tap)
        if "P5" in phases:
            _p5(p, nc, I, C, SC, ln2s, out, tap)

        if "OUT_STUB" in phases:
            with ExitStack() as e2:
                z = p.sb("zstub", [128, D], F32, es=e2)
                p.op("V", lambda e: e.memset(z.ap, 0.0), writes=[z])
                for i in range(T // 128):
                    p.dma("Yd", out.t[i * 128:(i + 1) * 128, :], z.ap,
                          reads=[z], writes=[out])

        p.barrier()

        with nc.Block() as block:
            p.finalize(block)
    return nc, dbg



def _p0(p, nc, I, C, hbuf, gammaT, tap):
    with ExitStack() as e0:
        xs = [p.sb(f"p0x{i}", [128, TH], F32, es=e0) for i in range(2)]
        sqs = [p.sb(f"p0sq{i}", [128, TH], BF16, es=e0) for i in range(2)]
        rstd_row = p.sb("p0rstd", [1, TH], F32, es=e0)
        rstd_sb = p.sb("p0rbs", [128, TH], F32, es=e0)
        gg1b = p.sb("p0gg1", [128, NDC * GGH], BF16, es=e0)
        gg2b = p.sb("p0gg2", [GGH, 1], BF16, es=e0)
        gsil = p.sb("p0gs", [GGH, TH], BF16, es=e0)

        p.dma("Gd", gg1b.t[:].rearrange("p (k c) -> p k c", c=GGH),
              I["gg1"].ap.rearrange("(k p) c -> p k c", p=128), writes=[gg1b])
        p.dma("Gd", gg2b.ap, I["gg2"].ap, writes=[gg2b])

        with ExitStack() as eA:
            ssum = p.ps("p0ss", [1, TH], F32, es=eA)
            rstd_b = p.ps("p0rb", [128, TH], F32, es=eA)
            for i in range(NDC):
                xb, sq = xs[i % 2], sqs[i % 2]
                p.dma("Yd", xb.ap, I["x_f"].t[i * 128:(i + 1) * 128, :], writes=[xb])
                p.op("S", lambda e, xb=xb, sq=sq: e.activation(sq.ap, xb.ap, AF.Square),
                     reads=[xb], writes=[sq])
                for t0 in range(0, TH, 512):
                    p.mm(ssum.t[:, t0:t0 + 512], C["onesb"].t[:, 0:1],
                         sq.t[:, t0:t0 + 512], ssum, reads=[C["onesb"], sq],
                         start=(i == 0), stop=(i == NDC - 1))
            p.op("S", lambda e: e.activation(rstd_row.ap, ssum.ap, AF.Ln,
                                             bias=C['epsb'].t[0:1, 0:1], scale=1.0 / D),
                 reads=[ssum], writes=[rstd_row])
            p.op("S", lambda e: e.activation(rstd_row.ap, rstd_row.ap, AF.Exp,
                                             scale=-0.5),
                 reads=[rstd_row], writes=[rstd_row])
            for t0 in range(0, TH, 512):
                p.mm(rstd_b.t[:, t0:t0 + 512], C["ones"].t[0:1, :],
                     rstd_row.t[:, t0:t0 + 512], rstd_b,
                     reads=[C["ones"], rstd_row], start=True, stop=True)
            p.op("V", lambda e: e.tensor_copy(rstd_sb.ap, rstd_b.ap),
                 reads=[rstd_b], writes=[rstd_sb])
            for i in range(NDC):
                xb = xs[i % 2]
                p.dma("Yd", xb.ap, I["x_f"].t[i * 128:(i + 1) * 128, :], writes=[xb])
                p.op("V", lambda e, i=i, xb=xb: e.scalar_tensor_tensor(
                        hbuf[i].ap, xb.ap, C["n1w"].t[:, i:i + 1], rstd_sb.ap,
                        ALU.mult, ALU.mult),
                     reads=[xb, C["n1w"], rstd_sb], writes=[hbuf[i]])
            p.barrier()
        with ExitStack() as eB:
            gpre = p.ps("p0gp", [GGH, TH], F32, es=eB)
            grow = p.ps("p0gr", [128, 12], F32, es=eB)
            gg1v = gg1b.t[:].rearrange("p (k c) -> p k c", c=GGH)
            for t0 in range(0, TH, 512):
                p.mm_group(gpre,
                           [(gpre.t[:, t0:t0 + 512], gg1v[:, i, :],
                             hbuf[i].t[:, t0:t0 + 512]) for i in range(NDC)],
                           reads=[gg1b] + hbuf)
            p.op("S", lambda e: e.activation(gsil.ap, gpre.ap, AF.Silu),
                 reads=[gpre], writes=[gsil])
            for j in range(12):
                p.mm(grow.t[:, j:j + 1], gsil.t[:, j * 128:(j + 1) * 128], gg2b.ap,
                     grow, reads=[gsil, gg2b], start=True, stop=True)
            p.op("S", lambda e: e.activation(gammaT.ap, grow.ap, AF.Sigmoid),
                 reads=[grow], writes=[gammaT])

            tp = tap("h", [NDC, 128, TH], BF16)
            if tp is not None:
                for i in range(NDC):
                    p.dma("Yd", tp.t[i, :, :], hbuf[i].ap, reads=[hbuf[i]], writes=[tp])
            tp = tap("gammaT", [128, 12])
            if tp is not None:
                p.dma("Yd", tp.ap, gammaT.ap, reads=[gammaT], writes=[tp])
            p.barrier()


def _p1(p, nc, I, C, hbuf, gammaT, qmem, SC, tap, heads=None, upto="full"):
    with ExitStack() as e1:
        ispec = {nm: (s, d) for nm, s, d in IN_SPECS}
        for nm in P1_CONST_NAMES:
            C[nm] = p.sb("c_" + nm, ispec[nm][0], ispec[nm][1], es=e1)
            p.dma("Yd", C[nm].ap, I[nm].ap, writes=[C[nm]])
        wb = [p.sb(f"p1w{i}", [128, NDC * 384], BF16, es=e1) for i in range(2)]
        q_work = p.sb("p1qw", [128, T], F32, es=e1)
        k_work = p.sb("p1kw", [128, TH], F32, es=e1)
        row = p.sb("p1row", [1, 512], F32, es=e1)
        q1 = p.sb("p1q1", [128, T], F32, es=e1)
        k1 = p.sb("p1k1", [128, TH], F32, es=e1)
        t1 = p.sb("p1t1", [128, TH], F32, es=e1)
        t2 = p.sb("p1t2", [128, TH], F32, es=e1)
        sq = p.sb("p1sqb", [128, 512], BF16, es=e1)
        qr = t2
        kr = t2
        q2 = p.sb("p1q2", [128, T], BF16, es=e1)
        k2 = p.sb("p1k2", [128, TH], BF16, es=e1)
        vts = [p.sb(f"p1vt{i}", [128, 12 * 128], BF16, es=e1) for i in range(2)]
        kpts = [p.sb(f"p1kpt{i}", [128, 12 * 128], BF16, es=e1) for i in range(2)]
        sms_ = [p.sb(f"p1sm{i}", [128, 640], F32, es=e1) for i in range(2)]
        pns = [p.sb(f"p1pn{i}", [128, 640], BF16, es=e1) for i in range(2)]
        oms = [p.sb(f"p1om{i}", [128, 256], F32, es=e1) for i in range(2)]
        poms = [p.sb(f"p1pom{i}", [128, 256], BF16, es=e1) for i in range(2)]
        pTs = [p.sb(f"p1pT{i}", [128, 896], BF16, es=e1) for i in range(2)]
        attnhs = [p.sb(f"p1ah{i}", [128, T], BF16, es=e1) for i in range(2)]
        mcols = [p.sb(f"p1m{i}", [128, 1], F32, es=e1) for i in range(2)]
        sumcs = [p.sb(f"p1su{i}", [128, 1], F32, es=e1) for i in range(2)]
        osums = [p.sb(f"p1os{i}", [128, 1], F32, es=e1) for i in range(2)]
        recs = [p.sb(f"p1re{i}", [128, 1], F32, es=e1) for i in range(2)]
        orecs = [p.sb(f"p1or{i}", [128, 1], F32, es=e1) for i in range(2)]
        PJ = [p.ps(f"p1pj{i}", [128, 512], F32, es=e1) for i in range(2)]
        PS = [p.ps(f"p1ps{i}", [128, 640], F32, es=e1) for i in range(2)]
        TP = p.ps("p1ptp", [128, 896], BF16, es=e1)
        PO = p.ps("p1po", [128, 512], F32, es=e1)
        ROW = PJ[0]
        BC = PJ[1]

        def wdma(hd, dst):
            src = I["wqkv_p"].t[:, 384 * hd:384 * (hd + 1)].rearrange(
                "(k p) c -> p k c", p=128)
            p.dma("Gd", dst.t[:].rearrange("p (k c) -> p k c", c=384), src,
                  writes=[dst])

        wdma(0, wb[0])
        tap_q2 = tap("q2h0", [128, T], BF16)
        tap_k2 = tap("k2h0", [128, TH], BF16)
        tap_attn = tap("attn", [NH, 128, T], BF16)
        tap_qm = tap("qmemt", [NH, 128, T], BF16)

        def proj_items(hd):
            wcur = wb[hd % 2]
            wv = wcur.t[:].rearrange("p (k c) -> p k c", c=384)
            vt = vts[hd % 2]
            items = []
            pjc = [0]
            def nxt():
                b = PJ[pjc[0] % 2]
                pjc[0] += 1
                return b
            for tt in range(2):
                def it(tt=tt):
                    PJb = nxt()
                    p.mm_group(PJb, [(PJb.ap, wv[:, k, 0:128],
                                      hbuf[k].t[:, HALO + 512 * tt:HALO + 512 * (tt + 1)])
                                     for k in range(NDC)], reads=[wcur] + hbuf)
                    p.op("S", lambda e, tt=tt, PJb=PJb: e.copy(
                            q_work.t[:, 512 * tt:512 * (tt + 1)], PJb.ap),
                         reads=[PJb], writes=[q_work])
                items.append(it)
            for tt in range(3):
                def it(tt=tt):
                    PJb = nxt()
                    p.mm_group(PJb, [(PJb.ap, wv[:, k, 128:256],
                                      hbuf[k].t[:, 512 * tt:512 * (tt + 1)])
                                     for k in range(NDC)], reads=[wcur] + hbuf)
                    p.op("S", lambda e, tt=tt, PJb=PJb: e.copy(
                            k_work.t[:, 512 * tt:512 * (tt + 1)], PJb.ap),
                         reads=[PJb], writes=[k_work])
                items.append(it)
            for j in range(12):
                def it(j=j):
                    PJb = nxt()
                    p.mm_group(PJb, [(PJb.t[:, 0:128],
                                      hbuf[k].t[:, 128 * j:128 * (j + 1)],
                                      wv[:, k, 256:384]) for k in range(NDC)],
                               reads=[wcur] + hbuf)
                    p.op("S", lambda e, j=j, PJb=PJb, vt=vt: e.copy(
                            vt.t[:, 128 * j:128 * (j + 1)], PJb.t[:, 0:128]),
                         reads=[PJb], writes=[vt])
                items.append(it)
            return items

        def proj(hd):
            for it in proj_items(hd):
                it()

        hlist = list(heads if heads is not None else range(NH))
        proj(hlist[0])
        if len(hlist) > 1:
            wdma(hlist[1], wb[hlist[1] % 2])
        for hd in hlist:
            vt = vts[hd % 2]
            kpt = kpts[hd % 2]
            attnh = attnhs[hd % 2]

            def headnorm(work, scale_c, out_b, n):
                for tt in range(n // 512):
                    p.op("S", lambda e, tt=tt: e.activation(
                            sq.ap, work.t[:, 512 * tt:512 * (tt + 1)], AF.Square),
                         reads=[work], writes=[sq])
                    p.mm(ROW.t[0:1, 0:512], C["onesb"].t[:, 0:1],
                         sq.ap, ROW, reads=[C["onesb"], sq])
                    p.op("S", lambda e: e.activation(
                            row.ap, ROW.t[0:1, 0:512],
                            AF.Ln, bias=C["epsb"].t[0:1, 0:1], scale=1.0 / DH),
                         reads=[ROW, C["epsb"]], writes=[row])
                    p.op("S", lambda e: e.activation(row.ap, row.ap,
                                                     AF.Exp, scale=-0.5),
                         reads=[row], writes=[row])
                    p.mm(BC.t[:, 0:512], C["ones"].t[0:1, :], row.ap,
                         BC, reads=[C["ones"], row])
                    p.op("V", lambda e, tt=tt: e.scalar_tensor_tensor(
                            out_b.t[:, 512 * tt:512 * (tt + 1)],
                            work.t[:, 512 * tt:512 * (tt + 1)],
                            C[scale_c].ap, BC.t[:, 0:512], ALU.mult, ALU.mult),
                         reads=[work, C[scale_c], BC], writes=[out_b])
            if hd == 0:
                tpx = tap("qworkh0", [128, T], F32)
                if tpx is not None:
                    p.dma("Yd", tpx.ap, q_work.ap, reads=[q_work], writes=[tpx])
            if upto == "qwork":
                continue
            headnorm(q_work, "qscale", q1, T)
            if hd == 0:
                tpx = tap("q1h0", [128, T], F32)
                if tpx is not None:
                    p.dma("Yd", tpx.ap, q1.ap, reads=[q1], writes=[tpx])
                tpx = tap("rowh0", [1, T], F32)
                if tpx is not None:
                    p.dma("Yd", tpx.ap, row.t[:, 0:T], reads=[row], writes=[tpx])
            headnorm(k_work, "kscale", k1, TH)

            def rope(src, rot, cos_c, sin_c, out_b, n):
                p.op("S", lambda e: e.copy(rot.t[0:64, 0:n], src.t[64:128, 0:n]),
                     reads=[src], writes=[rot])
                p.op("S", lambda e: e.copy(rot.t[64:128, 0:n], src.t[0:64, 0:n]),
                     reads=[src], writes=[rot])
                p.op("V", lambda e: e.tensor_tensor(t1.t[:, 0:n], src.t[:, 0:n],
                                                    C[cos_c].t[:, 0:n], ALU.mult),
                     reads=[src, C[cos_c]], writes=[t1])
                p.op("V", lambda e: e.tensor_tensor(t2.t[:, 0:n], rot.t[:, 0:n],
                                                    C[sin_c].t[:, 0:n], ALU.mult),
                     reads=[rot, C[sin_c]], writes=[t2])
                p.op("V", lambda e: e.tensor_tensor(out_b.t[:, 0:n], t1.t[:, 0:n],
                                                    t2.t[:, 0:n], ALU.add),
                     reads=[t1, t2], writes=[out_b])
            if upto == "norm":
                continue
            rope(q1, qr, "cosq", "sinq", q2, T)
            rope(k1, kr, "cosk", "sink", k2, TH)

            for j in range(12):
                sl = 128 * (j % 7)
                p.op("T", lambda e, j=j, sl=sl: e.transpose(
                        TP.t[:, sl:sl + 128], k2.t[:, 128 * j:128 * (j + 1)],
                        C["ident"].ap),
                     reads=[k2, C["ident"]], writes=[TP])
                p.op("S", lambda e, j=j, sl=sl, kpt=kpt: e.activation(
                        kpt.t[:, 128 * j:128 * (j + 1)], TP.t[:, sl:sl + 128],
                        AF.Copy, scale=gammaT.t[:, j:j + 1]),
                     reads=[TP, gammaT], writes=[kpt])

            nxt_items = []
            if hd + 1 < NH:
                nxt_items = proj_items(hd + 1)
                if hd + 2 < NH:
                    wdma(hd + 2, wb[(hd + 2) % 2])

            def stA(qi):
                ks = 128 * qi
                par = qi % 2
                PSb = PS[par]
                p.mm(PSb.t[:, 0:512], q2.t[:, ks:ks + 128],
                     k2.t[:, ks:ks + 512], PSb, reads=[q2, k2])
                p.mm(PSb.t[:, 512:640], q2.t[:, ks:ks + 128],
                     k2.t[:, ks + 512:ks + 640], PSb, reads=[q2, k2])

            def stB(qi):
                par = qi % 2
                PSb, smb, pnb, omb, pomb = PS[par], sms_[par], pns[par], oms[par], poms[par]
                mcol, sumc, osum, rec, orec = (mcols[par], sumcs[par], osums[par],
                                               recs[par], orecs[par])
                p.op("V", lambda e, qi=qi, PSb=PSb, smb=smb: e.tensor_tensor(
                        smb.ap, PSb.ap, C["swa_cat"].t[:, 640 * qi:640 * (qi + 1)],
                        ALU.add),
                     reads=[PSb, C["swa_cat"]], writes=[smb])
                p.op("V", lambda e, smb=smb, mcol=mcol: e.reduce_max(
                        mcol.ap, smb.ap, axis=AX.X, negate=True),
                     reads=[smb], writes=[mcol])
                p.op("S", lambda e, smb=smb, pnb=pnb, mcol=mcol, sumc=sumc:
                     e.activation(pnb.ap, smb.ap, AF.Exp, bias=mcol.ap,
                                  accum_out=sumc.ap),
                     reads=[smb, mcol], writes=[pnb, sumc])
                p.op("V", lambda e, rec=rec, sumc=sumc: e.reciprocal(rec.ap, sumc.ap),
                     reads=[sumc], writes=[rec])
                p.op("V", lambda e, pnb=pnb, rec=rec: e.tensor_scalar(
                        pnb.ap, pnb.ap, rec.ap, None, ALU.mult),
                     reads=[pnb, rec], writes=[pnb])
                p.op("V", lambda e, qi=qi, smb=smb, omb=omb: e.tensor_tensor(
                        omb.ap, smb.t[:, 384:640],
                        C["om_cat"].t[:, 256 * qi:256 * (qi + 1)], ALU.add),
                     reads=[smb, C["om_cat"]], writes=[omb])
                p.op("S", lambda e, omb=omb, pomb=pomb, mcol=mcol, osum=osum:
                     e.activation(pomb.ap, omb.ap, AF.Exp, bias=mcol.ap,
                                  accum_out=osum.ap),
                     reads=[omb, mcol], writes=[pomb, osum])
                p.op("V", lambda e, orec=orec, osum=osum: e.reciprocal(
                        orec.ap, osum.ap),
                     reads=[osum], writes=[orec])
                p.op("V", lambda e, pomb=pomb, orec=orec: e.scalar_tensor_tensor(
                        pomb.ap, pomb.ap, orec.ap, C["d_cat"].ap,
                        ALU.mult, ALU.mult),
                     reads=[pomb, orec, C["d_cat"]], writes=[pomb])

            def stC(qi, hd):
                ks = 128 * qi
                par = qi % 2
                pnb, pomb, pTb = pns[par], poms[par], pTs[par]
                for b in range(5):
                    p.op("T", lambda e, b=b, pnb=pnb: e.transpose(
                            TP.t[:, 128 * b:128 * (b + 1)],
                            pnb.t[:, 128 * b:128 * (b + 1)], C["ident"].ap),
                         reads=[pnb, C["ident"]], writes=[TP])
                for oi in range(2):
                    p.op("T", lambda e, oi=oi, pomb=pomb: e.transpose(
                            TP.t[:, 640 + 128 * oi:640 + 128 * (oi + 1)],
                            pomb.t[:, 128 * oi:128 * (oi + 1)], C["ident"].ap),
                         reads=[pomb, C["ident"]], writes=[TP])
                p.op("V", lambda e, pTb=pTb: e.tensor_copy(pTb.ap, TP.ap),
                     reads=[TP], writes=[pTb])
                o0 = 256 * par
                p.mm_group(PO, [(PO.t[:, o0:o0 + 128],
                                 vt.t[:, 128 * (qi + b):128 * (qi + b + 1)],
                                 pTb.t[:, 128 * b:128 * (b + 1)]) for b in range(5)],
                           reads=[vt, pTb])
                p.mm_group(PO, [(PO.t[:, o0 + 128:o0 + 256],
                                 kpt.t[:, 128 * (qi + b):128 * (qi + b + 1)],
                                 pTb.t[:, 640 + 128 * (b - 3):640 + 128 * (b - 2)])
                                for b in (3, 4)],
                           reads=[kpt, pTb])
                p.op("S", lambda e, ks=ks, o0=o0, attnh=attnh: e.copy(
                        attnh.t[:, ks:ks + 128], PO.t[:, o0:o0 + 128]),
                     reads=[PO], writes=[attnh])
                p.op("S", lambda e, ks=ks, hd=hd, o0=o0: e.copy(
                        qmem[hd].t[:, ks:ks + 128], PO.t[:, o0 + 128:o0 + 256]),
                     reads=[PO], writes=[qmem[hd]])

            stA(0)
            stA(1)
            stB(0)
            for qi in range(8):
                if qi + 2 < 8:
                    stA(qi + 2)
                stC(qi, hd)
                for _ in range(3 if qi < 5 else 1):
                    if nxt_items:
                        nxt_items.pop(0)()
                if qi + 1 < 8:
                    stB(qi + 1)
            while nxt_items:
                nxt_items.pop(0)()

            p.dma("Yd", SC["attn"].t[hd, :, :], attnh.ap,
                  reads=[attnh], writes=[SC["attn"]])
            if tap_attn is not None:
                p.dma("Yd", tap_attn.t[hd, :, :], attnh.ap,
                      reads=[attnh], writes=[tap_attn])
            if tap_qm is not None:
                p.dma("Yd", tap_qm.t[hd, :, :], qmem[hd].ap,
                      reads=[qmem[hd]], writes=[tap_qm])
            if hd == 0:
                if tap_q2 is not None:
                    p.dma("Yd", tap_q2.ap, q2.ap, reads=[q2], writes=[tap_q2])
                if tap_k2 is not None:
                    p.dma("Yd", tap_k2.ap, k2.ap, reads=[k2], writes=[tap_k2])
                for nmt, bt, shp, dt in (("vth0", vt, [128, 1536], BF16),
                                         ("kpth0", kpt, [128, 1536], BF16)):
                    tpx = tap(nmt, shp, dt)
                    if tpx is not None:
                        p.dma("Yd", tpx.ap, bt.ap, reads=[bt], writes=[tpx])
        p.barrier()



def _p2(p, nc, I, C, hbuf, qmem, SC, rms2ss, tap):
    tap_m = tap("macc", [NDC, 128, T], F32)
    tap_xa = tap("xa", [NDC, 128, T], F32)
    tap_r = tap("r", [NDC, 128, T], F32)
    with ExitStack() as e2:
        w1b = [p.sb(f"p2w1{i}", [128, NDC * 128], BF16, es=e2) for i in range(2)]
        w2b = [p.sb(f"p2w2{i}", [128, 8 * 128], BF16, es=e2) for i in range(2)]
        wob = [p.sb(f"p2wo{i}", [128, NDC * 128], BF16, es=e2) for i in range(2)]
        macc = [p.sb(f"p2ma{m}", [128, T], F32, es=e2) for m in range(NDC)]
        poly = [p.sb(f"p2po{f}", [128, T], BF16, es=e2) for f in range(8)]
        scr = p.sb("p2scr", [128, 512], F32, es=e2)
        sqb = p.sb("p2sqb", [128, 512], BF16, es=e2)
        sg = p.sb("p2sg", [128, 512], F32, es=e2)
        t2b = sg
        xac = p.sb("p2xa", [128, T], F32, es=e2)
        rc = p.sb("p2rc", [128, T], F32, es=e2)
        xcs = [p.sb(f"p2xc{i}", [128, T], F32, es=e2) for i in range(2)]
        PH = [p.ps(f"p2ph{i}", [128, 512], F32, es=e2) for i in range(2)]
        PW = [p.ps(f"p2pw{i}", [128, 512], F32, es=e2) for i in range(2)]
        PA = [p.ps(f"p2pa{i}", [128, 512], F32, es=e2) for i in range(2)]
        RS = p.ps("p2rs", [1, T], F32, es=e2)

        def ldw1(idx, dst):
            p.dma("Gd", dst.t[:].rearrange("p (k c) -> p k c", c=128),
                  I["mw1"].t[:, 128 * idx:128 * (idx + 1)]
                  .rearrange("(k p) c -> p k c", p=128), writes=[dst])

        def ldw2(win, m, dst):
            p.dma("Gd", dst.t[:].rearrange("p (k c) -> p k c", c=128),
                  I["mw2"].t[1024 * win:1024 * (win + 1), 128 * m:128 * (m + 1)]
                  .rearrange("(k p) c -> p k c", p=128), writes=[dst])

        def ldwo(m, dst):
            p.dma("Gd", dst.t[:].rearrange("p (k c) -> p k c", c=128),
                  I["wo"].t[:, 128 * m:128 * (m + 1)]
                  .rearrange("(k p) c -> p k c", p=128), writes=[dst])

        for i in range(NDC):
            p.dma("Yd", hbuf[i].t[:, 0:T], SC["attn"].t[i, :, :],
                  reads=[SC["attn"]], writes=[hbuf[i]])
        ldw1(0, w1b[0])
        for win in range(8):
            for fs8 in range(8):
                idx = 8 * win + fs8
                wc = w1b[idx % 2]
                if idx + 1 < 64:
                    ldw1(idx + 1, w1b[(idx + 1) % 2])
                wv = wc.t[:].rearrange("p (k c) -> p k c", c=128)
                for th in range(2):
                    t0 = 512 * th
                    PHb = PH[th]
                    p.mm_group(PHb, [(PHb.ap, wv[:, k, :], qmem[k].t[:, t0:t0 + 512])
                                     for k in range(NDC)], reads=[wc] + qmem)
                    p.op("S", lambda e, PHb=PHb: e.activation(
                            scr.ap, PHb.ap, AF.Silu),
                         reads=[PHb], writes=[scr])
                    p.op("V", lambda e, fs8=fs8, t0=t0: e.scalar_tensor_tensor(
                            poly[fs8].t[:, t0:t0 + 512], scr.ap,
                            1.0, scr.ap, ALU.add, ALU.mult),
                         reads=[scr], writes=[poly[fs8]])
            if win == 0:
                ldw2(0, 0, w2b[0])
            for m in range(NDC):
                wc = w2b[m % 2]
                nwin, nm = (win, m + 1) if m + 1 < NDC else (win + 1, 0)
                if nwin < 8:
                    ldw2(nwin, nm, w2b[(m + 1) % 2])
                elif win == 7 and m == NDC - 1:
                    ldwo(0, wob[0])
                wv = wc.t[:].rearrange("p (k c) -> p k c", c=128)
                for th in range(2):
                    t0 = 512 * th
                    PWb = PW[th]
                    p.mm_group(PWb, [(PWb.ap, wv[:, k, :],
                                      poly[k].t[:, t0:t0 + 512])
                                     for k in range(8)], reads=[wc] + poly)
                    if win == 0:
                        p.op("V", lambda e, m=m, t0=t0, PWb=PWb: e.tensor_copy(
                                macc[m].t[:, t0:t0 + 512], PWb.ap),
                             reads=[PWb], writes=[macc[m]])
                    else:
                        p.op("V", lambda e, m=m, t0=t0, PWb=PWb: e.tensor_tensor(
                                macc[m].t[:, t0:t0 + 512],
                                macc[m].t[:, t0:t0 + 512], PWb.ap, ALU.add),
                             reads=[macc[m], PWb], writes=[macc[m]])
        p.dma("Yd", xcs[0].ap, I["x_f"].t[0:128, HALO:HALO + T], writes=[xcs[0]])
        for m in range(NDC):
            wc = wob[m % 2]
            if m + 1 < NDC:
                ldwo(m + 1, wob[(m + 1) % 2])
                p.dma("Yd", xcs[(m + 1) % 2].ap,
                      I["x_f"].t[128 * (m + 1):128 * (m + 2), HALO:HALO + T],
                      writes=[xcs[(m + 1) % 2]])
            wv = wc.t[:].rearrange("p (k c) -> p k c", c=128)
            xc = xcs[m % 2]
            for th in range(2):
                t0 = 512 * th
                PAb = PA[th]
                p.mm_group(PAb, [(PAb.ap, wv[:, k, :], hbuf[k].t[:, t0:t0 + 512])
                                 for k in range(NDC)], reads=[wc] + hbuf)
                p.op("S", lambda e, m=m, t0=t0: e.activation(
                        sg.ap, macc[m].t[:, t0:t0 + 512], AF.Sigmoid),
                     reads=[macc[m]], writes=[sg])
                p.op("V", lambda e, PAb=PAb: e.tensor_tensor(t2b.ap, sg.ap, PAb.ap,
                                                             ALU.mult),
                     reads=[sg, PAb], writes=[t2b])
                p.op("V", lambda e, xc=xc, t0=t0: e.tensor_tensor(
                        xac.t[:, t0:t0 + 512], xc.t[:, t0:t0 + 512], t2b.ap,
                        ALU.add),
                     reads=[xc, t2b], writes=[xac])
                p.op("V", lambda e, xc=xc, t0=t0: e.scalar_tensor_tensor(
                        rc.t[:, t0:t0 + 512], xc.t[:, t0:t0 + 512], 2.0, t2b.ap,
                        ALU.mult, ALU.add),
                     reads=[xc, t2b], writes=[rc])
                p.op("S", lambda e, t0=t0: e.activation(
                        sqb.ap, xac.t[:, t0:t0 + 512], AF.Square),
                     reads=[xac], writes=[sqb])
                p.mm(RS.t[:, t0:t0 + 512], C["onesb"].t[:, 0:1],
                     sqb.ap, RS, reads=[C["onesb"], sqb],
                     start=(m == 0), stop=(m == NDC - 1))
            p.dma("Yd", SC["xa"].t[m, :, :], xac.ap, reads=[xac], writes=[SC["xa"]])
            p.dma("Yd", SC["r"].t[m, :, :], rc.ap, reads=[rc], writes=[SC["r"]])
            if tap_m is not None:
                p.dma("Yd", tap_m.t[m, :, :], macc[m].ap,
                      reads=[macc[m]], writes=[tap_m])
            if tap_xa is not None:
                p.dma("Yd", tap_xa.t[m, :, :], xac.ap, reads=[xac], writes=[tap_xa])
            if tap_r is not None:
                p.dma("Yd", tap_r.t[m, :, :], rc.ap, reads=[rc], writes=[tap_r])
        p.op("V", lambda e: e.tensor_copy(rms2ss.ap, RS.ap),
             reads=[RS], writes=[rms2ss])
        p.barrier()



def _p3(p, nc, I, C, SC, rms2ss, tap):
    tap_y = tap("y", [NDC, 128, T], BF16)
    tap_rf = tap("rfin", [NDC, 128, T], F32)
    tap_x1 = tap("x1f", [NDC, 128, T], F32)
    NF3 = FFHP // 128
    with ExitStack() as e3:
        y = [p.sb(f"p3y{i}", [128, T], BF16, es=e3) for i in range(NDC)]
        g = [p.sb(f"p3g{f}", [128, T], BF16, es=e3) for f in range(NF3)]
        w1c = [p.sb(f"p3w1{i}", [128, NDC * 128], BF16, es=e3) for i in range(2)]
        w3c = [p.sb(f"p3w3{i}", [128, NDC * 128], BF16, es=e3) for i in range(2)]
        w2c = [p.sb(f"p3w2{i}", [128, NF3 * 128], BF16, es=e3) for i in range(2)]
        xc = [p.sb(f"p3xc{i}", [128, T], F32, es=e3) for i in range(2)]
        rfin = p.sb("p3rf", [128, T], F32, es=e3)
        scr = p.sb("p3scr", [128, T], F32, es=e3)
        sqb = p.sb("p3sqb", [128, 512], BF16, es=e3)
        rows = p.sb("p3rows", [1, 4 * T], F32, es=e3)
        R2 = rows.t[:, 0:T]; LS = rows.t[:, T:2*T]; LQ = rows.t[:, 2*T:3*T]
        MR = rows.t[:, 3*T:4*T]; R1 = R2
        bco = p.sb("p3bco", [128, T], F32, es=e3)
        mb = p.sb("p3mb", [128, T], F32, es=e3)
        x1b = p.sb("p3x1b", [128, T], BF16, es=e3)
        x1f = rfin
        BC = p.ps("p3bc", [128, 512], F32, es=e3)
        PH = [p.ps(f"p3ph{i}", [128, 512], F32, es=e3) for i in range(2)]
        LR = [p.ps(f"p3lr{i}", [1, 512], F32, es=e3) for i in range(4)]

        def bcast(src_slice, dst):
            for th in range(2):
                p.mm(BC.ap, C["ones"].t[0:1, :], src_slice[:, 512*th:512*(th+1)],
                     BC, reads=[C["ones"], rows])
                p.op("V", lambda e, th=th, dst=dst: e.tensor_copy(
                        dst.t[:, 512*th:512*(th+1)], BC.ap),
                     reads=[BC], writes=[dst])

        p.op("S", lambda e: e.activation(R2, rms2ss.ap, AF.Ln,
                                         bias=C["epsb"].t[0:1, 0:1], scale=1.0 / D),
             reads=[rms2ss, C["epsb"]], writes=[rows])
        p.op("S", lambda e: e.activation(R2, R2, AF.Exp, scale=-0.5),
             reads=[rows], writes=[rows])
        bcast(R2, bco)
        for i in range(NDC):
            xb = xc[i % 2]
            p.dma("Yd", xb.ap, SC["xa"].t[i, :, :], reads=[SC["xa"]], writes=[xb])
            p.op("V", lambda e, i=i, xb=xb: e.scalar_tensor_tensor(
                    y[i].ap, xb.ap, C["n2w"].t[:, i:i+1], bco.ap, ALU.mult, ALU.mult),
                 reads=[xb, C["n2w"], bco], writes=[y[i]])
            if tap_y is not None:
                p.dma("Yd", tap_y.t[i, :, :], y[i].ap, reads=[y[i]], writes=[tap_y])

        def ldw13(fs, b):
            for nm, dst in (("fw1", w1c[b]), ("fw3", w3c[b])):
                p.dma("Gd", dst.t[:].rearrange("p (k c) -> p k c", c=128),
                      I[nm].t[:, 128*fs:128*(fs+1)].rearrange("(k p) c -> p k c", p=128),
                      writes=[dst])
        ldw13(0, 0)
        for fs in range(NF3):
            if fs + 1 < NF3:
                ldw13(fs + 1, (fs + 1) % 2)
            wv1 = w1c[fs % 2].t[:].rearrange("p (k c) -> p k c", c=128)
            wv3 = w3c[fs % 2].t[:].rearrange("p (k c) -> p k c", c=128)
            for th in range(2):
                t0 = 512 * th
                p.mm_group(PH[0], [(PH[0].ap, wv1[:, k, :], y[k].t[:, t0:t0+512])
                                   for k in range(NDC)], reads=[w1c[fs % 2]] + y)
                p.op("S", lambda e, t0=t0: e.activation(scr.t[:, t0:t0+512],
                                                        PH[0].ap, AF.Silu),
                     reads=[PH[0]], writes=[scr])
                p.mm_group(PH[1], [(PH[1].ap, wv3[:, k, :], y[k].t[:, t0:t0+512])
                                   for k in range(NDC)], reads=[w3c[fs % 2]] + y)
                p.op("V", lambda e, fs=fs, t0=t0: e.tensor_tensor(
                        g[fs].t[:, t0:t0+512], scr.t[:, t0:t0+512], PH[1].ap,
                        ALU.mult),
                     reads=[scr, PH[1]], writes=[g[fs]])

        def ldw2(m, b):
            p.dma("Gd", w2c[b].t[:].rearrange("p (k c) -> p k c", c=128),
                  I["fw2"].t[:, 128*m:128*(m+1)].rearrange("(k p) c -> p k c", p=128),
                  writes=[w2c[b]])
        ldw2(0, 0)
        for m in range(NDC):
            if m + 1 < NDC:
                ldw2(m + 1, (m + 1) % 2)
            rb = xc[m % 2]
            p.dma("Yd", rb.ap, SC["r"].t[m, :, :], reads=[SC["r"]], writes=[rb])
            wv2 = w2c[m % 2].t[:].rearrange("p (k c) -> p k c", c=128)
            for th in range(2):
                t0 = 512 * th
                PHb = PH[th]
                p.mm_group(PHb, [(PHb.ap, wv2[:, k, :], g[k].t[:, t0:t0+512])
                                 for k in range(NF3)], reads=[w2c[m % 2]] + g)
                p.op("V", lambda e, t0=t0, rb=rb, PHb=PHb: e.tensor_tensor(
                        rfin.t[:, t0:t0+512], rb.t[:, t0:t0+512], PHb.ap, ALU.add),
                     reads=[rb, PHb], writes=[rfin])
                p.mm(LR[th].ap, C["ones"].t[:, 0:1], rfin.t[:, t0:t0+512],
                     LR[th], reads=[C["ones"], rfin],
                     start=(m == 0), stop=(m == NDC - 1))
                p.op("S", lambda e, t0=t0: e.activation(sqb.ap,
                                                        rfin.t[:, t0:t0+512],
                                                        AF.Square),
                     reads=[rfin], writes=[sqb])
                p.mm(LR[2+th].ap, C["onesb"].t[:, 0:1], sqb.ap,
                     LR[2+th], reads=[C["onesb"], sqb],
                     start=(m == 0), stop=(m == NDC - 1))
            p.dma("Yd", SC["r"].t[m, :, :], rfin.ap, reads=[rfin], writes=[SC["r"]])
            if tap_rf is not None:
                p.dma("Yd", tap_rf.t[m, :, :], rfin.ap, reads=[rfin], writes=[tap_rf])

        for th in range(2):
            p.op("V", lambda e, th=th: e.tensor_copy(LS[:, 512*th:512*(th+1)],
                                                     LR[th].ap),
                 reads=[LR[th]], writes=[rows])
            p.op("V", lambda e, th=th: e.tensor_copy(LQ[:, 512*th:512*(th+1)],
                                                     LR[2+th].ap),
                 reads=[LR[2+th]], writes=[rows])
        p.op("V", lambda e: e.tensor_scalar(MR, LS, 1.0 / D, None, ALU.mult),
             reads=[rows], writes=[rows])
        p.op("V", lambda e: e.tensor_scalar(LQ, LQ, 1.0 / D, None, ALU.mult),
             reads=[rows], writes=[rows])
        p.op("V", lambda e: e.tensor_tensor(R1, MR, MR, ALU.mult),
             reads=[rows], writes=[rows])
        p.op("V", lambda e: e.tensor_tensor(LQ, LQ, R1, ALU.subtract),
             reads=[rows], writes=[rows])
        p.op("S", lambda e: e.activation(R1, LQ, AF.Ln, bias=C["epsb"].t[0:1, 1:2]),
             reads=[rows, C["epsb"]], writes=[rows])
        p.op("S", lambda e: e.activation(R1, R1, AF.Exp, scale=-0.5),
             reads=[rows], writes=[rows])
        bcast(MR, mb)
        bcast(R1, bco)
        for m in range(NDC):
            rb = xc[m % 2]
            p.dma("Yd", rb.ap, SC["r"].t[m, :, :], reads=[SC["r"]], writes=[rb])
            p.op("G", lambda e, rb=rb: e.tensor_tensor(scr.ap, rb.ap, mb.ap,
                                                       ALU.subtract),
                 reads=[rb, mb], writes=[scr])
            p.op("V", lambda e: e.tensor_tensor(x1f.ap, scr.ap, bco.ap, ALU.mult),
                 reads=[scr, bco], writes=[x1f])
            p.op("S", lambda e: e.copy(x1b.ap, x1f.ap), reads=[x1f], writes=[x1b])
            p.dma("Yd", SC["x1f"].t[m, :, :], x1f.ap, reads=[x1f], writes=[SC["x1f"]])
            p.dma("Yd", SC["x1b"].t[m, :, :], x1b.ap, reads=[x1b], writes=[SC["x1b"]])
            if tap_x1 is not None:
                p.dma("Yd", tap_x1.t[m, :, :], x1f.ap, reads=[x1f], writes=[tap_x1])
        p.barrier()



def _p4(p, nc, I, C, SC, ln2s, tap):
    tap_w = tap("gatew", [2, 128, 32], F32)
    tap_r2 = tap("r2", [NDC, 128, T], F32)
    NF4 = DFF // 128
    with ExitStack() as e4:
        x1h = [p.sb(f"p4x{k}", [128, 512], BF16, es=e4) for k in range(NDC)]
        wrb = p.sb("p4wr", [128, NDC * NEXP], BF16, es=e4)
        bdb = p.sb("p4bd", [128, D], BF16, es=e4)
        atile = [p.sb(f"p4at{i}", [128, NDC * 128], BF16, es=e4) for i in range(1)]
        t_sb = p.sb("p4t", [16, 16 * 512], BF16, es=e4)
        gbp = [p.sb(f"p4gb{f}", [128, 1024], BF16, es=e4) for f in range(NF4 // 2)]
        ubp = [p.sb(f"p4ub{f}", [128, 1024], BF16, es=e4) for f in range(NF4 // 2)]
        def gb_ap(fs):
            return gbp[fs // 2].t[:, 512 * (fs % 2):512 * (fs % 2) + 512]
        def ub_ap(fs):
            return ubp[fs // 2].t[:, 512 * (fs % 2):512 * (fs % 2) + 512]
        bgc = p.sb("p4bgc", [16, DFF], BF16, es=e4)
        buc = p.sb("p4buc", [16, DFF], BF16, es=e4)
        adb = [p.sb(f"p4ad{i}", [128, NF4 * R], BF16, es=e4) for i in range(1)]
        wdc = [p.sb(f"p4wd{i}", [128, 22 * 128], BF16, es=e4) for i in range(2)]
        lgs = p.sb("p4lg", [128, 32], F32, es=e4)
        wgt = p.sb("p4wt", [128, 32], F32, es=e4)
        wtr = p.sb("p4wtr", [8, 512], F32, es=e4)
        wrows = p.sb("p4wrow", [1, 512], F32, es=e4)
        wsb = p.sb("p4wsb", [16, 512], F32, es=e4)
        dtt = p.sb("p4dtt", [16, 512], BF16, es=e4)
        dT_all = p.sb("p4dta", [128, 512], BF16, es=e4)
        scr1 = p.sb("p4s1", [128, 512], F32, es=e4)
        scr2 = p.sb("p4s2", [128, 512], F32, es=e4)
        scr3 = None
        heb = p.sb("p4he", [128, 512], BF16, es=e4)
        scr3 = scr1
        xfc = [p.sb("p4xf0", [128, 512], F32, es=e4), scr2]
        r2c = p.sb("p4r2", [128, 512], F32, es=e4)
        col = p.sb("p4col", [128, 4], F32, es=e4)

        p.dma("Gd", wrb.t[:].rearrange("p (k c) -> p k c", c=NEXP),
              I["wr"].ap.rearrange("(k p) c -> p k c", p=128), writes=[wrb])
        p.dma("Gd", bdb.ap, I["bd"].ap, writes=[bdb])

        for hi, t0 in enumerate((0, 512)):
            for k in range(NDC):
                p.dma("Yd", x1h[k].ap, SC["x1b"].t[k, :, t0:t0 + 512],
                      reads=[SC["x1b"]], writes=[x1h[k]])

            with ExitStack() as eR:
                PT4 = p.ps(f"p4pt{hi}", [128, 32], F32, es=eR)
                TRP = p.ps(f"p4trp{hi}", [8, 512], F32, es=eR)
                wrv = wrb.t[:].rearrange("p (k c) -> p k c", c=NEXP)
                for tt in range(4):
                    p.mm_group(PT4, [(PT4.t[:, 8 * tt:8 * (tt + 1)],
                                      x1h[k].t[:, 128 * tt:128 * (tt + 1)],
                                      wrv[:, k, :]) for k in range(NDC)],
                               reads=[wrb] + x1h)
                p.op("V", lambda e: e.tensor_copy(lgs.ap, PT4.ap),
                     reads=[PT4], writes=[lgs])
                for tt in range(4):
                    sl = lgs.t[:, 8 * tt:8 * (tt + 1)]
                    wl = wgt.t[:, 8 * tt:8 * (tt + 1)]
                    p.op("V", lambda e, sl=sl: e.reduce_max(col.t[:, 0:1], sl, axis=AX.X),
                         reads=[lgs], writes=[col])
                    p.op("V", lambda e: e.tensor_scalar_mul(col.t[:, 3:4],
                                                            col.t[:, 0:1], -1.0),
                         reads=[col], writes=[col])
                    p.op("S", lambda e, sl=sl, wl=wl: e.activation(
                            wl, sl, AF.Exp, bias=col.t[:, 3:4],
                            accum_out=col.t[:, 2:3]),
                         reads=[lgs, col], writes=[wgt, col])
                    p.op("V", lambda e: e.reciprocal(col.t[:, 2:3], col.t[:, 2:3]),
                         reads=[col], writes=[col])
                    p.op("V", lambda e, sl=sl: e.tensor_scalar(
                            scr1.t[:, 0:8], sl, col.t[:, 0:1], None, ALU.is_equal),
                         reads=[lgs, col], writes=[scr1])
                    p.op("V", lambda e, sl=sl: e.scalar_tensor_tensor(
                            scr1.t[:, 0:8], scr1.t[:, 0:8], NEG, sl,
                            ALU.mult, ALU.add),
                         reads=[scr1, lgs], writes=[scr1])
                    p.op("V", lambda e: e.reduce_max(col.t[:, 1:2], scr1.t[:, 0:8],
                                                     axis=AX.X),
                         reads=[scr1], writes=[col])
                    p.op("V", lambda e, sl=sl: e.tensor_scalar(
                            scr1.t[:, 0:8], sl, col.t[:, 1:2], None, ALU.is_ge),
                         reads=[lgs, col], writes=[scr1])
                    p.op("V", lambda e, wl=wl: e.scalar_tensor_tensor(
                            wl, wl, col.t[:, 2:3], scr1.t[:, 0:8],
                            ALU.mult, ALU.mult),
                         reads=[wgt, col, scr1], writes=[wgt])
                if tap_w is not None:
                    p.dma("Yd", tap_w.t[hi, :, :], wgt.ap, reads=[wgt], writes=[tap_w])
                for tt in range(4):
                    p.op("T", lambda e, tt=tt: e.transpose(
                            TRP.t[:, 128 * tt:128 * (tt + 1)],
                            wgt.t[:, 8 * tt:8 * (tt + 1)], C["identf"].ap),
                         reads=[wgt, C["identf"]], writes=[TRP])
                p.op("V", lambda e: e.tensor_scalar_mul(wtr.ap, TRP.t[0:8, :],
                                                        SCALING),
                     reads=[TRP], writes=[wtr])
                p.barrier(("T", "V", "S"))

            with ExitStack() as eE:
                PAW = [p.ps(f"p4pa{hi}_{i}", [128, 1024], F32, es=eE)
                       for i in range(2)]
                PUW = p.ps(f"p4pu{hi}", [128, 1024], F32, es=eE)
                PD = p.ps(f"p4pd{hi}", [16, 512], F32, es=eE)
                for ti in range(8):
                    p.dma("Gd", atile[0].t[:].rearrange("p (k c) -> p k c", c=128),
                          I["a_all"].t[:, 128 * ti:128 * (ti + 1)]
                          .rearrange("(k p) c -> p k c", p=128),
                          writes=[atile[0]])
                    av = atile[0].t[:].rearrange("p (k c) -> p k c", c=128)
                    PAb = PAW[ti % 2]
                    p.mm_group(PAb, [(PAb.t[:, 0:512], av[:, k, :], x1h[k].ap)
                                     for k in range(NDC)], reads=[atile[0]] + x1h)
                    base_slot = 0 if ti < 4 else 8
                    uu = ti % 4
                    for half_e, pbase in ((0, 0), (1, 64)):
                        slot = base_slot + 2 * uu + half_e
                        p.op("S", lambda e, slot=slot, pbase=pbase, PAb=PAb:
                             e.activation(t_sb.t[:, 512 * slot:512 * (slot + 1)],
                                          PAb.t[pbase:pbase + 16, 0:512], AF.Copy,
                                          scale=SCALING),
                             reads=[PAb], writes=[t_sb])

                with ExitStack() as eB:
                    wgc = [p.sb(f"p4wg{hi}_{i}", [128, NDC * 128], BF16, es=eB)
                           for i in range(2)]
                    wuc = [p.sb(f"p4wu{hi}_{i}", [128, NDC * 128], BF16, es=eB)
                           for i in range(2)]
                    def ldg(fs, b):
                        for nm, dst in (("wg", wgc[b]), ("wu", wuc[b])):
                            p.dma("Gd", dst.t[:].rearrange("p (k c) -> p k c", c=128),
                                  I[nm].t[:, 128 * fs:128 * (fs + 1)]
                                  .rearrange("(k p) c -> p k c", p=128), writes=[dst])
                    ldg(0, 0)
                    for fs in range(NF4):
                        if fs + 1 < NF4:
                            ldg(fs + 1, (fs + 1) % 2)
                        gv = wgc[fs % 2].t[:].rearrange("p (k c) -> p k c", c=128)
                        uv = wuc[fs % 2].t[:].rearrange("p (k c) -> p k c", c=128)
                        PAb = PAW[fs % 2]
                        p.mm_group(PAb, [(PAb.t[:, 0:512], gv[:, k, :], x1h[k].ap)
                                         for k in range(NDC)], reads=[wgc[fs % 2]] + x1h)
                        p.op("S", lambda e, fs=fs, PAb=PAb: e.copy(
                                gb_ap(fs), PAb.t[:, 0:512]),
                             reads=[PAb], writes=[gbp[fs // 2]])
                        p.mm_group(PUW, [(PUW.t[:, 0:512], uv[:, k, :], x1h[k].ap)
                                         for k in range(NDC)], reads=[wuc[fs % 2]] + x1h)
                        p.op("S", lambda e, fs=fs: e.copy(ub_ap(fs), PUW.t[:, 0:512]),
                             reads=[PUW], writes=[ubp[fs // 2]])
                    p.barrier(("T", "V", "S", "Gd"))

                def ldcombo(ex, b):
                    p.dma("Gd", bgc.ap, I["bg"].t[ex, :, :], writes=[bgc])
                    p.dma("Gd", buc.ap, I["bu"].t[ex, :, :], writes=[buc])
                    p.dma("Gd", adb[b].t[:].rearrange("p (f r) -> p f r", r=R),
                          I["ad"].t[ex, :, :].rearrange("(f p) r -> p f r", p=128),
                          writes=[adb[b]])
                NP4 = NF4 // 2
                with ExitStack() as eX:
                    s1 = [p.sb(f"p4s1{hi}_{i}", [128, 1024], BF16, es=eX)
                          for i in range(2)]
                    s2 = [p.sb(f"p4s2{hi}_{i}", [128, 1024], BF16, es=eX)
                          for i in range(2)]
                    s3 = [p.sb(f"p4s3{hi}_{i}", [128, 1024], BF16, es=eX)
                          for i in range(2)]
                    hebs = [p.sb(f"p4he{hi}_{i}", [128, 1024], BF16, es=eX)
                            for i in range(2)]
                    bbscr = p.sb(f"p4bbs{hi}", [128, 1024], BF16, es=eX)
                    for ex in range(NEXP):
                        ldcombo(ex, 0)
                        ab = adb[0].t[:].rearrange("p (f r) -> p f r", r=R)
                        tg = t_sb.t[:, 512 * ex:512 * (ex + 1)]
                        tu = t_sb.t[:, 512 * (8 + ex):512 * (9 + ex)]

                        def mms_g(j):
                            par = j % 2
                            for h2 in range(2):
                                fs = 2 * j + h2
                                p.mm(PAW[par].t[:, 512 * h2:512 * (h2 + 1)],
                                     bgc.t[:, 128 * fs:128 * (fs + 1)],
                                     tg, PAW[par], reads=[bgc, t_sb])

                        def mms_u(j):
                            for h2 in range(2):
                                fs = 2 * j + h2
                                p.mm(PUW.t[:, 512 * h2:512 * (h2 + 1)],
                                     buc.t[:, 128 * fs:128 * (fs + 1)],
                                     tu, PUW, reads=[buc, t_sb])

                        def elem(j):
                            par = j % 2
                            p.op("V", lambda e, j=j, par=par: e.tensor_tensor(
                                    s1[par].ap, gbp[j].ap, PAW[par].ap, ALU.add),
                                 reads=[gbp[j], PAW[par]], writes=[s1[par]])
                            p.op("S", lambda e, par=par: e.activation(
                                    s2[par].ap, s1[par].ap, AF.Silu),
                                 reads=[s1[par]], writes=[s2[par]])
                            p.op("V", lambda e, j=j, par=par: e.tensor_tensor(
                                    s3[par].ap, ubp[j].ap, PUW.ap, ALU.add),
                                 reads=[ubp[j], PUW], writes=[s3[par]])
                            p.op("V", lambda e, par=par: e.tensor_tensor(
                                    hebs[par].ap, s2[par].ap, s3[par].ap, ALU.mult),
                                 reads=[s2[par], s3[par]], writes=[hebs[par]])
                            for h2 in range(2):
                                fs = 2 * j + h2
                                p.mm(PD.ap, ab[:, fs, :],
                                     hebs[par].t[:, 512 * h2:512 * (h2 + 1)], PD,
                                     reads=[adb[0], hebs[par]],
                                     start=(fs == 0), stop=(fs == NF4 - 1))

                        def bb_make(j):
                            p.op("S", lambda e, j=j: e.activation(
                                    bbscr.ap, gbp[j].ap, AF.Silu),
                                 reads=[gbp[j]], writes=[bbscr])
                            p.op("V", lambda e, j=j: e.tensor_tensor(
                                    gbp[j].ap, bbscr.ap, ubp[j].ap, ALU.mult),
                                 reads=[bbscr, ubp[j]], writes=[gbp[j]])
                        mms_g(0)
                        mms_u(0)
                        for j in range(1, NP4):
                            mms_g(j)
                            elem(j - 1)
                            mms_u(j)
                            if ex == NEXP - 1 and j >= 2:
                                bb_make(j - 2)
                        elem(NP4 - 1)
                        if ex == NEXP - 1:
                            bb_make(NP4 - 2)
                            bb_make(NP4 - 1)
                        for tt in range(4):
                            p.dma("Yd", wrows.t[:, 128 * tt:128 * (tt + 1)],
                                  wtr.t[ex:ex + 1, 128 * tt:128 * (tt + 1)],
                                  reads=[wtr], writes=[wrows])
                        p.mm(PUW.t[0:16, 0:512], C["ones"].t[0:1, 0:16],
                             wrows.ap, PUW, reads=[C["ones"], wrows])
                        p.op("V", lambda e: e.tensor_copy(wsb.ap, PUW.t[0:16, 0:512]),
                             reads=[PUW], writes=[wsb])
                        p.op("V", lambda e: e.tensor_tensor(dtt.ap, PD.ap, wsb.ap,
                                                            ALU.mult),
                             reads=[PD, wsb], writes=[dtt])
                        p.dma("Yd", dT_all.t[16 * ex:16 * (ex + 1), :], dtt.ap,
                              reads=[dtt], writes=[dT_all])

                p.barrier(("T", "V", "S", "G"))

            with ExitStack() as eF:
                PF = [p.ps(f"p4pf{hi}_{i}", [128, 512], F32, es=eF) for i in range(2)]
                LR2 = [p.ps(f"p4lr{hi}_{i}", [1, 512], F32, es=eF) for i in range(2)]
                def ldwd(m, b):
                    p.dma("Gd", wdc[b].t[:].rearrange("p (k c) -> p k c", c=128),
                          I["wd"].t[2816 * b:2816 * (b + 1), 128 * m:128 * (m + 1)]
                          .rearrange("(k p) c -> p k c", p=128), writes=[wdc[b]])
                ldwd(0, 0)
                ldwd(0, 1)
                p.dma("Yd", xfc[0].ap, SC["x1f"].t[0, :, t0:t0 + 512],
                      reads=[SC["x1f"]], writes=[xfc[0]])
                for m in range(NDC):
                    if m + 1 < NDC:
                        p.dma("Yd", xfc[(m + 1) % 2].ap,
                              SC["x1f"].t[m + 1, :, t0:t0 + 512],
                              reads=[SC["x1f"]], writes=[xfc[(m + 1) % 2]])
                    dvA = wdc[0].t[:].rearrange("p (k c) -> p k c", c=128)
                    dvB = wdc[1].t[:].rearrange("p (k c) -> p k c", c=128)
                    PFb = PF[m % 2]
                    p.mm_group(PFb, [(PFb.ap, dvA[:, k, :], gb_ap(k))
                                     for k in range(22)],
                               reads=[wdc[0]] + gbp[0:11], start=True, stop=False)
                    if m + 1 < NDC:
                        ldwd(m + 1, 0)
                    mmsB = [(PFb.ap, dvB[:, k, :], gb_ap(22 + k))
                            for k in range(22)]
                    mmsB.append((PFb.ap, bdb.t[:, 128 * m:128 * (m + 1)],
                                 dT_all.ap))
                    p.mm_group(PFb, mmsB, reads=[wdc[1], bdb, dT_all] + gbp[11:],
                               start=False, stop=True)
                    if m + 1 < NDC:
                        ldwd(m + 1, 1)
                    xb = xfc[m % 2]
                    p.op("V", lambda e, xb=xb, PFb=PFb: e.tensor_tensor(
                            r2c.ap, xb.ap, PFb.ap, ALU.add),
                         reads=[xb, PFb], writes=[r2c])
                    p.mm(LR2[0].ap, C["ones"].t[:, 0:1], r2c.ap, LR2[0],
                         reads=[C["ones"], r2c], start=(m == 0), stop=(m == NDC - 1))
                    p.op("S", lambda e: e.activation(scr1.ap, r2c.ap, AF.Square),
                         reads=[r2c], writes=[scr1])
                    p.mm(LR2[1].ap, C["ones"].t[:, 0:1], scr1.ap, LR2[1],
                         reads=[C["ones"], scr1], start=(m == 0), stop=(m == NDC - 1))
                    p.dma("Yd", SC["r2"].t[m, :, t0:t0 + 512], r2c.ap,
                          reads=[r2c], writes=[SC["r2"]])
                    if tap_r2 is not None:
                        p.dma("Yd", tap_r2.t[m, :, t0:t0 + 512], r2c.ap,
                              reads=[r2c], writes=[tap_r2])
                p.op("V", lambda e, t0=t0: e.tensor_copy(
                        ln2s.t[:, t0:t0 + 512], LR2[0].ap),
                     reads=[LR2[0]], writes=[ln2s])
                p.op("V", lambda e, t0=t0: e.tensor_copy(
                        ln2s.t[:, T + t0:T + t0 + 512], LR2[1].ap),
                     reads=[LR2[1]], writes=[ln2s])
                p.barrier(("T", "V", "S"))
        p.barrier()



def _p5(p, nc, I, C, SC, ln2s, out, tap):
    with ExitStack() as e5:
        rows = p.sb("p5rows", [1, 3 * T], F32, es=e5)
        MR = rows.t[:, 0:T]; VR = rows.t[:, T:2*T]; RS = rows.t[:, 2*T:3*T]
        mb = p.sb("p5mb", [128, T], F32, es=e5)
        bco = p.sb("p5bco", [128, T], F32, es=e5)
        rc = [p.sb(f"p5rc{i}", [128, T], F32, es=e5) for i in range(2)]
        oc = p.sb("p5oc", [128, T], F32, es=e5)
        scr = p.sb("p5scr", [128, T], F32, es=e5)
        BC = p.ps("p5bc", [128, 512], F32, es=e5)

        p.op("V", lambda e: e.tensor_scalar(MR, ln2s.t[:, 0:T], 1.0 / D, None,
                                            ALU.mult),
             reads=[ln2s], writes=[rows])
        p.op("V", lambda e: e.tensor_scalar(VR, ln2s.t[:, T:2*T], 1.0 / D, None,
                                            ALU.mult),
             reads=[ln2s], writes=[rows])
        p.op("V", lambda e: e.tensor_tensor(RS, MR, MR, ALU.mult),
             reads=[rows], writes=[rows])
        p.op("V", lambda e: e.tensor_tensor(VR, VR, RS, ALU.subtract),
             reads=[rows], writes=[rows])
        p.op("S", lambda e: e.activation(RS, VR, AF.Ln, bias=C["epsb"].t[0:1, 1:2]),
             reads=[rows, C["epsb"]], writes=[rows])
        p.op("S", lambda e: e.activation(RS, RS, AF.Exp, scale=-0.5),
             reads=[rows], writes=[rows])
        for src_sl, dst in ((MR, mb), (RS, bco)):
            for th in range(2):
                p.mm(BC.ap, C["ones"].t[0:1, :], src_sl[:, 512*th:512*(th+1)],
                     BC, reads=[C["ones"], rows])
                p.op("V", lambda e, th=th, dst=dst: e.tensor_copy(
                        dst.t[:, 512*th:512*(th+1)], BC.ap),
                     reads=[BC], writes=[dst])
        for m in range(NDC):
            rb = rc[m % 2]
            p.dma("Yd", rb.ap, SC["r2"].t[m, :, :], reads=[SC["r2"]], writes=[rb])
            p.op("V", lambda e, rb=rb: e.tensor_tensor(scr.ap, rb.ap, mb.ap,
                                                       ALU.subtract),
                 reads=[rb, mb], writes=[scr])
            p.op("V", lambda e: e.tensor_tensor(oc.ap, scr.ap, bco.ap, ALU.mult),
                 reads=[scr, bco], writes=[oc])
            p.dma("Yd", out.t[m, :, :], oc.ap, reads=[oc], writes=[out])
        p.barrier()



def kernel(**inputs):
    shared, per_core = prep_inputs(inputs)
    nc, _ = build(phases=("P0", "P1", "P2", "P3", "P4", "P5"))
    in_maps = [dict(shared, **pc) for pc in per_core]
    res = run_bass_kernel_spmd(nc, in_maps, core_ids=list(range(NCORES)))
    outs = [np.asarray(res.results[c]["out"]).transpose(2, 0, 1).reshape(T, D)
            for c in range(NCORES)]
    return np.concatenate(outs, axis=0).reshape(1, S, D).astype(np.float32)


# version 15
# speedup vs baseline: 1.0213x; 1.0213x over previous
import numpy as np
import ml_dtypes
from contextlib import ExitStack

import concourse.bass as bass
import concourse.mybir as mybir
from concourse.bass_utils import run_bass_kernel_spmd

F32 = mybir.dt.float32
BF16 = mybir.dt.bfloat16
AF = mybir.ActivationFunctionType
ALU = mybir.AluOpType
AX = mybir.AxisListType
BFD = ml_dtypes.bfloat16

NCORES = 8
S = 8192
D = 2048
T = 1024
HALO = 512
TH = T + HALO
NH = 16
DH = 128
NDC = D // 128
W_SWA = 512
W_OM = 64
DECAY = 0.9
GGH = 64
MEMF = 8192
FFH = 5461
FFHP = 5504
DFF = 5632
NEXP = 8
R = 16
SCALING = 2.0
EPS_RMS = 1e-6
EPS_LN = 1e-5
SCALE = DH ** -0.5
NEG = -1e30


ENGINE_OF = {"T": "tensor", "V": "vector", "S": "scalar", "Sd": "scalar",
             "G": "gpsimd", "Gd": "gpsimd", "Yd": "sync"}

NSLOT = 6

class Q:
    def __init__(self, name, step):
        self.name = name
        self.step = step
        self.is_dma = name.endswith("d")
        if self.is_dma:
            self.slot_cnt = [0] * NSLOT
            self.issue = 0
        else:
            self.cnt = 0

class Buf:
    __slots__ = ("t", "w", "readers", "name")
    def __init__(self, t, name=""):
        self.t = t
        self.name = name
        self.w = None
        self.readers = {}
    def __getitem__(self, idx):
        return self.t[idx]
    @property
    def ap(self):
        return self.t[:]

class Prog:
    def __init__(self, nc, es):
        self.nc = nc
        self.es = es
        self.queues = {n: Q(n, 16 if n.endswith("d") else 1)
                       for n in ("T", "V", "S", "G", "Yd", "Gd", "Sd")}
        self.sems = {}
        for n, q in self.queues.items():
            if q.is_dma:
                for s in range(NSLOT):
                    self.sems[f"{n}{s}"] = es.enter_context(nc.semaphore(f"sem_{n}{s}"))
            else:
                self.sems[n] = es.enter_context(nc.semaphore(f"sem_{n}"))
        self.streams = {e: [] for e in ("tensor", "vector", "scalar", "gpsimd", "sync")}
        self.hw = {e: {} for e in self.streams}

    def sb(self, name, shape, dt, es=None):
        return Buf((es or self.es).enter_context(
            self.nc.sbuf_tensor(name, list(shape), dt)), name)

    def ps(self, name, shape, dt=F32, es=None):
        return Buf((es or self.es).enter_context(
            self.nc.psum_tensor(name, list(shape), dt)), name)

    def dram(self, name, shape, dt):
        return Buf(self.nc.dram_tensor(name, list(shape), dt), name)

    def _edges(self, qname, reads, writes, skip_self_T=True):
        eng = ENGINE_OF[qname]
        need = {}
        def upd(e):
            if e is None:
                return
            sk, c = e
            if sk == "T" and qname == "T" and skip_self_T:
                return
            if need.get(sk, -1) < c:
                need[sk] = c
        for b in reads:
            upd(b.w)
        for b in writes:
            upd(b.w)
            for sk, c in b.readers.items():
                upd((sk, c))
        hw = self.hw[eng]
        out = []
        for sk in sorted(need):
            c = need[sk]
            if c <= 0 or hw.get(sk, 0) >= c:
                continue
            out.append((sk, c))
            hw[sk] = c
        return out

    def _push(self, qname, waits, emit, incsem):
        self.streams[ENGINE_OF[qname]].append((waits, emit, incsem))

    def _record(self, semkey, val, reads, writes):
        for b in writes:
            b.w = (semkey, val)
            b.readers = {}
        for b in reads:
            if b.readers.get(semkey, -1) < val:
                b.readers[semkey] = val

    def op(self, qname, emit, reads=(), writes=()):
        q = self.queues[qname]
        assert not q.is_dma
        waits = self._edges(qname, reads, writes)
        q.cnt += 1
        self._push(qname, waits, emit, (qname, 1))
        self._record(qname, q.cnt, reads, writes)

    def dma(self, qname, dst_ap, src_ap, reads=(), writes=()):
        q = self.queues[qname]
        assert q.is_dma
        slot = q.issue % NSLOT
        q.issue += 1
        semkey = f"{qname}{slot}"
        waits = []
        prior = q.slot_cnt[slot] * 16
        eng = ENGINE_OF[qname]
        if prior > 0 and self.hw[eng].get(semkey, 0) < prior:
            waits.append((semkey, prior))
            self.hw[eng][semkey] = prior
        waits += self._edges(qname, reads, writes)
        q.slot_cnt[slot] += 1
        self._push(qname, waits,
                   lambda e: e.dma_start(dst_ap, src_ap), (semkey, 16))
        self._record(semkey, q.slot_cnt[slot] * 16, reads, writes)

    def mm(self, out_ap, lhsT_ap, rhs_ap, psum_buf, reads=(), start=True, stop=True):
        self.op("T", lambda e: e.matmul(out_ap, lhsT_ap, rhs_ap, start=start, stop=stop),
                reads=list(reads), writes=[psum_buf])

    def mm_group(self, psum_buf, mms, reads=(), start=True, stop=True):
        waits = self._edges("T", list(reads), [psum_buf])
        q = self.queues["T"]
        q.cnt += 1
        n = len(mms)
        def emit(e, mms=tuple(mms), n=n, start=start, stop=stop):
            ins = None
            for i, (o, l, r) in enumerate(mms):
                ins = e.matmul(o, l, r, start=(start and i == 0),
                               stop=(stop and i == n - 1))
            return ins
        self._push("T", waits, emit, ("T", 1))
        self._record("T", q.cnt, list(reads), [psum_buf])

    def _cur_semvals(self):
        vals = {}
        for n, q in self.queues.items():
            if q.is_dma:
                for s in range(NSLOT):
                    if q.slot_cnt[s] > 0:
                        vals[f"{n}{s}"] = q.slot_cnt[s] * 16
            else:
                if q.cnt > 0:
                    vals[n] = q.cnt
        return vals

    def wait_all(self, qname):
        eng = ENGINE_OF[qname]
        hw = self.hw[eng]
        waits = []
        own = {qname} if not self.queues[qname].is_dma else \
            {f"{qname}{s}" for s in range(NSLOT)}
        for sk, c in sorted(self._cur_semvals().items()):
            if sk in own:
                continue
            if hw.get(sk, 0) < c:
                waits.append((sk, c))
                hw[sk] = c
        if waits:
            self._push(qname, waits, None, None)

    def barrier(self, qnames=("T", "V", "S", "Yd", "Gd", "Sd")):
        for qa in qnames:
            self.wait_all(qa)

    def finalize(self, block):
        def mk(engname):
            items = self.streams[engname]
            def body(eng):
                for waits, emit, incsem in items:
                    for sk, c in waits:
                        eng.wait_ge(self.sems[sk], c)
                    if emit is not None:
                        ins = emit(eng)
                        if incsem is not None:
                            ins.then_inc(self.sems[incsem[0]], incsem[1])
            return body
        block.tensor(mk("tensor"))
        block.vector(mk("vector"))
        block.scalar(mk("scalar"))
        block.gpsimd(mk("gpsimd"))
        block.sync(mk("sync"))



def _rope_tables():
    inv_freq = 1.0 / (10000.0 ** (np.arange(0, DH, 2, dtype=np.float64) / DH))
    t = np.arange(-HALO, S, dtype=np.float64)
    freqs = t[:, None] * inv_freq[None, :]
    cos = np.cos(freqs)
    sin = np.sin(freqs)
    cosT = np.concatenate([cos.T, cos.T], axis=0)
    sinT = np.concatenate([-sin.T, sin.T], axis=0)
    return cosT, sinT

def _masks(core):
    i = np.arange(128)[:, None]
    j = np.arange(128)[None, :]
    m_far = np.where(j > i, 0.0, NEG)
    m_causal = np.where(j <= i, 0.0, NEG)
    rel4 = i - j
    rel3 = i + 128 - j
    m_om4 = np.where((rel4 >= 0) & (rel4 < W_OM), 0.0, NEG)
    m_om3 = np.where((rel3 >= 0) & (rel3 < W_OM), 0.0, NEG)
    d_om4 = np.where((rel4 >= 0) & (rel4 < W_OM),
                     DECAY ** np.clip(rel4, 0, 255), 0.0)
    d_om3 = np.where((rel3 >= 0) & (rel3 < W_OM),
                     DECAY ** np.clip(rel3, 0, 255), 0.0)
    swa_cat = np.zeros((8, 128, 640))
    om_cat = np.zeros((8, 128, 256))
    for qi in range(8):
        for b in range(5):
            blk = np.zeros((128, 128))
            if b == 0:
                blk = blk + m_far
            elif b == 4:
                blk = blk + m_causal
            if core == 0 and (qi + b) < 4:
                blk = blk + NEG
            swa_cat[qi, :, 128*b:128*(b+1)] = blk
        om_cat[qi, :, 0:128] = m_om3 + (NEG if (core == 0 and qi == 0) else 0.0)
        om_cat[qi, :, 128:256] = m_om4
    np.clip(swa_cat, -3e38, 0, out=swa_cat)
    np.clip(om_cat, -3e38, 0, out=om_cat)
    d_cat = np.concatenate([d_om3, d_om4], axis=1)
    return dict(
        swa_cat=np.ascontiguousarray(
            swa_cat.transpose(1, 0, 2).reshape(128, 8 * 640)).astype(BFD),
        om_cat=np.ascontiguousarray(
            om_cat.transpose(1, 0, 2).reshape(128, 8 * 256)).astype(BFD),
        d_cat=d_cat.astype(np.float32))

def prep_inputs(inp):
    x = np.asarray(inp["L0_attn_x"])[0]
    shared = {}
    wqkv = np.asarray(inp["L0_attn_w_qkv"])
    cols = []
    for h in range(NH):
        cols.append(wqkv[:, h*DH:(h+1)*DH])
        cols.append(wqkv[:, D + h*DH:D + (h+1)*DH])
        cols.append(wqkv[:, 2*D + h*DH:2*D + (h+1)*DH])
    shared["wqkv_p"] = np.ascontiguousarray(np.concatenate(cols, axis=1))
    shared["qscale"] = np.ascontiguousarray(np.asarray(inp["L0_attn_q_scale"]).reshape(DH, 1))
    shared["kscale"] = np.ascontiguousarray(np.asarray(inp["L0_attn_k_scale"]).reshape(DH, 1))
    shared["n1w"] = np.ascontiguousarray(np.asarray(inp["L0_attn_norm1_w"]).reshape(NDC, 128).T)
    shared["n2w"] = np.ascontiguousarray(np.asarray(inp["L0_attn_norm2_w"]).reshape(NDC, 128).T)
    shared["gg1"] = np.asarray(inp["L0_attn_gg_w1"])
    shared["gg2"] = np.asarray(inp["L0_attn_gg_w2"])
    shared["wo"] = np.asarray(inp["L0_attn_w_o"])
    shared["mw1"] = np.asarray(inp["L0_attn_mem_w1"])
    shared["mw2"] = np.asarray(inp["L0_attn_mem_w2"])
    z = np.zeros((D, FFHP - FFH), np.float32)
    shared["fw1"] = np.ascontiguousarray(np.concatenate([np.asarray(inp["L0_attn_ffn_w1"]), z], axis=1))
    shared["fw3"] = np.ascontiguousarray(np.concatenate([np.asarray(inp["L0_attn_ffn_w3"]), z], axis=1))
    shared["fw2"] = np.ascontiguousarray(np.concatenate(
        [np.asarray(inp["L0_attn_ffn_w2"]), np.zeros((FFHP - FFH, D), np.float32)], axis=0))
    shared["wg"] = np.ascontiguousarray(np.asarray(inp["L0_ffn_Wg"]).T)
    shared["wu"] = np.ascontiguousarray(np.asarray(inp["L0_ffn_Wu"]).T)
    shared["wd"] = np.ascontiguousarray(np.asarray(inp["L0_ffn_Wd"]).T)
    shared["wr"] = np.ascontiguousarray(np.asarray(inp["L0_ffn_Wr"]).T)
    Ag = np.asarray(inp["L0_ffn_Ag"])
    Au = np.asarray(inp["L0_ffn_Au"])
    a_all = np.zeros((D, 8 * 128), np.float32)
    for t in range(4):
        a_all[:, 128*t + 0:128*t + 16] = Ag[2*t].T
        a_all[:, 128*t + 64:128*t + 80] = Ag[2*t + 1].T
        base = 512 + 128*t
        a_all[:, base + 0:base + 16] = Au[2*t].T
        a_all[:, base + 64:base + 80] = Au[2*t + 1].T
    shared["a_all"] = a_all
    shared["bg"] = np.ascontiguousarray(np.asarray(inp["L0_ffn_Bg"]).transpose(0, 2, 1))
    shared["bu"] = np.ascontiguousarray(np.asarray(inp["L0_ffn_Bu"]).transpose(0, 2, 1))
    shared["ad"] = np.ascontiguousarray(np.asarray(inp["L0_ffn_Ad"]).transpose(0, 2, 1))
    shared["bd"] = np.ascontiguousarray(
        np.asarray(inp["L0_ffn_Bd"]).transpose(0, 2, 1).reshape(NEXP * R, D))
    shared["ident"] = np.eye(128, dtype=BFD)
    shared["identf"] = np.eye(128, dtype=np.float32)
    shared["ones"] = np.ones((128, 128), np.float32)
    shared["onesb"] = np.ones((128, 128), dtype=BFD)
    epsb = np.zeros((128, 2), np.float32); epsb[:, 0] = EPS_RMS; epsb[:, 1] = EPS_LN
    shared['epsb'] = epsb

    cosT, sinT = _rope_tables()
    per_core = []
    for c in range(NCORES):
        gs = c * T
        m = {}
        lo = gs - HALO
        if lo < 0:
            xs = np.concatenate([np.zeros((HALO, D), np.float32),
                                 x[0:gs + T]], axis=0)
        else:
            xs = x[lo:gs + T]
        m["x_f"] = np.ascontiguousarray(xs.T)
        m["cosq"] = np.ascontiguousarray(cosT[:, HALO + gs: HALO + gs + T] * SCALE).astype(BFD)
        m["sinq"] = np.ascontiguousarray(sinT[:, HALO + gs: HALO + gs + T] * SCALE).astype(BFD)
        m["cosk"] = np.ascontiguousarray(cosT[:, gs: gs + TH]).astype(BFD)
        m["sink"] = np.ascontiguousarray(sinT[:, gs: gs + TH]).astype(BFD)
        m.update(_masks(c))
        per_core.append(m)
    return shared, per_core



IN_SPECS = [
    ("x_f", [D, TH], F32), ("wqkv_p", [D, 3 * D], F32),
    ("qscale", [DH, 1], F32), ("kscale", [DH, 1], F32),
    ("n1w", [128, NDC], F32), ("n2w", [128, NDC], F32),
    ("gg1", [D, GGH], F32), ("gg2", [GGH, 1], F32),
    ("wo", [D, D], F32), ("mw1", [D, MEMF], F32), ("mw2", [MEMF, D], F32),
    ("fw1", [D, FFHP], F32), ("fw3", [D, FFHP], F32), ("fw2", [FFHP, D], F32),
    ("wg", [D, DFF], F32), ("wu", [D, DFF], F32), ("wd", [DFF, D], F32),
    ("wr", [D, NEXP], F32), ("a_all", [D, 1024], F32),
    ("bg", [NEXP, R, DFF], F32), ("bu", [NEXP, R, DFF], F32),
    ("ad", [NEXP, DFF, R], F32), ("bd", [NEXP * R, D], F32),
    ("ident", [128, 128], BF16), ("identf", [128, 128], F32),
    ("onesb", [128, 128], BF16),
    ("ones", [128, 128], F32),
    ("swa_cat", [128, 8 * 640], BF16), ("om_cat", [128, 8 * 256], BF16),
    ("d_cat", [128, 256], F32),
    ("cosq", [128, T], BF16), ("sinq", [128, T], BF16),
    ("cosk", [128, TH], BF16), ("sink", [128, TH], BF16),
    ("epsb", [128, 2], F32),
]

CONST_NAMES = ["ident", "identf", "ones", "onesb", "n1w", "n2w", "qscale", "kscale", "epsb"]
P1_CONST_NAMES = ["swa_cat", "om_cat", "d_cat",
                  "cosq", "sinq", "cosk", "sink"]

def build(phases=("P0",), debug_taps=()):
    nc = bass.Bass("TRN2", target_bir_lowering=False)
    es = ExitStack()
    p = Prog(nc, es)
    dbg = {}

    I = {nm: Buf(nc.dram_tensor(nm, list(shape), dt, kind="ExternalInput"), nm)
         for nm, shape, dt in IN_SPECS}
    out = Buf(nc.dram_tensor("out", [NDC, 128, T], F32, kind="ExternalOutput"), "out")

    SC = {
        "attn": p.dram("sc_attn", [NH, 128, T], BF16),
        "xa": p.dram("sc_xa", [NDC, 128, T], F32),
        "r": p.dram("sc_r", [NDC, 128, T], F32),
        "x1b": p.dram("sc_x1b", [NDC, 128, T], BF16),
        "x1f": p.dram("sc_x1f", [NDC, 128, T], F32),
        "r2": p.dram("sc_r2", [NDC, 128, T], F32),
    }

    def tap(name, shape, dt=F32):
        if name in debug_taps:
            b = Buf(nc.dram_tensor("dbg_" + name, list(shape), dt,
                                   kind="ExternalOutput"), "dbg_" + name)
            dbg[name] = b
            return b
        return None

    with es:
        C = {}
        for nm in CONST_NAMES:
            shape = dict((n, s) for n, s, _ in IN_SPECS)[nm]
            dt = dict((n, d) for n, s, d in IN_SPECS)[nm]
            C[nm] = p.sb("c_" + nm, shape, dt)
            p.dma("Yd", C[nm].ap, I[nm].ap, writes=[C[nm]])

        es_p03 = es.enter_context(ExitStack())
        rms2ss = p.sb("rms2ss", [1, T], F32, es=es_p03)

        with ExitStack() as es_p02:
            qmem = [p.sb(f"qmem{h}", [128, T], BF16, es=es_p02) for h in range(NH)]
            hbuf = [p.sb(f"h{i}", [128, TH], BF16, es=es_p02) for i in range(NDC)]
            gammaT = p.sb("gammaT", [128, 12], F32, es=es_p02)

            if "P0" in phases:
                _p0(p, nc, I, C, hbuf, gammaT, tap)
            p1sel = [ph for ph in phases if ph.startswith("P1")]
            if p1sel:
                parts = p1sel[0].split(":")
                upto = parts[1] if len(parts) > 1 else "full"
                nh = int(parts[2]) if len(parts) > 2 else NH
                _p1(p, nc, I, C, hbuf, gammaT, qmem, SC, tap,
                    heads=range(nh), upto=upto)

            if "P2" in phases:
                _p2(p, nc, I, C, hbuf, qmem, SC, rms2ss, tap)

        if "P3" in phases:
            _p3(p, nc, I, C, SC, rms2ss, tap)
        es_p03.close()

        ln2s = p.sb("ln2s", [1, 2 * T], F32)
        if "P4" in phases:
            _p4(p, nc, I, C, SC, ln2s, tap)
        if "P5" in phases:
            _p5(p, nc, I, C, SC, ln2s, out, tap)

        if "OUT_STUB" in phases:
            with ExitStack() as e2:
                z = p.sb("zstub", [128, D], F32, es=e2)
                p.op("V", lambda e: e.memset(z.ap, 0.0), writes=[z])
                for i in range(T // 128):
                    p.dma("Yd", out.t[i * 128:(i + 1) * 128, :], z.ap,
                          reads=[z], writes=[out])

        p.barrier()

        with nc.Block() as block:
            p.finalize(block)
    return nc, dbg



def _p0(p, nc, I, C, hbuf, gammaT, tap):
    with ExitStack() as e0:
        xs = [p.sb(f"p0x{i}", [128, TH], F32, es=e0) for i in range(2)]
        sqs = [p.sb(f"p0sq{i}", [128, TH], BF16, es=e0) for i in range(2)]
        rstd_row = p.sb("p0rstd", [1, TH], F32, es=e0)
        rstd_sb = p.sb("p0rbs", [128, TH], F32, es=e0)
        gg1b = p.sb("p0gg1", [128, NDC * GGH], BF16, es=e0)
        gg2b = p.sb("p0gg2", [GGH, 1], BF16, es=e0)
        gsil = p.sb("p0gs", [GGH, TH], BF16, es=e0)

        p.dma("Gd", gg1b.t[:].rearrange("p (k c) -> p k c", c=GGH),
              I["gg1"].ap.rearrange("(k p) c -> p k c", p=128), writes=[gg1b])
        p.dma("Gd", gg2b.ap, I["gg2"].ap, writes=[gg2b])

        with ExitStack() as eA:
            ssum = p.ps("p0ss", [1, TH], F32, es=eA)
            rstd_b = p.ps("p0rb", [128, TH], F32, es=eA)
            for i in range(NDC):
                xb, sq = xs[i % 2], sqs[i % 2]
                p.dma("Yd", xb.ap, I["x_f"].t[i * 128:(i + 1) * 128, :], writes=[xb])
                p.op("S", lambda e, xb=xb, sq=sq: e.activation(sq.ap, xb.ap, AF.Square),
                     reads=[xb], writes=[sq])
                for t0 in range(0, TH, 512):
                    p.mm(ssum.t[:, t0:t0 + 512], C["onesb"].t[:, 0:1],
                         sq.t[:, t0:t0 + 512], ssum, reads=[C["onesb"], sq],
                         start=(i == 0), stop=(i == NDC - 1))
            p.op("S", lambda e: e.activation(rstd_row.ap, ssum.ap, AF.Ln,
                                             bias=C['epsb'].t[0:1, 0:1], scale=1.0 / D),
                 reads=[ssum], writes=[rstd_row])
            p.op("S", lambda e: e.activation(rstd_row.ap, rstd_row.ap, AF.Exp,
                                             scale=-0.5),
                 reads=[rstd_row], writes=[rstd_row])
            for t0 in range(0, TH, 512):
                p.mm(rstd_b.t[:, t0:t0 + 512], C["ones"].t[0:1, :],
                     rstd_row.t[:, t0:t0 + 512], rstd_b,
                     reads=[C["ones"], rstd_row], start=True, stop=True)
            p.op("V", lambda e: e.tensor_copy(rstd_sb.ap, rstd_b.ap),
                 reads=[rstd_b], writes=[rstd_sb])
            for i in range(NDC):
                xb = xs[i % 2]
                p.dma("Yd", xb.ap, I["x_f"].t[i * 128:(i + 1) * 128, :], writes=[xb])
                p.op("V", lambda e, i=i, xb=xb: e.scalar_tensor_tensor(
                        hbuf[i].ap, xb.ap, C["n1w"].t[:, i:i + 1], rstd_sb.ap,
                        ALU.mult, ALU.mult),
                     reads=[xb, C["n1w"], rstd_sb], writes=[hbuf[i]])
            p.barrier()
        with ExitStack() as eB:
            gpre = p.ps("p0gp", [GGH, TH], F32, es=eB)
            grow = p.ps("p0gr", [128, 12], F32, es=eB)
            gg1v = gg1b.t[:].rearrange("p (k c) -> p k c", c=GGH)
            for t0 in range(0, TH, 512):
                p.mm_group(gpre,
                           [(gpre.t[:, t0:t0 + 512], gg1v[:, i, :],
                             hbuf[i].t[:, t0:t0 + 512]) for i in range(NDC)],
                           reads=[gg1b] + hbuf)
            p.op("S", lambda e: e.activation(gsil.ap, gpre.ap, AF.Silu),
                 reads=[gpre], writes=[gsil])
            for j in range(12):
                p.mm(grow.t[:, j:j + 1], gsil.t[:, j * 128:(j + 1) * 128], gg2b.ap,
                     grow, reads=[gsil, gg2b], start=True, stop=True)
            p.op("S", lambda e: e.activation(gammaT.ap, grow.ap, AF.Sigmoid),
                 reads=[grow], writes=[gammaT])

            tp = tap("h", [NDC, 128, TH], BF16)
            if tp is not None:
                for i in range(NDC):
                    p.dma("Yd", tp.t[i, :, :], hbuf[i].ap, reads=[hbuf[i]], writes=[tp])
            tp = tap("gammaT", [128, 12])
            if tp is not None:
                p.dma("Yd", tp.ap, gammaT.ap, reads=[gammaT], writes=[tp])
            p.barrier()


def _p1(p, nc, I, C, hbuf, gammaT, qmem, SC, tap, heads=None, upto="full"):
    with ExitStack() as e1:
        ispec = {nm: (s, d) for nm, s, d in IN_SPECS}
        for nm in P1_CONST_NAMES:
            C[nm] = p.sb("c_" + nm, ispec[nm][0], ispec[nm][1], es=e1)
            p.dma("Yd", C[nm].ap, I[nm].ap, writes=[C[nm]])
        wb = [p.sb(f"p1w{i}", [128, NDC * 384], BF16, es=e1) for i in range(2)]
        q_work = p.sb("p1qw", [128, T], F32, es=e1)
        k_work = p.sb("p1kw", [128, TH], F32, es=e1)
        row = p.sb("p1row", [1, 512], F32, es=e1)
        q1 = p.sb("p1q1", [128, T], F32, es=e1)
        k1 = p.sb("p1k1", [128, TH], F32, es=e1)
        t1 = p.sb("p1t1", [128, TH], F32, es=e1)
        t2 = p.sb("p1t2", [128, TH], F32, es=e1)
        sq = p.sb("p1sqb", [128, 512], BF16, es=e1)
        qr = t2
        kr = t2
        q2 = p.sb("p1q2", [128, T], BF16, es=e1)
        k2 = p.sb("p1k2", [128, TH], BF16, es=e1)
        vts = [p.sb(f"p1vt{i}", [128, 12 * 128], BF16, es=e1) for i in range(2)]
        kpts = [p.sb(f"p1kpt{i}", [128, 12 * 128], BF16, es=e1) for i in range(2)]
        sms_ = [p.sb(f"p1sm{i}", [128, 640], F32, es=e1) for i in range(2)]
        pns = [p.sb(f"p1pn{i}", [128, 640], BF16, es=e1) for i in range(2)]
        oms = [p.sb(f"p1om{i}", [128, 256], F32, es=e1) for i in range(2)]
        poms = [p.sb(f"p1pom{i}", [128, 256], BF16, es=e1) for i in range(2)]
        pTs = [p.sb(f"p1pT{i}", [128, 896], BF16, es=e1) for i in range(2)]
        attnhs = [p.sb(f"p1ah{i}", [128, T], BF16, es=e1) for i in range(2)]
        mcols = [p.sb(f"p1m{i}", [128, 1], F32, es=e1) for i in range(2)]
        sumcs = [p.sb(f"p1su{i}", [128, 1], F32, es=e1) for i in range(2)]
        osums = [p.sb(f"p1os{i}", [128, 1], F32, es=e1) for i in range(2)]
        recs = [p.sb(f"p1re{i}", [128, 1], F32, es=e1) for i in range(2)]
        orecs = [p.sb(f"p1or{i}", [128, 1], F32, es=e1) for i in range(2)]
        PJ = [p.ps(f"p1pj{i}", [128, 512], F32, es=e1) for i in range(2)]
        PS = [p.ps(f"p1ps{i}", [128, 640], F32, es=e1) for i in range(2)]
        TP = p.ps("p1ptp", [128, 896], BF16, es=e1)
        PO = p.ps("p1po", [128, 512], F32, es=e1)
        ROW = PJ[0]
        BC = PJ[1]

        def wdma(hd, dst):
            src = I["wqkv_p"].t[:, 384 * hd:384 * (hd + 1)].rearrange(
                "(k p) c -> p k c", p=128)
            p.dma("Gd", dst.t[:].rearrange("p (k c) -> p k c", c=384), src,
                  writes=[dst])

        wdma(0, wb[0])
        tap_q2 = tap("q2h0", [128, T], BF16)
        tap_k2 = tap("k2h0", [128, TH], BF16)
        tap_attn = tap("attn", [NH, 128, T], BF16)
        tap_qm = tap("qmemt", [NH, 128, T], BF16)

        def proj_items(hd):
            wcur = wb[hd % 2]
            wv = wcur.t[:].rearrange("p (k c) -> p k c", c=384)
            vt = vts[hd % 2]
            items = []
            pjc = [0]
            def nxt():
                b = PJ[pjc[0] % 2]
                pjc[0] += 1
                return b
            for tt in range(2):
                def it(tt=tt):
                    PJb = nxt()
                    p.mm_group(PJb, [(PJb.ap, wv[:, k, 0:128],
                                      hbuf[k].t[:, HALO + 512 * tt:HALO + 512 * (tt + 1)])
                                     for k in range(NDC)], reads=[wcur] + hbuf)
                    p.op("S", lambda e, tt=tt, PJb=PJb: e.copy(
                            q_work.t[:, 512 * tt:512 * (tt + 1)], PJb.ap),
                         reads=[PJb], writes=[q_work])
                items.append(it)
            for tt in range(3):
                def it(tt=tt):
                    PJb = nxt()
                    p.mm_group(PJb, [(PJb.ap, wv[:, k, 128:256],
                                      hbuf[k].t[:, 512 * tt:512 * (tt + 1)])
                                     for k in range(NDC)], reads=[wcur] + hbuf)
                    p.op("S", lambda e, tt=tt, PJb=PJb: e.copy(
                            k_work.t[:, 512 * tt:512 * (tt + 1)], PJb.ap),
                         reads=[PJb], writes=[k_work])
                items.append(it)
            for j in range(12):
                def it(j=j):
                    PJb = nxt()
                    p.mm_group(PJb, [(PJb.t[:, 0:128],
                                      hbuf[k].t[:, 128 * j:128 * (j + 1)],
                                      wv[:, k, 256:384]) for k in range(NDC)],
                               reads=[wcur] + hbuf)
                    p.op("S", lambda e, j=j, PJb=PJb, vt=vt: e.copy(
                            vt.t[:, 128 * j:128 * (j + 1)], PJb.t[:, 0:128]),
                         reads=[PJb], writes=[vt])
                items.append(it)
            return items

        def proj(hd):
            for it in proj_items(hd):
                it()

        hlist = list(heads if heads is not None else range(NH))
        proj(hlist[0])
        if len(hlist) > 1:
            wdma(hlist[1], wb[hlist[1] % 2])
        for hd in hlist:
            vt = vts[hd % 2]
            kpt = kpts[hd % 2]
            attnh = attnhs[hd % 2]
            nxt_items = []
            if hd + 1 < NH:
                nxt_items = proj_items(hd + 1)
                if hd + 2 < NH:
                    wdma(hd + 2, wb[(hd + 2) % 2])

            def headnorm(work, scale_c, out_b, n):
                for tt in range(n // 512):
                    p.op("S", lambda e, tt=tt: e.activation(
                            sq.ap, work.t[:, 512 * tt:512 * (tt + 1)], AF.Square),
                         reads=[work], writes=[sq])
                    p.mm(ROW.t[0:1, 0:512], C["onesb"].t[:, 0:1],
                         sq.ap, ROW, reads=[C["onesb"], sq])
                    p.op("S", lambda e: e.activation(
                            row.ap, ROW.t[0:1, 0:512],
                            AF.Ln, bias=C["epsb"].t[0:1, 0:1], scale=1.0 / DH),
                         reads=[ROW, C["epsb"]], writes=[row])
                    p.op("S", lambda e: e.activation(row.ap, row.ap,
                                                     AF.Exp, scale=-0.5),
                         reads=[row], writes=[row])
                    p.mm(BC.t[:, 0:512], C["ones"].t[0:1, :], row.ap,
                         BC, reads=[C["ones"], row])
                    p.op("V", lambda e, tt=tt: e.scalar_tensor_tensor(
                            out_b.t[:, 512 * tt:512 * (tt + 1)],
                            work.t[:, 512 * tt:512 * (tt + 1)],
                            C[scale_c].ap, BC.t[:, 0:512], ALU.mult, ALU.mult),
                         reads=[work, C[scale_c], BC], writes=[out_b])
            if hd == 0:
                tpx = tap("qworkh0", [128, T], F32)
                if tpx is not None:
                    p.dma("Yd", tpx.ap, q_work.ap, reads=[q_work], writes=[tpx])
            if upto == "qwork":
                continue
            headnorm(q_work, "qscale", q1, T)
            if hd == 0:
                tpx = tap("q1h0", [128, T], F32)
                if tpx is not None:
                    p.dma("Yd", tpx.ap, q1.ap, reads=[q1], writes=[tpx])
                tpx = tap("rowh0", [1, T], F32)
                if tpx is not None:
                    p.dma("Yd", tpx.ap, row.t[:, 0:T], reads=[row], writes=[tpx])
            headnorm(k_work, "kscale", k1, TH)

            def rope(src, rot, cos_c, sin_c, out_b, n):
                p.op("V", lambda e: e.tensor_copy(rot.t[0:64, 0:n], src.t[64:128, 0:n]),
                     reads=[src], writes=[rot])
                p.op("V", lambda e: e.tensor_copy(rot.t[64:128, 0:n], src.t[0:64, 0:n]),
                     reads=[src], writes=[rot])
                p.op("V", lambda e: e.tensor_tensor(t1.t[:, 0:n], src.t[:, 0:n],
                                                    C[cos_c].t[:, 0:n], ALU.mult),
                     reads=[src, C[cos_c]], writes=[t1])
                p.op("V", lambda e: e.tensor_tensor(t2.t[:, 0:n], rot.t[:, 0:n],
                                                    C[sin_c].t[:, 0:n], ALU.mult),
                     reads=[rot, C[sin_c]], writes=[t2])
                p.op("V", lambda e: e.tensor_tensor(out_b.t[:, 0:n], t1.t[:, 0:n],
                                                    t2.t[:, 0:n], ALU.add),
                     reads=[t1, t2], writes=[out_b])
            if upto == "norm":
                continue
            for _ in range(6):
                if nxt_items:
                    nxt_items.pop(0)()
            rope(q1, qr, "cosq", "sinq", q2, T)
            rope(k1, kr, "cosk", "sink", k2, TH)

            for j in range(12):
                sl = 128 * (j % 7)
                p.op("T", lambda e, j=j, sl=sl: e.transpose(
                        TP.t[:, sl:sl + 128], k2.t[:, 128 * j:128 * (j + 1)],
                        C["ident"].ap),
                     reads=[k2, C["ident"]], writes=[TP])
                p.op("S", lambda e, j=j, sl=sl, kpt=kpt: e.activation(
                        kpt.t[:, 128 * j:128 * (j + 1)], TP.t[:, sl:sl + 128],
                        AF.Copy, scale=gammaT.t[:, j:j + 1]),
                     reads=[TP, gammaT], writes=[kpt])

            def stA(qi):
                ks = 128 * qi
                par = qi % 2
                PSb = PS[par]
                p.mm(PSb.t[:, 0:512], q2.t[:, ks:ks + 128],
                     k2.t[:, ks:ks + 512], PSb, reads=[q2, k2])
                p.mm(PSb.t[:, 512:640], q2.t[:, ks:ks + 128],
                     k2.t[:, ks + 512:ks + 640], PSb, reads=[q2, k2])

            def stB(qi):
                par = qi % 2
                PSb, smb, pnb, omb, pomb = PS[par], sms_[par], pns[par], oms[par], poms[par]
                mcol, sumc, osum, rec, orec = (mcols[par], sumcs[par], osums[par],
                                               recs[par], orecs[par])
                p.op("V", lambda e, qi=qi, PSb=PSb, smb=smb: e.tensor_tensor(
                        smb.ap, PSb.ap, C["swa_cat"].t[:, 640 * qi:640 * (qi + 1)],
                        ALU.add),
                     reads=[PSb, C["swa_cat"]], writes=[smb])
                p.op("V", lambda e, smb=smb, mcol=mcol: e.reduce_max(
                        mcol.ap, smb.ap, axis=AX.X, negate=True),
                     reads=[smb], writes=[mcol])
                p.op("S", lambda e, smb=smb, pnb=pnb, mcol=mcol, sumc=sumc:
                     e.activation(pnb.ap, smb.ap, AF.Exp, bias=mcol.ap,
                                  accum_out=sumc.ap),
                     reads=[smb, mcol], writes=[pnb, sumc])
                p.op("V", lambda e, rec=rec, sumc=sumc: e.reciprocal(rec.ap, sumc.ap),
                     reads=[sumc], writes=[rec])
                p.op("V", lambda e, pnb=pnb, rec=rec: e.tensor_scalar(
                        pnb.ap, pnb.ap, rec.ap, None, ALU.mult),
                     reads=[pnb, rec], writes=[pnb])
                p.op("V", lambda e, qi=qi, smb=smb, omb=omb: e.tensor_tensor(
                        omb.ap, smb.t[:, 384:640],
                        C["om_cat"].t[:, 256 * qi:256 * (qi + 1)], ALU.add),
                     reads=[smb, C["om_cat"]], writes=[omb])
                p.op("S", lambda e, omb=omb, pomb=pomb, mcol=mcol, osum=osum:
                     e.activation(pomb.ap, omb.ap, AF.Exp, bias=mcol.ap,
                                  accum_out=osum.ap),
                     reads=[omb, mcol], writes=[pomb, osum])
                p.op("V", lambda e, orec=orec, osum=osum: e.reciprocal(
                        orec.ap, osum.ap),
                     reads=[osum], writes=[orec])
                p.op("V", lambda e, pomb=pomb, orec=orec: e.scalar_tensor_tensor(
                        pomb.ap, pomb.ap, orec.ap, C["d_cat"].ap,
                        ALU.mult, ALU.mult),
                     reads=[pomb, orec, C["d_cat"]], writes=[pomb])

            def stC(qi, hd):
                ks = 128 * qi
                par = qi % 2
                pnb, pomb, pTb = pns[par], poms[par], pTs[par]
                for b in range(5):
                    p.op("T", lambda e, b=b, pnb=pnb: e.transpose(
                            TP.t[:, 128 * b:128 * (b + 1)],
                            pnb.t[:, 128 * b:128 * (b + 1)], C["ident"].ap),
                         reads=[pnb, C["ident"]], writes=[TP])
                for oi in range(2):
                    p.op("T", lambda e, oi=oi, pomb=pomb: e.transpose(
                            TP.t[:, 640 + 128 * oi:640 + 128 * (oi + 1)],
                            pomb.t[:, 128 * oi:128 * (oi + 1)], C["ident"].ap),
                         reads=[pomb, C["ident"]], writes=[TP])
                p.op("V", lambda e, pTb=pTb: e.tensor_copy(pTb.ap, TP.ap),
                     reads=[TP], writes=[pTb])
                o0 = 256 * par
                p.mm_group(PO, [(PO.t[:, o0:o0 + 128],
                                 vt.t[:, 128 * (qi + b):128 * (qi + b + 1)],
                                 pTb.t[:, 128 * b:128 * (b + 1)]) for b in range(5)],
                           reads=[vt, pTb])
                p.mm_group(PO, [(PO.t[:, o0 + 128:o0 + 256],
                                 kpt.t[:, 128 * (qi + b):128 * (qi + b + 1)],
                                 pTb.t[:, 640 + 128 * (b - 3):640 + 128 * (b - 2)])
                                for b in (3, 4)],
                           reads=[kpt, pTb])
                p.op("S", lambda e, ks=ks, o0=o0, attnh=attnh: e.copy(
                        attnh.t[:, ks:ks + 128], PO.t[:, o0:o0 + 128]),
                     reads=[PO], writes=[attnh])
                p.op("S", lambda e, ks=ks, hd=hd, o0=o0: e.copy(
                        qmem[hd].t[:, ks:ks + 128], PO.t[:, o0 + 128:o0 + 256]),
                     reads=[PO], writes=[qmem[hd]])

            stA(0)
            stA(1)
            stB(0)
            for qi in range(8):
                if qi + 2 < 8:
                    stA(qi + 2)
                stC(qi, hd)
                for _ in range(3 if qi < 5 else 1):
                    if nxt_items:
                        nxt_items.pop(0)()
                if qi + 1 < 8:
                    stB(qi + 1)
            while nxt_items:
                nxt_items.pop(0)()

            p.dma("Yd", SC["attn"].t[hd, :, :], attnh.ap,
                  reads=[attnh], writes=[SC["attn"]])
            if tap_attn is not None:
                p.dma("Yd", tap_attn.t[hd, :, :], attnh.ap,
                      reads=[attnh], writes=[tap_attn])
            if tap_qm is not None:
                p.dma("Yd", tap_qm.t[hd, :, :], qmem[hd].ap,
                      reads=[qmem[hd]], writes=[tap_qm])
            if hd == 0:
                if tap_q2 is not None:
                    p.dma("Yd", tap_q2.ap, q2.ap, reads=[q2], writes=[tap_q2])
                if tap_k2 is not None:
                    p.dma("Yd", tap_k2.ap, k2.ap, reads=[k2], writes=[tap_k2])
                for nmt, bt, shp, dt in (("vth0", vt, [128, 1536], BF16),
                                         ("kpth0", kpt, [128, 1536], BF16)):
                    tpx = tap(nmt, shp, dt)
                    if tpx is not None:
                        p.dma("Yd", tpx.ap, bt.ap, reads=[bt], writes=[tpx])
        p.barrier()



def _p2(p, nc, I, C, hbuf, qmem, SC, rms2ss, tap):
    tap_m = tap("macc", [NDC, 128, T], F32)
    tap_xa = tap("xa", [NDC, 128, T], F32)
    tap_r = tap("r", [NDC, 128, T], F32)
    with ExitStack() as e2:
        w1b = [p.sb(f"p2w1{i}", [128, NDC * 128], BF16, es=e2) for i in range(2)]
        w2b = [p.sb(f"p2w2{i}", [128, 8 * 128], BF16, es=e2) for i in range(2)]
        wob = [p.sb(f"p2wo{i}", [128, NDC * 128], BF16, es=e2) for i in range(2)]
        macc = [p.sb(f"p2ma{m}", [128, T], F32, es=e2) for m in range(NDC)]
        poly = [p.sb(f"p2po{f}", [128, T], BF16, es=e2) for f in range(8)]
        scr = p.sb("p2scr", [128, 512], F32, es=e2)
        sqb = p.sb("p2sqb", [128, 512], BF16, es=e2)
        sg = p.sb("p2sg", [128, 512], F32, es=e2)
        t2b = sg
        xac = p.sb("p2xa", [128, T], F32, es=e2)
        rc = p.sb("p2rc", [128, T], F32, es=e2)
        xcs = [p.sb(f"p2xc{i}", [128, T], F32, es=e2) for i in range(2)]
        PH = [p.ps(f"p2ph{i}", [128, 512], F32, es=e2) for i in range(2)]
        PW = [p.ps(f"p2pw{i}", [128, 512], F32, es=e2) for i in range(2)]
        PA = [p.ps(f"p2pa{i}", [128, 512], F32, es=e2) for i in range(2)]
        RS = p.ps("p2rs", [1, T], F32, es=e2)

        def ldw1(idx, dst):
            p.dma("Gd", dst.t[:].rearrange("p (k c) -> p k c", c=128),
                  I["mw1"].t[:, 128 * idx:128 * (idx + 1)]
                  .rearrange("(k p) c -> p k c", p=128), writes=[dst])

        def ldw2(win, m, dst):
            p.dma("Gd", dst.t[:].rearrange("p (k c) -> p k c", c=128),
                  I["mw2"].t[1024 * win:1024 * (win + 1), 128 * m:128 * (m + 1)]
                  .rearrange("(k p) c -> p k c", p=128), writes=[dst])

        def ldwo(m, dst):
            p.dma("Gd", dst.t[:].rearrange("p (k c) -> p k c", c=128),
                  I["wo"].t[:, 128 * m:128 * (m + 1)]
                  .rearrange("(k p) c -> p k c", p=128), writes=[dst])

        for i in range(NDC):
            p.dma("Yd", hbuf[i].t[:, 0:T], SC["attn"].t[i, :, :],
                  reads=[SC["attn"]], writes=[hbuf[i]])
        ldw1(0, w1b[0])
        for win in range(8):
            for fs8 in range(8):
                idx = 8 * win + fs8
                wc = w1b[idx % 2]
                if idx + 1 < 64:
                    ldw1(idx + 1, w1b[(idx + 1) % 2])
                wv = wc.t[:].rearrange("p (k c) -> p k c", c=128)
                for th in range(2):
                    t0 = 512 * th
                    PHb = PH[th]
                    p.mm_group(PHb, [(PHb.ap, wv[:, k, :], qmem[k].t[:, t0:t0 + 512])
                                     for k in range(NDC)], reads=[wc] + qmem)
                    p.op("S", lambda e, PHb=PHb: e.activation(
                            scr.ap, PHb.ap, AF.Silu),
                         reads=[PHb], writes=[scr])
                    p.op("V", lambda e, fs8=fs8, t0=t0: e.scalar_tensor_tensor(
                            poly[fs8].t[:, t0:t0 + 512], scr.ap,
                            1.0, scr.ap, ALU.add, ALU.mult),
                         reads=[scr], writes=[poly[fs8]])
            if win == 0:
                ldw2(0, 0, w2b[0])
            for m in range(NDC):
                wc = w2b[m % 2]
                nwin, nm = (win, m + 1) if m + 1 < NDC else (win + 1, 0)
                if nwin < 8:
                    ldw2(nwin, nm, w2b[(m + 1) % 2])
                elif win == 7 and m == NDC - 1:
                    ldwo(0, wob[0])
                wv = wc.t[:].rearrange("p (k c) -> p k c", c=128)
                for th in range(2):
                    t0 = 512 * th
                    PWb = PW[th]
                    p.mm_group(PWb, [(PWb.ap, wv[:, k, :],
                                      poly[k].t[:, t0:t0 + 512])
                                     for k in range(8)], reads=[wc] + poly)
                    if win == 0:
                        p.op("V", lambda e, m=m, t0=t0, PWb=PWb: e.tensor_copy(
                                macc[m].t[:, t0:t0 + 512], PWb.ap),
                             reads=[PWb], writes=[macc[m]])
                    else:
                        p.op("V", lambda e, m=m, t0=t0, PWb=PWb: e.tensor_tensor(
                                macc[m].t[:, t0:t0 + 512],
                                macc[m].t[:, t0:t0 + 512], PWb.ap, ALU.add),
                             reads=[macc[m], PWb], writes=[macc[m]])
        p.dma("Yd", xcs[0].ap, I["x_f"].t[0:128, HALO:HALO + T], writes=[xcs[0]])
        for m in range(NDC):
            wc = wob[m % 2]
            if m + 1 < NDC:
                ldwo(m + 1, wob[(m + 1) % 2])
                p.dma("Yd", xcs[(m + 1) % 2].ap,
                      I["x_f"].t[128 * (m + 1):128 * (m + 2), HALO:HALO + T],
                      writes=[xcs[(m + 1) % 2]])
            wv = wc.t[:].rearrange("p (k c) -> p k c", c=128)
            xc = xcs[m % 2]
            for th in range(2):
                t0 = 512 * th
                PAb = PA[th]
                p.mm_group(PAb, [(PAb.ap, wv[:, k, :], hbuf[k].t[:, t0:t0 + 512])
                                 for k in range(NDC)], reads=[wc] + hbuf)
                p.op("S", lambda e, m=m, t0=t0: e.activation(
                        sg.ap, macc[m].t[:, t0:t0 + 512], AF.Sigmoid),
                     reads=[macc[m]], writes=[sg])
                p.op("V", lambda e, PAb=PAb: e.tensor_tensor(t2b.ap, sg.ap, PAb.ap,
                                                             ALU.mult),
                     reads=[sg, PAb], writes=[t2b])
                p.op("V", lambda e, xc=xc, t0=t0: e.tensor_tensor(
                        xac.t[:, t0:t0 + 512], xc.t[:, t0:t0 + 512], t2b.ap,
                        ALU.add),
                     reads=[xc, t2b], writes=[xac])
                p.op("V", lambda e, xc=xc, t0=t0: e.scalar_tensor_tensor(
                        rc.t[:, t0:t0 + 512], xc.t[:, t0:t0 + 512], 2.0, t2b.ap,
                        ALU.mult, ALU.add),
                     reads=[xc, t2b], writes=[rc])
                p.op("S", lambda e, t0=t0: e.activation(
                        sqb.ap, xac.t[:, t0:t0 + 512], AF.Square),
                     reads=[xac], writes=[sqb])
                p.mm(RS.t[:, t0:t0 + 512], C["onesb"].t[:, 0:1],
                     sqb.ap, RS, reads=[C["onesb"], sqb],
                     start=(m == 0), stop=(m == NDC - 1))
            p.dma("Yd", SC["xa"].t[m, :, :], xac.ap, reads=[xac], writes=[SC["xa"]])
            p.dma("Yd", SC["r"].t[m, :, :], rc.ap, reads=[rc], writes=[SC["r"]])
            if tap_m is not None:
                p.dma("Yd", tap_m.t[m, :, :], macc[m].ap,
                      reads=[macc[m]], writes=[tap_m])
            if tap_xa is not None:
                p.dma("Yd", tap_xa.t[m, :, :], xac.ap, reads=[xac], writes=[tap_xa])
            if tap_r is not None:
                p.dma("Yd", tap_r.t[m, :, :], rc.ap, reads=[rc], writes=[tap_r])
        p.op("V", lambda e: e.tensor_copy(rms2ss.ap, RS.ap),
             reads=[RS], writes=[rms2ss])
        p.barrier()



def _p3(p, nc, I, C, SC, rms2ss, tap):
    tap_y = tap("y", [NDC, 128, T], BF16)
    tap_rf = tap("rfin", [NDC, 128, T], F32)
    tap_x1 = tap("x1f", [NDC, 128, T], F32)
    NF3 = FFHP // 128
    with ExitStack() as e3:
        y = [p.sb(f"p3y{i}", [128, T], BF16, es=e3) for i in range(NDC)]
        g = [p.sb(f"p3g{f}", [128, T], BF16, es=e3) for f in range(NF3)]
        w1c = [p.sb(f"p3w1{i}", [128, NDC * 128], BF16, es=e3) for i in range(2)]
        w3c = [p.sb(f"p3w3{i}", [128, NDC * 128], BF16, es=e3) for i in range(2)]
        w2c = [p.sb(f"p3w2{i}", [128, NF3 * 128], BF16, es=e3) for i in range(2)]
        xc = [p.sb(f"p3xc{i}", [128, T], F32, es=e3) for i in range(2)]
        rfin = p.sb("p3rf", [128, T], F32, es=e3)
        scr = p.sb("p3scr", [128, T], F32, es=e3)
        sqb = p.sb("p3sqb", [128, 512], BF16, es=e3)
        rows = p.sb("p3rows", [1, 4 * T], F32, es=e3)
        R2 = rows.t[:, 0:T]; LS = rows.t[:, T:2*T]; LQ = rows.t[:, 2*T:3*T]
        MR = rows.t[:, 3*T:4*T]; R1 = R2
        bco = p.sb("p3bco", [128, T], F32, es=e3)
        mb = p.sb("p3mb", [128, T], F32, es=e3)
        x1b = p.sb("p3x1b", [128, T], BF16, es=e3)
        x1f = rfin
        BC = p.ps("p3bc", [128, 512], F32, es=e3)
        PH = [p.ps(f"p3ph{i}", [128, 512], F32, es=e3) for i in range(2)]
        LR = [p.ps(f"p3lr{i}", [1, 512], F32, es=e3) for i in range(4)]

        def bcast(src_slice, dst):
            for th in range(2):
                p.mm(BC.ap, C["ones"].t[0:1, :], src_slice[:, 512*th:512*(th+1)],
                     BC, reads=[C["ones"], rows])
                p.op("V", lambda e, th=th, dst=dst: e.tensor_copy(
                        dst.t[:, 512*th:512*(th+1)], BC.ap),
                     reads=[BC], writes=[dst])

        p.op("S", lambda e: e.activation(R2, rms2ss.ap, AF.Ln,
                                         bias=C["epsb"].t[0:1, 0:1], scale=1.0 / D),
             reads=[rms2ss, C["epsb"]], writes=[rows])
        p.op("S", lambda e: e.activation(R2, R2, AF.Exp, scale=-0.5),
             reads=[rows], writes=[rows])
        bcast(R2, bco)
        for i in range(NDC):
            xb = xc[i % 2]
            p.dma("Yd", xb.ap, SC["xa"].t[i, :, :], reads=[SC["xa"]], writes=[xb])
            p.op("V", lambda e, i=i, xb=xb: e.scalar_tensor_tensor(
                    y[i].ap, xb.ap, C["n2w"].t[:, i:i+1], bco.ap, ALU.mult, ALU.mult),
                 reads=[xb, C["n2w"], bco], writes=[y[i]])
            if tap_y is not None:
                p.dma("Yd", tap_y.t[i, :, :], y[i].ap, reads=[y[i]], writes=[tap_y])

        def ldw13(fs, b):
            for nm, dst in (("fw1", w1c[b]), ("fw3", w3c[b])):
                p.dma("Gd", dst.t[:].rearrange("p (k c) -> p k c", c=128),
                      I[nm].t[:, 128*fs:128*(fs+1)].rearrange("(k p) c -> p k c", p=128),
                      writes=[dst])
        ldw13(0, 0)
        for fs in range(NF3):
            if fs + 1 < NF3:
                ldw13(fs + 1, (fs + 1) % 2)
            wv1 = w1c[fs % 2].t[:].rearrange("p (k c) -> p k c", c=128)
            wv3 = w3c[fs % 2].t[:].rearrange("p (k c) -> p k c", c=128)
            for th in range(2):
                t0 = 512 * th
                p.mm_group(PH[0], [(PH[0].ap, wv1[:, k, :], y[k].t[:, t0:t0+512])
                                   for k in range(NDC)], reads=[w1c[fs % 2]] + y)
                p.op("S", lambda e, t0=t0: e.activation(scr.t[:, t0:t0+512],
                                                        PH[0].ap, AF.Silu),
                     reads=[PH[0]], writes=[scr])
                p.mm_group(PH[1], [(PH[1].ap, wv3[:, k, :], y[k].t[:, t0:t0+512])
                                   for k in range(NDC)], reads=[w3c[fs % 2]] + y)
                p.op("V", lambda e, fs=fs, t0=t0: e.tensor_tensor(
                        g[fs].t[:, t0:t0+512], scr.t[:, t0:t0+512], PH[1].ap,
                        ALU.mult),
                     reads=[scr, PH[1]], writes=[g[fs]])

        def ldw2(m, b):
            p.dma("Gd", w2c[b].t[:].rearrange("p (k c) -> p k c", c=128),
                  I["fw2"].t[:, 128*m:128*(m+1)].rearrange("(k p) c -> p k c", p=128),
                  writes=[w2c[b]])
        ldw2(0, 0)
        for m in range(NDC):
            if m + 1 < NDC:
                ldw2(m + 1, (m + 1) % 2)
            rb = xc[m % 2]
            p.dma("Yd", rb.ap, SC["r"].t[m, :, :], reads=[SC["r"]], writes=[rb])
            wv2 = w2c[m % 2].t[:].rearrange("p (k c) -> p k c", c=128)
            for th in range(2):
                t0 = 512 * th
                PHb = PH[th]
                p.mm_group(PHb, [(PHb.ap, wv2[:, k, :], g[k].t[:, t0:t0+512])
                                 for k in range(NF3)], reads=[w2c[m % 2]] + g)
                p.op("V", lambda e, t0=t0, rb=rb, PHb=PHb: e.tensor_tensor(
                        rfin.t[:, t0:t0+512], rb.t[:, t0:t0+512], PHb.ap, ALU.add),
                     reads=[rb, PHb], writes=[rfin])
                p.mm(LR[th].ap, C["ones"].t[:, 0:1], rfin.t[:, t0:t0+512],
                     LR[th], reads=[C["ones"], rfin],
                     start=(m == 0), stop=(m == NDC - 1))
                p.op("S", lambda e, t0=t0: e.activation(sqb.ap,
                                                        rfin.t[:, t0:t0+512],
                                                        AF.Square),
                     reads=[rfin], writes=[sqb])
                p.mm(LR[2+th].ap, C["onesb"].t[:, 0:1], sqb.ap,
                     LR[2+th], reads=[C["onesb"], sqb],
                     start=(m == 0), stop=(m == NDC - 1))
            p.dma("Yd", SC["r"].t[m, :, :], rfin.ap, reads=[rfin], writes=[SC["r"]])
            if tap_rf is not None:
                p.dma("Yd", tap_rf.t[m, :, :], rfin.ap, reads=[rfin], writes=[tap_rf])

        for th in range(2):
            p.op("V", lambda e, th=th: e.tensor_copy(LS[:, 512*th:512*(th+1)],
                                                     LR[th].ap),
                 reads=[LR[th]], writes=[rows])
            p.op("V", lambda e, th=th: e.tensor_copy(LQ[:, 512*th:512*(th+1)],
                                                     LR[2+th].ap),
                 reads=[LR[2+th]], writes=[rows])
        p.op("V", lambda e: e.tensor_scalar(MR, LS, 1.0 / D, None, ALU.mult),
             reads=[rows], writes=[rows])
        p.op("V", lambda e: e.tensor_scalar(LQ, LQ, 1.0 / D, None, ALU.mult),
             reads=[rows], writes=[rows])
        p.op("V", lambda e: e.tensor_tensor(R1, MR, MR, ALU.mult),
             reads=[rows], writes=[rows])
        p.op("V", lambda e: e.tensor_tensor(LQ, LQ, R1, ALU.subtract),
             reads=[rows], writes=[rows])
        p.op("S", lambda e: e.activation(R1, LQ, AF.Ln, bias=C["epsb"].t[0:1, 1:2]),
             reads=[rows, C["epsb"]], writes=[rows])
        p.op("S", lambda e: e.activation(R1, R1, AF.Exp, scale=-0.5),
             reads=[rows], writes=[rows])
        bcast(MR, mb)
        bcast(R1, bco)
        for m in range(NDC):
            rb = xc[m % 2]
            p.dma("Yd", rb.ap, SC["r"].t[m, :, :], reads=[SC["r"]], writes=[rb])
            p.op("G", lambda e, rb=rb: e.tensor_tensor(scr.ap, rb.ap, mb.ap,
                                                       ALU.subtract),
                 reads=[rb, mb], writes=[scr])
            p.op("V", lambda e: e.tensor_tensor(x1f.ap, scr.ap, bco.ap, ALU.mult),
                 reads=[scr, bco], writes=[x1f])
            p.op("S", lambda e: e.copy(x1b.ap, x1f.ap), reads=[x1f], writes=[x1b])
            p.dma("Yd", SC["x1f"].t[m, :, :], x1f.ap, reads=[x1f], writes=[SC["x1f"]])
            p.dma("Yd", SC["x1b"].t[m, :, :], x1b.ap, reads=[x1b], writes=[SC["x1b"]])
            if tap_x1 is not None:
                p.dma("Yd", tap_x1.t[m, :, :], x1f.ap, reads=[x1f], writes=[tap_x1])
        p.barrier()



def _p4(p, nc, I, C, SC, ln2s, tap):
    tap_w = tap("gatew", [2, 128, 32], F32)
    tap_r2 = tap("r2", [NDC, 128, T], F32)
    NF4 = DFF // 128
    with ExitStack() as e4:
        x1h = [p.sb(f"p4x{k}", [128, 512], BF16, es=e4) for k in range(NDC)]
        wrb = p.sb("p4wr", [128, NDC * NEXP], BF16, es=e4)
        bdb = p.sb("p4bd", [128, D], BF16, es=e4)
        atile = [p.sb(f"p4at{i}", [128, NDC * 128], BF16, es=e4) for i in range(1)]
        t_sb = p.sb("p4t", [16, 16 * 512], BF16, es=e4)
        gbp = [p.sb(f"p4gb{f}", [128, 1024], BF16, es=e4) for f in range(NF4 // 2)]
        ubp = [p.sb(f"p4ub{f}", [128, 1024], BF16, es=e4) for f in range(NF4 // 2)]
        def gb_ap(fs):
            return gbp[fs // 2].t[:, 512 * (fs % 2):512 * (fs % 2) + 512]
        def ub_ap(fs):
            return ubp[fs // 2].t[:, 512 * (fs % 2):512 * (fs % 2) + 512]
        bgc = p.sb("p4bgc", [16, DFF], BF16, es=e4)
        buc = p.sb("p4buc", [16, DFF], BF16, es=e4)
        adb = [p.sb(f"p4ad{i}", [128, NF4 * R], BF16, es=e4) for i in range(1)]
        wdc = [p.sb(f"p4wd{i}", [128, 22 * 128], BF16, es=e4) for i in range(2)]
        lgs = p.sb("p4lg", [128, 32], F32, es=e4)
        wgt = p.sb("p4wt", [128, 32], F32, es=e4)
        wtr = p.sb("p4wtr", [8, 512], F32, es=e4)
        wrows = p.sb("p4wrow", [1, 512], F32, es=e4)
        wsb = p.sb("p4wsb", [16, 512], F32, es=e4)
        dtt = p.sb("p4dtt", [16, 512], BF16, es=e4)
        dT_all = p.sb("p4dta", [128, 512], BF16, es=e4)
        scr1 = p.sb("p4s1", [128, 512], F32, es=e4)
        scr2 = p.sb("p4s2", [128, 512], F32, es=e4)
        scr3 = None
        heb = p.sb("p4he", [128, 512], BF16, es=e4)
        scr3 = scr1
        xfc = [p.sb("p4xf0", [128, 512], F32, es=e4), scr2]
        r2c = p.sb("p4r2", [128, 512], F32, es=e4)
        col = p.sb("p4col", [128, 4], F32, es=e4)

        p.dma("Gd", wrb.t[:].rearrange("p (k c) -> p k c", c=NEXP),
              I["wr"].ap.rearrange("(k p) c -> p k c", p=128), writes=[wrb])
        p.dma("Gd", bdb.ap, I["bd"].ap, writes=[bdb])

        for hi, t0 in enumerate((0, 512)):
            for k in range(NDC):
                p.dma("Yd", x1h[k].ap, SC["x1b"].t[k, :, t0:t0 + 512],
                      reads=[SC["x1b"]], writes=[x1h[k]])

            with ExitStack() as eR:
                PT4 = p.ps(f"p4pt{hi}", [128, 32], F32, es=eR)
                TRP = p.ps(f"p4trp{hi}", [8, 512], F32, es=eR)
                wrv = wrb.t[:].rearrange("p (k c) -> p k c", c=NEXP)
                for tt in range(4):
                    p.mm_group(PT4, [(PT4.t[:, 8 * tt:8 * (tt + 1)],
                                      x1h[k].t[:, 128 * tt:128 * (tt + 1)],
                                      wrv[:, k, :]) for k in range(NDC)],
                               reads=[wrb] + x1h)
                p.op("V", lambda e: e.tensor_copy(lgs.ap, PT4.ap),
                     reads=[PT4], writes=[lgs])
                for tt in range(4):
                    sl = lgs.t[:, 8 * tt:8 * (tt + 1)]
                    wl = wgt.t[:, 8 * tt:8 * (tt + 1)]
                    p.op("V", lambda e, sl=sl: e.reduce_max(col.t[:, 0:1], sl, axis=AX.X),
                         reads=[lgs], writes=[col])
                    p.op("V", lambda e: e.tensor_scalar_mul(col.t[:, 3:4],
                                                            col.t[:, 0:1], -1.0),
                         reads=[col], writes=[col])
                    p.op("S", lambda e, sl=sl, wl=wl: e.activation(
                            wl, sl, AF.Exp, bias=col.t[:, 3:4],
                            accum_out=col.t[:, 2:3]),
                         reads=[lgs, col], writes=[wgt, col])
                    p.op("V", lambda e: e.reciprocal(col.t[:, 2:3], col.t[:, 2:3]),
                         reads=[col], writes=[col])
                    p.op("V", lambda e, sl=sl: e.tensor_scalar(
                            scr1.t[:, 0:8], sl, col.t[:, 0:1], None, ALU.is_equal),
                         reads=[lgs, col], writes=[scr1])
                    p.op("V", lambda e, sl=sl: e.scalar_tensor_tensor(
                            scr1.t[:, 0:8], scr1.t[:, 0:8], NEG, sl,
                            ALU.mult, ALU.add),
                         reads=[scr1, lgs], writes=[scr1])
                    p.op("V", lambda e: e.reduce_max(col.t[:, 1:2], scr1.t[:, 0:8],
                                                     axis=AX.X),
                         reads=[scr1], writes=[col])
                    p.op("V", lambda e, sl=sl: e.tensor_scalar(
                            scr1.t[:, 0:8], sl, col.t[:, 1:2], None, ALU.is_ge),
                         reads=[lgs, col], writes=[scr1])
                    p.op("V", lambda e, wl=wl: e.scalar_tensor_tensor(
                            wl, wl, col.t[:, 2:3], scr1.t[:, 0:8],
                            ALU.mult, ALU.mult),
                         reads=[wgt, col, scr1], writes=[wgt])
                if tap_w is not None:
                    p.dma("Yd", tap_w.t[hi, :, :], wgt.ap, reads=[wgt], writes=[tap_w])
                for tt in range(4):
                    p.op("T", lambda e, tt=tt: e.transpose(
                            TRP.t[:, 128 * tt:128 * (tt + 1)],
                            wgt.t[:, 8 * tt:8 * (tt + 1)], C["identf"].ap),
                         reads=[wgt, C["identf"]], writes=[TRP])
                p.op("V", lambda e: e.tensor_scalar_mul(wtr.ap, TRP.t[0:8, :],
                                                        SCALING),
                     reads=[TRP], writes=[wtr])
                p.barrier(("T", "V", "S"))

            with ExitStack() as eE:
                PAW = [p.ps(f"p4pa{hi}_{i}", [128, 1024], F32, es=eE)
                       for i in range(2)]
                PUW = p.ps(f"p4pu{hi}", [128, 1024], F32, es=eE)
                PD = p.ps(f"p4pd{hi}", [16, 512], F32, es=eE)
                for ti in range(8):
                    p.dma("Gd", atile[0].t[:].rearrange("p (k c) -> p k c", c=128),
                          I["a_all"].t[:, 128 * ti:128 * (ti + 1)]
                          .rearrange("(k p) c -> p k c", p=128),
                          writes=[atile[0]])
                    av = atile[0].t[:].rearrange("p (k c) -> p k c", c=128)
                    PAb = PAW[ti % 2]
                    p.mm_group(PAb, [(PAb.t[:, 0:512], av[:, k, :], x1h[k].ap)
                                     for k in range(NDC)], reads=[atile[0]] + x1h)
                    base_slot = 0 if ti < 4 else 8
                    uu = ti % 4
                    for half_e, pbase in ((0, 0), (1, 64)):
                        slot = base_slot + 2 * uu + half_e
                        p.op("S", lambda e, slot=slot, pbase=pbase, PAb=PAb:
                             e.activation(t_sb.t[:, 512 * slot:512 * (slot + 1)],
                                          PAb.t[pbase:pbase + 16, 0:512], AF.Copy,
                                          scale=SCALING),
                             reads=[PAb], writes=[t_sb])

                with ExitStack() as eB:
                    wgc = [p.sb(f"p4wg{hi}_{i}", [128, NDC * 128], BF16, es=eB)
                           for i in range(2)]
                    wuc = [p.sb(f"p4wu{hi}_{i}", [128, NDC * 128], BF16, es=eB)
                           for i in range(2)]
                    def ldg(fs, b):
                        for nm, dst in (("wg", wgc[b]), ("wu", wuc[b])):
                            p.dma("Gd", dst.t[:].rearrange("p (k c) -> p k c", c=128),
                                  I[nm].t[:, 128 * fs:128 * (fs + 1)]
                                  .rearrange("(k p) c -> p k c", p=128), writes=[dst])
                    ldg(0, 0)
                    for fs in range(NF4):
                        if fs + 1 < NF4:
                            ldg(fs + 1, (fs + 1) % 2)
                        gv = wgc[fs % 2].t[:].rearrange("p (k c) -> p k c", c=128)
                        uv = wuc[fs % 2].t[:].rearrange("p (k c) -> p k c", c=128)
                        PAb = PAW[fs % 2]
                        p.mm_group(PAb, [(PAb.t[:, 0:512], gv[:, k, :], x1h[k].ap)
                                         for k in range(NDC)], reads=[wgc[fs % 2]] + x1h)
                        p.op("S", lambda e, fs=fs, PAb=PAb: e.copy(
                                gb_ap(fs), PAb.t[:, 0:512]),
                             reads=[PAb], writes=[gbp[fs // 2]])
                        p.mm_group(PUW, [(PUW.t[:, 0:512], uv[:, k, :], x1h[k].ap)
                                         for k in range(NDC)], reads=[wuc[fs % 2]] + x1h)
                        p.op("S", lambda e, fs=fs: e.copy(ub_ap(fs), PUW.t[:, 0:512]),
                             reads=[PUW], writes=[ubp[fs // 2]])
                    p.barrier(("T", "V", "S", "Gd"))

                def ldcombo(ex, b):
                    p.dma("Gd", bgc.ap, I["bg"].t[ex, :, :], writes=[bgc])
                    p.dma("Gd", buc.ap, I["bu"].t[ex, :, :], writes=[buc])
                    p.dma("Gd", adb[b].t[:].rearrange("p (f r) -> p f r", r=R),
                          I["ad"].t[ex, :, :].rearrange("(f p) r -> p f r", p=128),
                          writes=[adb[b]])
                NP4 = NF4 // 2
                with ExitStack() as eX:
                    s1 = [p.sb(f"p4s1{hi}_{i}", [128, 1024], BF16, es=eX)
                          for i in range(2)]
                    s2 = [p.sb(f"p4s2{hi}_{i}", [128, 1024], BF16, es=eX)
                          for i in range(2)]
                    s3 = [p.sb(f"p4s3{hi}_{i}", [128, 1024], BF16, es=eX)
                          for i in range(2)]
                    hebs = [p.sb(f"p4he{hi}_{i}", [128, 1024], BF16, es=eX)
                            for i in range(2)]
                    bbscr = p.sb(f"p4bbs{hi}", [128, 1024], BF16, es=eX)
                    for ex in range(NEXP):
                        ldcombo(ex, 0)
                        ab = adb[0].t[:].rearrange("p (f r) -> p f r", r=R)
                        tg = t_sb.t[:, 512 * ex:512 * (ex + 1)]
                        tu = t_sb.t[:, 512 * (8 + ex):512 * (9 + ex)]

                        def mms_g(j):
                            par = j % 2
                            for h2 in range(2):
                                fs = 2 * j + h2
                                p.mm(PAW[par].t[:, 512 * h2:512 * (h2 + 1)],
                                     bgc.t[:, 128 * fs:128 * (fs + 1)],
                                     tg, PAW[par], reads=[bgc, t_sb])

                        def mms_u(j):
                            for h2 in range(2):
                                fs = 2 * j + h2
                                p.mm(PUW.t[:, 512 * h2:512 * (h2 + 1)],
                                     buc.t[:, 128 * fs:128 * (fs + 1)],
                                     tu, PUW, reads=[buc, t_sb])

                        def elem(j):
                            par = j % 2
                            p.op("V", lambda e, j=j, par=par: e.tensor_tensor(
                                    s1[par].ap, gbp[j].ap, PAW[par].ap, ALU.add),
                                 reads=[gbp[j], PAW[par]], writes=[s1[par]])
                            p.op("S", lambda e, par=par: e.activation(
                                    s2[par].ap, s1[par].ap, AF.Silu),
                                 reads=[s1[par]], writes=[s2[par]])
                            p.op("V", lambda e, j=j, par=par: e.tensor_tensor(
                                    s3[par].ap, ubp[j].ap, PUW.ap, ALU.add),
                                 reads=[ubp[j], PUW], writes=[s3[par]])
                            p.op("V", lambda e, par=par: e.tensor_tensor(
                                    hebs[par].ap, s2[par].ap, s3[par].ap, ALU.mult),
                                 reads=[s2[par], s3[par]], writes=[hebs[par]])
                            for h2 in range(2):
                                fs = 2 * j + h2
                                p.mm(PD.ap, ab[:, fs, :],
                                     hebs[par].t[:, 512 * h2:512 * (h2 + 1)], PD,
                                     reads=[adb[0], hebs[par]],
                                     start=(fs == 0), stop=(fs == NF4 - 1))

                        def bb_make(j):
                            p.op("S", lambda e, j=j: e.activation(
                                    bbscr.ap, gbp[j].ap, AF.Silu),
                                 reads=[gbp[j]], writes=[bbscr])
                            p.op("V", lambda e, j=j: e.tensor_tensor(
                                    gbp[j].ap, bbscr.ap, ubp[j].ap, ALU.mult),
                                 reads=[bbscr, ubp[j]], writes=[gbp[j]])
                        mms_g(0)
                        mms_u(0)
                        for j in range(1, NP4):
                            mms_g(j)
                            elem(j - 1)
                            mms_u(j)
                            if ex == NEXP - 1 and j >= 2:
                                bb_make(j - 2)
                        elem(NP4 - 1)
                        if ex == NEXP - 1:
                            bb_make(NP4 - 2)
                            bb_make(NP4 - 1)
                        for tt in range(4):
                            p.dma("Yd", wrows.t[:, 128 * tt:128 * (tt + 1)],
                                  wtr.t[ex:ex + 1, 128 * tt:128 * (tt + 1)],
                                  reads=[wtr], writes=[wrows])
                        p.mm(PUW.t[0:16, 0:512], C["ones"].t[0:1, 0:16],
                             wrows.ap, PUW, reads=[C["ones"], wrows])
                        p.op("V", lambda e: e.tensor_copy(wsb.ap, PUW.t[0:16, 0:512]),
                             reads=[PUW], writes=[wsb])
                        p.op("V", lambda e: e.tensor_tensor(dtt.ap, PD.ap, wsb.ap,
                                                            ALU.mult),
                             reads=[PD, wsb], writes=[dtt])
                        p.dma("Yd", dT_all.t[16 * ex:16 * (ex + 1), :], dtt.ap,
                              reads=[dtt], writes=[dT_all])

                p.barrier(("T", "V", "S", "G"))

            with ExitStack() as eF:
                PF = [p.ps(f"p4pf{hi}_{i}", [128, 512], F32, es=eF) for i in range(2)]
                LR2 = [p.ps(f"p4lr{hi}_{i}", [1, 512], F32, es=eF) for i in range(2)]
                def ldwd(m, b):
                    p.dma("Gd", wdc[b].t[:].rearrange("p (k c) -> p k c", c=128),
                          I["wd"].t[2816 * b:2816 * (b + 1), 128 * m:128 * (m + 1)]
                          .rearrange("(k p) c -> p k c", p=128), writes=[wdc[b]])
                ldwd(0, 0)
                ldwd(0, 1)
                p.dma("Yd", xfc[0].ap, SC["x1f"].t[0, :, t0:t0 + 512],
                      reads=[SC["x1f"]], writes=[xfc[0]])
                for m in range(NDC):
                    if m + 1 < NDC:
                        p.dma("Yd", xfc[(m + 1) % 2].ap,
                              SC["x1f"].t[m + 1, :, t0:t0 + 512],
                              reads=[SC["x1f"]], writes=[xfc[(m + 1) % 2]])
                    dvA = wdc[0].t[:].rearrange("p (k c) -> p k c", c=128)
                    dvB = wdc[1].t[:].rearrange("p (k c) -> p k c", c=128)
                    PFb = PF[m % 2]
                    p.mm_group(PFb, [(PFb.ap, dvA[:, k, :], gb_ap(k))
                                     for k in range(22)],
                               reads=[wdc[0]] + gbp[0:11], start=True, stop=False)
                    if m + 1 < NDC:
                        ldwd(m + 1, 0)
                    mmsB = [(PFb.ap, dvB[:, k, :], gb_ap(22 + k))
                            for k in range(22)]
                    mmsB.append((PFb.ap, bdb.t[:, 128 * m:128 * (m + 1)],
                                 dT_all.ap))
                    p.mm_group(PFb, mmsB, reads=[wdc[1], bdb, dT_all] + gbp[11:],
                               start=False, stop=True)
                    if m + 1 < NDC:
                        ldwd(m + 1, 1)
                    xb = xfc[m % 2]
                    p.op("V", lambda e, xb=xb, PFb=PFb: e.tensor_tensor(
                            r2c.ap, xb.ap, PFb.ap, ALU.add),
                         reads=[xb, PFb], writes=[r2c])
                    p.mm(LR2[0].ap, C["ones"].t[:, 0:1], r2c.ap, LR2[0],
                         reads=[C["ones"], r2c], start=(m == 0), stop=(m == NDC - 1))
                    p.op("S", lambda e: e.activation(scr1.ap, r2c.ap, AF.Square),
                         reads=[r2c], writes=[scr1])
                    p.mm(LR2[1].ap, C["ones"].t[:, 0:1], scr1.ap, LR2[1],
                         reads=[C["ones"], scr1], start=(m == 0), stop=(m == NDC - 1))
                    p.dma("Yd", SC["r2"].t[m, :, t0:t0 + 512], r2c.ap,
                          reads=[r2c], writes=[SC["r2"]])
                    if tap_r2 is not None:
                        p.dma("Yd", tap_r2.t[m, :, t0:t0 + 512], r2c.ap,
                              reads=[r2c], writes=[tap_r2])
                p.op("V", lambda e, t0=t0: e.tensor_copy(
                        ln2s.t[:, t0:t0 + 512], LR2[0].ap),
                     reads=[LR2[0]], writes=[ln2s])
                p.op("V", lambda e, t0=t0: e.tensor_copy(
                        ln2s.t[:, T + t0:T + t0 + 512], LR2[1].ap),
                     reads=[LR2[1]], writes=[ln2s])
                p.barrier(("T", "V", "S"))
        p.barrier()



def _p5(p, nc, I, C, SC, ln2s, out, tap):
    with ExitStack() as e5:
        rows = p.sb("p5rows", [1, 3 * T], F32, es=e5)
        MR = rows.t[:, 0:T]; VR = rows.t[:, T:2*T]; RS = rows.t[:, 2*T:3*T]
        mb = p.sb("p5mb", [128, T], F32, es=e5)
        bco = p.sb("p5bco", [128, T], F32, es=e5)
        rc = [p.sb(f"p5rc{i}", [128, T], F32, es=e5) for i in range(2)]
        oc = p.sb("p5oc", [128, T], F32, es=e5)
        scr = p.sb("p5scr", [128, T], F32, es=e5)
        BC = p.ps("p5bc", [128, 512], F32, es=e5)

        p.op("V", lambda e: e.tensor_scalar(MR, ln2s.t[:, 0:T], 1.0 / D, None,
                                            ALU.mult),
             reads=[ln2s], writes=[rows])
        p.op("V", lambda e: e.tensor_scalar(VR, ln2s.t[:, T:2*T], 1.0 / D, None,
                                            ALU.mult),
             reads=[ln2s], writes=[rows])
        p.op("V", lambda e: e.tensor_tensor(RS, MR, MR, ALU.mult),
             reads=[rows], writes=[rows])
        p.op("V", lambda e: e.tensor_tensor(VR, VR, RS, ALU.subtract),
             reads=[rows], writes=[rows])
        p.op("S", lambda e: e.activation(RS, VR, AF.Ln, bias=C["epsb"].t[0:1, 1:2]),
             reads=[rows, C["epsb"]], writes=[rows])
        p.op("S", lambda e: e.activation(RS, RS, AF.Exp, scale=-0.5),
             reads=[rows], writes=[rows])
        for src_sl, dst in ((MR, mb), (RS, bco)):
            for th in range(2):
                p.mm(BC.ap, C["ones"].t[0:1, :], src_sl[:, 512*th:512*(th+1)],
                     BC, reads=[C["ones"], rows])
                p.op("V", lambda e, th=th, dst=dst: e.tensor_copy(
                        dst.t[:, 512*th:512*(th+1)], BC.ap),
                     reads=[BC], writes=[dst])
        for m in range(NDC):
            rb = rc[m % 2]
            p.dma("Yd", rb.ap, SC["r2"].t[m, :, :], reads=[SC["r2"]], writes=[rb])
            p.op("V", lambda e, rb=rb: e.tensor_tensor(scr.ap, rb.ap, mb.ap,
                                                       ALU.subtract),
                 reads=[rb, mb], writes=[scr])
            p.op("V", lambda e: e.tensor_tensor(oc.ap, scr.ap, bco.ap, ALU.mult),
                 reads=[scr, bco], writes=[oc])
            p.dma("Yd", out.t[m, :, :], oc.ap, reads=[oc], writes=[out])
        p.barrier()



def kernel(**inputs):
    shared, per_core = prep_inputs(inputs)
    nc, _ = build(phases=("P0", "P1", "P2", "P3", "P4", "P5"))
    in_maps = [dict(shared, **pc) for pc in per_core]
    res = run_bass_kernel_spmd(nc, in_maps, core_ids=list(range(NCORES)))
    outs = [np.asarray(res.results[c]["out"]).transpose(2, 0, 1).reshape(T, D)
            for c in range(NCORES)]
    return np.concatenate(outs, axis=0).reshape(1, S, D).astype(np.float32)


# version 16
# speedup vs baseline: 1.0398x; 1.0181x over previous
import numpy as np
import ml_dtypes
from contextlib import ExitStack

import concourse.bass as bass
import concourse.mybir as mybir
from concourse.bass_utils import run_bass_kernel_spmd

F32 = mybir.dt.float32
BF16 = mybir.dt.bfloat16
AF = mybir.ActivationFunctionType
ALU = mybir.AluOpType
AX = mybir.AxisListType
BFD = ml_dtypes.bfloat16

NCORES = 8
S = 8192
D = 2048
T = 1024
HALO = 512
TH = T + HALO
NH = 16
DH = 128
NDC = D // 128
W_SWA = 512
W_OM = 64
DECAY = 0.9
GGH = 64
MEMF = 8192
FFH = 5461
FFHP = 5504
DFF = 5632
NEXP = 8
R = 16
SCALING = 2.0
EPS_RMS = 1e-6
EPS_LN = 1e-5
SCALE = DH ** -0.5
NEG = -1e30


ENGINE_OF = {"T": "tensor", "V": "vector", "S": "scalar", "Sd": "scalar",
             "G": "gpsimd", "Gd": "gpsimd", "Yd": "sync"}

NSLOT = 6

class Q:
    def __init__(self, name, step):
        self.name = name
        self.step = step
        self.is_dma = name.endswith("d")
        if self.is_dma:
            self.slot_cnt = [0] * NSLOT
            self.issue = 0
        else:
            self.cnt = 0

class Buf:
    __slots__ = ("t", "w", "readers", "name")
    def __init__(self, t, name=""):
        self.t = t
        self.name = name
        self.w = None
        self.readers = {}
    def __getitem__(self, idx):
        return self.t[idx]
    @property
    def ap(self):
        return self.t[:]

class Prog:
    def __init__(self, nc, es):
        self.nc = nc
        self.es = es
        self.queues = {n: Q(n, 16 if n.endswith("d") else 1)
                       for n in ("T", "V", "S", "G", "Yd", "Gd", "Sd")}
        self.sems = {}
        for n, q in self.queues.items():
            if q.is_dma:
                for s in range(NSLOT):
                    self.sems[f"{n}{s}"] = es.enter_context(nc.semaphore(f"sem_{n}{s}"))
            else:
                self.sems[n] = es.enter_context(nc.semaphore(f"sem_{n}"))
        self.streams = {e: [] for e in ("tensor", "vector", "scalar", "gpsimd", "sync")}
        self.hw = {e: {} for e in self.streams}

    def sb(self, name, shape, dt, es=None):
        return Buf((es or self.es).enter_context(
            self.nc.sbuf_tensor(name, list(shape), dt)), name)

    def ps(self, name, shape, dt=F32, es=None):
        return Buf((es or self.es).enter_context(
            self.nc.psum_tensor(name, list(shape), dt)), name)

    def dram(self, name, shape, dt):
        return Buf(self.nc.dram_tensor(name, list(shape), dt), name)

    def _edges(self, qname, reads, writes, skip_self_T=True):
        eng = ENGINE_OF[qname]
        need = {}
        def upd(e):
            if e is None:
                return
            sk, c = e
            if sk == "T" and qname == "T" and skip_self_T:
                return
            if need.get(sk, -1) < c:
                need[sk] = c
        for b in reads:
            upd(b.w)
        for b in writes:
            upd(b.w)
            for sk, c in b.readers.items():
                upd((sk, c))
        hw = self.hw[eng]
        out = []
        for sk in sorted(need):
            c = need[sk]
            if c <= 0 or hw.get(sk, 0) >= c:
                continue
            out.append((sk, c))
            hw[sk] = c
        return out

    def _push(self, qname, waits, emit, incsem):
        self.streams[ENGINE_OF[qname]].append((waits, emit, incsem))

    def _record(self, semkey, val, reads, writes):
        for b in writes:
            b.w = (semkey, val)
            b.readers = {}
        for b in reads:
            if b.readers.get(semkey, -1) < val:
                b.readers[semkey] = val

    def op(self, qname, emit, reads=(), writes=()):
        q = self.queues[qname]
        assert not q.is_dma
        waits = self._edges(qname, reads, writes)
        q.cnt += 1
        self._push(qname, waits, emit, (qname, 1))
        self._record(qname, q.cnt, reads, writes)

    def dma(self, qname, dst_ap, src_ap, reads=(), writes=()):
        q = self.queues[qname]
        assert q.is_dma
        slot = q.issue % NSLOT
        q.issue += 1
        semkey = f"{qname}{slot}"
        waits = []
        prior = q.slot_cnt[slot] * 16
        eng = ENGINE_OF[qname]
        if prior > 0 and self.hw[eng].get(semkey, 0) < prior:
            waits.append((semkey, prior))
            self.hw[eng][semkey] = prior
        waits += self._edges(qname, reads, writes)
        q.slot_cnt[slot] += 1
        self._push(qname, waits,
                   lambda e: e.dma_start(dst_ap, src_ap), (semkey, 16))
        self._record(semkey, q.slot_cnt[slot] * 16, reads, writes)

    def mm(self, out_ap, lhsT_ap, rhs_ap, psum_buf, reads=(), start=True, stop=True):
        self.op("T", lambda e: e.matmul(out_ap, lhsT_ap, rhs_ap, start=start, stop=stop),
                reads=list(reads), writes=[psum_buf])

    def mm_group(self, psum_buf, mms, reads=(), start=True, stop=True):
        waits = self._edges("T", list(reads), [psum_buf])
        q = self.queues["T"]
        q.cnt += 1
        n = len(mms)
        def emit(e, mms=tuple(mms), n=n, start=start, stop=stop):
            ins = None
            for i, (o, l, r) in enumerate(mms):
                ins = e.matmul(o, l, r, start=(start and i == 0),
                               stop=(stop and i == n - 1))
            return ins
        self._push("T", waits, emit, ("T", 1))
        self._record("T", q.cnt, list(reads), [psum_buf])

    def _cur_semvals(self):
        vals = {}
        for n, q in self.queues.items():
            if q.is_dma:
                for s in range(NSLOT):
                    if q.slot_cnt[s] > 0:
                        vals[f"{n}{s}"] = q.slot_cnt[s] * 16
            else:
                if q.cnt > 0:
                    vals[n] = q.cnt
        return vals

    def wait_all(self, qname):
        eng = ENGINE_OF[qname]
        hw = self.hw[eng]
        waits = []
        own = {qname} if not self.queues[qname].is_dma else \
            {f"{qname}{s}" for s in range(NSLOT)}
        for sk, c in sorted(self._cur_semvals().items()):
            if sk in own:
                continue
            if hw.get(sk, 0) < c:
                waits.append((sk, c))
                hw[sk] = c
        if waits:
            self._push(qname, waits, None, None)

    def barrier(self, qnames=("T", "V", "S", "Yd", "Gd", "Sd")):
        for qa in qnames:
            self.wait_all(qa)

    def finalize(self, block):
        def mk(engname):
            items = self.streams[engname]
            def body(eng):
                for waits, emit, incsem in items:
                    for sk, c in waits:
                        eng.wait_ge(self.sems[sk], c)
                    if emit is not None:
                        ins = emit(eng)
                        if incsem is not None:
                            ins.then_inc(self.sems[incsem[0]], incsem[1])
            return body
        block.tensor(mk("tensor"))
        block.vector(mk("vector"))
        block.scalar(mk("scalar"))
        block.gpsimd(mk("gpsimd"))
        block.sync(mk("sync"))



def _rope_tables():
    inv_freq = 1.0 / (10000.0 ** (np.arange(0, DH, 2, dtype=np.float64) / DH))
    t = np.arange(-HALO, S, dtype=np.float64)
    freqs = t[:, None] * inv_freq[None, :]
    cos = np.cos(freqs)
    sin = np.sin(freqs)
    cosT = np.concatenate([cos.T, cos.T], axis=0)
    sinT = np.concatenate([-sin.T, sin.T], axis=0)
    return cosT, sinT

def _masks(core):
    i = np.arange(128)[:, None]
    j = np.arange(128)[None, :]
    m_far = np.where(j > i, 0.0, NEG)
    m_causal = np.where(j <= i, 0.0, NEG)
    rel4 = i - j
    rel3 = i + 128 - j
    m_om4 = np.where((rel4 >= 0) & (rel4 < W_OM), 0.0, NEG)
    m_om3 = np.where((rel3 >= 0) & (rel3 < W_OM), 0.0, NEG)
    d_om4 = np.where((rel4 >= 0) & (rel4 < W_OM),
                     DECAY ** np.clip(rel4, 0, 255), 0.0)
    d_om3 = np.where((rel3 >= 0) & (rel3 < W_OM),
                     DECAY ** np.clip(rel3, 0, 255), 0.0)
    swa_cat = np.zeros((8, 128, 640))
    om_cat = np.zeros((8, 128, 256))
    for qi in range(8):
        for b in range(5):
            blk = np.zeros((128, 128))
            if b == 0:
                blk = blk + m_far
            elif b == 4:
                blk = blk + m_causal
            if core == 0 and (qi + b) < 4:
                blk = blk + NEG
            swa_cat[qi, :, 128*b:128*(b+1)] = blk
        om_cat[qi, :, 0:128] = m_om3 + (NEG if (core == 0 and qi == 0) else 0.0)
        om_cat[qi, :, 128:256] = m_om4
    np.clip(swa_cat, -3e38, 0, out=swa_cat)
    np.clip(om_cat, -3e38, 0, out=om_cat)
    d_cat = np.concatenate([d_om3, d_om4], axis=1)
    return dict(
        swa_cat=np.ascontiguousarray(
            swa_cat.transpose(1, 0, 2).reshape(128, 8 * 640)).astype(BFD),
        om_cat=np.ascontiguousarray(
            om_cat.transpose(1, 0, 2).reshape(128, 8 * 256)).astype(BFD),
        d_cat=d_cat.astype(np.float32))

def prep_inputs(inp):
    x = np.asarray(inp["L0_attn_x"])[0]
    shared = {}
    wqkv = np.asarray(inp["L0_attn_w_qkv"])
    cols = []
    for h in range(NH):
        cols.append(wqkv[:, h*DH:(h+1)*DH])
        cols.append(wqkv[:, D + h*DH:D + (h+1)*DH])
        cols.append(wqkv[:, 2*D + h*DH:2*D + (h+1)*DH])
    shared["wqkv_p"] = np.ascontiguousarray(np.concatenate(cols, axis=1))
    shared["qscale"] = np.ascontiguousarray(np.asarray(inp["L0_attn_q_scale"]).reshape(DH, 1))
    shared["kscale"] = np.ascontiguousarray(np.asarray(inp["L0_attn_k_scale"]).reshape(DH, 1))
    shared["n1w"] = np.ascontiguousarray(np.asarray(inp["L0_attn_norm1_w"]).reshape(NDC, 128).T)
    shared["n2w"] = np.ascontiguousarray(np.asarray(inp["L0_attn_norm2_w"]).reshape(NDC, 128).T)
    shared["gg1"] = np.asarray(inp["L0_attn_gg_w1"])
    shared["gg2"] = np.asarray(inp["L0_attn_gg_w2"])
    shared["wo"] = np.asarray(inp["L0_attn_w_o"])
    shared["mw1"] = np.asarray(inp["L0_attn_mem_w1"])
    shared["mw2"] = np.asarray(inp["L0_attn_mem_w2"])
    z = np.zeros((D, FFHP - FFH), np.float32)
    shared["fw1"] = np.ascontiguousarray(np.concatenate([np.asarray(inp["L0_attn_ffn_w1"]), z], axis=1))
    shared["fw3"] = np.ascontiguousarray(np.concatenate([np.asarray(inp["L0_attn_ffn_w3"]), z], axis=1))
    shared["fw2"] = np.ascontiguousarray(np.concatenate(
        [np.asarray(inp["L0_attn_ffn_w2"]), np.zeros((FFHP - FFH, D), np.float32)], axis=0))
    shared["wg"] = np.ascontiguousarray(np.asarray(inp["L0_ffn_Wg"]).T)
    shared["wu"] = np.ascontiguousarray(np.asarray(inp["L0_ffn_Wu"]).T)
    shared["wd"] = np.ascontiguousarray(np.asarray(inp["L0_ffn_Wd"]).T)
    shared["wr"] = np.ascontiguousarray(np.asarray(inp["L0_ffn_Wr"]).T)
    Ag = np.asarray(inp["L0_ffn_Ag"])
    Au = np.asarray(inp["L0_ffn_Au"])
    a_all = np.zeros((D, 8 * 128), np.float32)
    for t in range(4):
        a_all[:, 128*t + 0:128*t + 16] = Ag[2*t].T
        a_all[:, 128*t + 64:128*t + 80] = Ag[2*t + 1].T
        base = 512 + 128*t
        a_all[:, base + 0:base + 16] = Au[2*t].T
        a_all[:, base + 64:base + 80] = Au[2*t + 1].T
    shared["a_all"] = a_all
    shared["bg"] = np.ascontiguousarray(np.asarray(inp["L0_ffn_Bg"]).transpose(0, 2, 1))
    shared["bu"] = np.ascontiguousarray(np.asarray(inp["L0_ffn_Bu"]).transpose(0, 2, 1))
    shared["ad"] = np.ascontiguousarray(np.asarray(inp["L0_ffn_Ad"]).transpose(0, 2, 1))
    shared["bd"] = np.ascontiguousarray(
        np.asarray(inp["L0_ffn_Bd"]).transpose(0, 2, 1).reshape(NEXP * R, D))
    shared["ident"] = np.eye(128, dtype=BFD)
    shared["identf"] = np.eye(128, dtype=np.float32)
    shared["ones"] = np.ones((128, 128), np.float32)
    shared["onesb"] = np.ones((128, 128), dtype=BFD)
    epsb = np.zeros((128, 2), np.float32); epsb[:, 0] = EPS_RMS; epsb[:, 1] = EPS_LN
    shared['epsb'] = epsb

    cosT, sinT = _rope_tables()
    per_core = []
    for c in range(NCORES):
        gs = c * T
        m = {}
        lo = gs - HALO
        if lo < 0:
            xs = np.concatenate([np.zeros((HALO, D), np.float32),
                                 x[0:gs + T]], axis=0)
        else:
            xs = x[lo:gs + T]
        m["x_f"] = np.ascontiguousarray(xs.T)
        m["cosq"] = np.ascontiguousarray(cosT[:, HALO + gs: HALO + gs + T] * SCALE).astype(BFD)
        m["sinq"] = np.ascontiguousarray(sinT[:, HALO + gs: HALO + gs + T] * SCALE).astype(BFD)
        m["cosk"] = np.ascontiguousarray(cosT[:, gs: gs + TH]).astype(BFD)
        m["sink"] = np.ascontiguousarray(sinT[:, gs: gs + TH]).astype(BFD)
        m.update(_masks(c))
        per_core.append(m)
    return shared, per_core



IN_SPECS = [
    ("x_f", [D, TH], F32), ("wqkv_p", [D, 3 * D], F32),
    ("qscale", [DH, 1], F32), ("kscale", [DH, 1], F32),
    ("n1w", [128, NDC], F32), ("n2w", [128, NDC], F32),
    ("gg1", [D, GGH], F32), ("gg2", [GGH, 1], F32),
    ("wo", [D, D], F32), ("mw1", [D, MEMF], F32), ("mw2", [MEMF, D], F32),
    ("fw1", [D, FFHP], F32), ("fw3", [D, FFHP], F32), ("fw2", [FFHP, D], F32),
    ("wg", [D, DFF], F32), ("wu", [D, DFF], F32), ("wd", [DFF, D], F32),
    ("wr", [D, NEXP], F32), ("a_all", [D, 1024], F32),
    ("bg", [NEXP, R, DFF], F32), ("bu", [NEXP, R, DFF], F32),
    ("ad", [NEXP, DFF, R], F32), ("bd", [NEXP * R, D], F32),
    ("ident", [128, 128], BF16), ("identf", [128, 128], F32),
    ("onesb", [128, 128], BF16),
    ("ones", [128, 128], F32),
    ("swa_cat", [128, 8 * 640], BF16), ("om_cat", [128, 8 * 256], BF16),
    ("d_cat", [128, 256], F32),
    ("cosq", [128, T], BF16), ("sinq", [128, T], BF16),
    ("cosk", [128, TH], BF16), ("sink", [128, TH], BF16),
    ("epsb", [128, 2], F32),
]

CONST_NAMES = ["ident", "identf", "ones", "onesb", "n1w", "n2w", "qscale", "kscale", "epsb"]
P1_CONST_NAMES = ["swa_cat", "om_cat", "d_cat",
                  "cosq", "sinq", "cosk", "sink"]

def build(phases=("P0",), debug_taps=()):
    nc = bass.Bass("TRN2", target_bir_lowering=False)
    es = ExitStack()
    p = Prog(nc, es)
    dbg = {}

    I = {nm: Buf(nc.dram_tensor(nm, list(shape), dt, kind="ExternalInput"), nm)
         for nm, shape, dt in IN_SPECS}
    out = Buf(nc.dram_tensor("out", [NDC, 128, T], F32, kind="ExternalOutput"), "out")

    SC = {
        "attn": p.dram("sc_attn", [NH, 128, T], BF16),
        "xa": p.dram("sc_xa", [NDC, 128, T], F32),
        "r": p.dram("sc_r", [NDC, 128, T], F32),
        "x1b": p.dram("sc_x1b", [NDC, 128, T], BF16),
        "x1f": p.dram("sc_x1f", [NDC, 128, T], F32),
        "r2": p.dram("sc_r2", [NDC, 128, T], F32),
    }

    def tap(name, shape, dt=F32):
        if name in debug_taps:
            b = Buf(nc.dram_tensor("dbg_" + name, list(shape), dt,
                                   kind="ExternalOutput"), "dbg_" + name)
            dbg[name] = b
            return b
        return None

    with es:
        C = {}
        for nm in CONST_NAMES:
            shape = dict((n, s) for n, s, _ in IN_SPECS)[nm]
            dt = dict((n, d) for n, s, d in IN_SPECS)[nm]
            C[nm] = p.sb("c_" + nm, shape, dt)
            p.dma("Yd", C[nm].ap, I[nm].ap, writes=[C[nm]])

        es_p03 = es.enter_context(ExitStack())
        rms2ss = p.sb("rms2ss", [1, T], F32, es=es_p03)

        with ExitStack() as es_p02:
            qmem = [p.sb(f"qmem{h}", [128, T], BF16, es=es_p02) for h in range(NH)]
            hbuf = [p.sb(f"h{i}", [128, TH], BF16, es=es_p02) for i in range(NDC)]
            gammaT = p.sb("gammaT", [128, 12], F32, es=es_p02)

            if "P0" in phases:
                _p0(p, nc, I, C, hbuf, gammaT, tap)
            p1sel = [ph for ph in phases if ph.startswith("P1")]
            if p1sel:
                parts = p1sel[0].split(":")
                upto = parts[1] if len(parts) > 1 else "full"
                nh = int(parts[2]) if len(parts) > 2 else NH
                _p1(p, nc, I, C, hbuf, gammaT, qmem, SC, tap,
                    heads=range(nh), upto=upto)

            if "P2" in phases:
                _p2(p, nc, I, C, hbuf, qmem, SC, rms2ss, tap)

        if "P3" in phases:
            _p3(p, nc, I, C, SC, rms2ss, tap)
        es_p03.close()

        ln2s = p.sb("ln2s", [1, 2 * T], F32)
        if "P4" in phases:
            _p4(p, nc, I, C, SC, ln2s, tap)
        if "P5" in phases:
            _p5(p, nc, I, C, SC, ln2s, out, tap)

        if "OUT_STUB" in phases:
            with ExitStack() as e2:
                z = p.sb("zstub", [128, D], F32, es=e2)
                p.op("V", lambda e: e.memset(z.ap, 0.0), writes=[z])
                for i in range(T // 128):
                    p.dma("Yd", out.t[i * 128:(i + 1) * 128, :], z.ap,
                          reads=[z], writes=[out])

        p.barrier()

        with nc.Block() as block:
            p.finalize(block)
    return nc, dbg



def _p0(p, nc, I, C, hbuf, gammaT, tap):
    with ExitStack() as e0:
        xs = [p.sb(f"p0x{i}", [128, TH], F32, es=e0) for i in range(2)]
        sqs = [p.sb(f"p0sq{i}", [128, TH], BF16, es=e0) for i in range(2)]
        rstd_row = p.sb("p0rstd", [1, TH], F32, es=e0)
        rstd_sb = p.sb("p0rbs", [128, TH], F32, es=e0)
        gg1b = p.sb("p0gg1", [128, NDC * GGH], BF16, es=e0)
        gg2b = p.sb("p0gg2", [GGH, 1], BF16, es=e0)
        gsil = p.sb("p0gs", [GGH, TH], BF16, es=e0)

        p.dma("Gd", gg1b.t[:].rearrange("p (k c) -> p k c", c=GGH),
              I["gg1"].ap.rearrange("(k p) c -> p k c", p=128), writes=[gg1b])
        p.dma("Gd", gg2b.ap, I["gg2"].ap, writes=[gg2b])

        with ExitStack() as eA:
            ssum = p.ps("p0ss", [1, TH], F32, es=eA)
            rstd_b = p.ps("p0rb", [128, TH], F32, es=eA)
            for i in range(NDC):
                xb, sq = xs[i % 2], sqs[i % 2]
                p.dma("Yd", xb.ap, I["x_f"].t[i * 128:(i + 1) * 128, :], writes=[xb])
                p.op("S", lambda e, xb=xb, sq=sq: e.activation(sq.ap, xb.ap, AF.Square),
                     reads=[xb], writes=[sq])
                for t0 in range(0, TH, 512):
                    p.mm(ssum.t[:, t0:t0 + 512], C["onesb"].t[:, 0:1],
                         sq.t[:, t0:t0 + 512], ssum, reads=[C["onesb"], sq],
                         start=(i == 0), stop=(i == NDC - 1))
            p.op("S", lambda e: e.activation(rstd_row.ap, ssum.ap, AF.Ln,
                                             bias=C['epsb'].t[0:1, 0:1], scale=1.0 / D),
                 reads=[ssum], writes=[rstd_row])
            p.op("S", lambda e: e.activation(rstd_row.ap, rstd_row.ap, AF.Exp,
                                             scale=-0.5),
                 reads=[rstd_row], writes=[rstd_row])
            for t0 in range(0, TH, 512):
                p.mm(rstd_b.t[:, t0:t0 + 512], C["ones"].t[0:1, :],
                     rstd_row.t[:, t0:t0 + 512], rstd_b,
                     reads=[C["ones"], rstd_row], start=True, stop=True)
            p.op("V", lambda e: e.tensor_copy(rstd_sb.ap, rstd_b.ap),
                 reads=[rstd_b], writes=[rstd_sb])
            for i in range(NDC):
                xb = xs[i % 2]
                p.dma("Yd", xb.ap, I["x_f"].t[i * 128:(i + 1) * 128, :], writes=[xb])
                p.op("V", lambda e, i=i, xb=xb: e.scalar_tensor_tensor(
                        hbuf[i].ap, xb.ap, C["n1w"].t[:, i:i + 1], rstd_sb.ap,
                        ALU.mult, ALU.mult),
                     reads=[xb, C["n1w"], rstd_sb], writes=[hbuf[i]])
            p.barrier()
        with ExitStack() as eB:
            gpre = p.ps("p0gp", [GGH, TH], F32, es=eB)
            grow = p.ps("p0gr", [128, 12], F32, es=eB)
            gg1v = gg1b.t[:].rearrange("p (k c) -> p k c", c=GGH)
            for t0 in range(0, TH, 512):
                p.mm_group(gpre,
                           [(gpre.t[:, t0:t0 + 512], gg1v[:, i, :],
                             hbuf[i].t[:, t0:t0 + 512]) for i in range(NDC)],
                           reads=[gg1b] + hbuf)
            p.op("S", lambda e: e.activation(gsil.ap, gpre.ap, AF.Silu),
                 reads=[gpre], writes=[gsil])
            for j in range(12):
                p.mm(grow.t[:, j:j + 1], gsil.t[:, j * 128:(j + 1) * 128], gg2b.ap,
                     grow, reads=[gsil, gg2b], start=True, stop=True)
            p.op("S", lambda e: e.activation(gammaT.ap, grow.ap, AF.Sigmoid),
                 reads=[grow], writes=[gammaT])

            tp = tap("h", [NDC, 128, TH], BF16)
            if tp is not None:
                for i in range(NDC):
                    p.dma("Yd", tp.t[i, :, :], hbuf[i].ap, reads=[hbuf[i]], writes=[tp])
            tp = tap("gammaT", [128, 12])
            if tp is not None:
                p.dma("Yd", tp.ap, gammaT.ap, reads=[gammaT], writes=[tp])
            p.barrier()


def _p1(p, nc, I, C, hbuf, gammaT, qmem, SC, tap, heads=None, upto="full"):
    with ExitStack() as e1:
        ispec = {nm: (s, d) for nm, s, d in IN_SPECS}
        for nm in P1_CONST_NAMES:
            C[nm] = p.sb("c_" + nm, ispec[nm][0], ispec[nm][1], es=e1)
            p.dma("Yd", C[nm].ap, I[nm].ap, writes=[C[nm]])
        wb = [p.sb(f"p1w{i}", [128, NDC * 384], BF16, es=e1) for i in range(2)]
        q_work = p.sb("p1qw", [128, T], F32, es=e1)
        k_work = p.sb("p1kw", [128, TH], F32, es=e1)
        row = p.sb("p1row", [1, 512], F32, es=e1)
        q1 = p.sb("p1q1", [128, T], F32, es=e1)
        k1 = p.sb("p1k1", [128, TH], F32, es=e1)
        t1 = p.sb("p1t1", [128, TH], F32, es=e1)
        t2 = p.sb("p1t2", [128, TH], F32, es=e1)
        sq = p.sb("p1sqb", [128, 512], BF16, es=e1)
        qr = t2
        kr = t2
        q2 = p.sb("p1q2", [128, T], BF16, es=e1)
        k2 = p.sb("p1k2", [128, TH], BF16, es=e1)
        vts = [p.sb(f"p1vt{i}", [128, 12 * 128], BF16, es=e1) for i in range(2)]
        kpts = [p.sb(f"p1kpt{i}", [128, 12 * 128], BF16, es=e1) for i in range(2)]
        sms_ = [p.sb(f"p1sm{i}", [128, 640], F32, es=e1) for i in range(2)]
        pns = [p.sb(f"p1pn{i}", [128, 640], BF16, es=e1) for i in range(2)]
        oms = [p.sb(f"p1om{i}", [128, 256], F32, es=e1) for i in range(2)]
        poms = [p.sb(f"p1pom{i}", [128, 256], BF16, es=e1) for i in range(2)]
        pTs = [p.sb(f"p1pT{i}", [128, 896], BF16, es=e1) for i in range(2)]
        attnhs = [p.sb(f"p1ah{i}", [128, T], BF16, es=e1) for i in range(2)]
        mcols = [p.sb(f"p1m{i}", [128, 1], F32, es=e1) for i in range(2)]
        sumcs = [p.sb(f"p1su{i}", [128, 1], F32, es=e1) for i in range(2)]
        osums = [p.sb(f"p1os{i}", [128, 1], F32, es=e1) for i in range(2)]
        recs = [p.sb(f"p1re{i}", [128, 1], F32, es=e1) for i in range(2)]
        orecs = [p.sb(f"p1or{i}", [128, 1], F32, es=e1) for i in range(2)]
        PJ = [p.ps(f"p1pj{i}", [128, 512], F32, es=e1) for i in range(2)]
        PS = [p.ps(f"p1ps{i}", [128, 640], F32, es=e1) for i in range(2)]
        TP = p.ps("p1ptp", [128, 896], BF16, es=e1)
        PO = p.ps("p1po", [128, 512], F32, es=e1)
        ROW = PJ[0]
        BC = PJ[1]

        def wdma(hd, dst):
            src = I["wqkv_p"].t[:, 384 * hd:384 * (hd + 1)].rearrange(
                "(k p) c -> p k c", p=128)
            p.dma("Gd", dst.t[:].rearrange("p (k c) -> p k c", c=384), src,
                  writes=[dst])

        wdma(0, wb[0])
        tap_q2 = tap("q2h0", [128, T], BF16)
        tap_k2 = tap("k2h0", [128, TH], BF16)
        tap_attn = tap("attn", [NH, 128, T], BF16)
        tap_qm = tap("qmemt", [NH, 128, T], BF16)

        def proj_items(hd):
            wcur = wb[hd % 2]
            wv = wcur.t[:].rearrange("p (k c) -> p k c", c=384)
            vt = vts[hd % 2]
            items = []
            pjc = [0]
            def nxt():
                b = PJ[pjc[0] % 2]
                pjc[0] += 1
                return b
            for tt in range(2):
                def it(tt=tt):
                    PJb = nxt()
                    p.mm_group(PJb, [(PJb.ap, wv[:, k, 0:128],
                                      hbuf[k].t[:, HALO + 512 * tt:HALO + 512 * (tt + 1)])
                                     for k in range(NDC)], reads=[wcur] + hbuf)
                    p.op("S", lambda e, tt=tt, PJb=PJb: e.copy(
                            q_work.t[:, 512 * tt:512 * (tt + 1)], PJb.ap),
                         reads=[PJb], writes=[q_work])
                items.append(it)
            for tt in range(3):
                def it(tt=tt):
                    PJb = nxt()
                    p.mm_group(PJb, [(PJb.ap, wv[:, k, 128:256],
                                      hbuf[k].t[:, 512 * tt:512 * (tt + 1)])
                                     for k in range(NDC)], reads=[wcur] + hbuf)
                    p.op("S", lambda e, tt=tt, PJb=PJb: e.copy(
                            k_work.t[:, 512 * tt:512 * (tt + 1)], PJb.ap),
                         reads=[PJb], writes=[k_work])
                items.append(it)
            for j in range(12):
                def it(j=j):
                    PJb = nxt()
                    p.mm_group(PJb, [(PJb.t[:, 0:128],
                                      hbuf[k].t[:, 128 * j:128 * (j + 1)],
                                      wv[:, k, 256:384]) for k in range(NDC)],
                               reads=[wcur] + hbuf)
                    p.op("S", lambda e, j=j, PJb=PJb, vt=vt: e.copy(
                            vt.t[:, 128 * j:128 * (j + 1)], PJb.t[:, 0:128]),
                         reads=[PJb], writes=[vt])
                items.append(it)
            return items

        def proj(hd):
            for it in proj_items(hd):
                it()

        hlist = list(heads if heads is not None else range(NH))
        proj(hlist[0])
        if len(hlist) > 1:
            wdma(hlist[1], wb[hlist[1] % 2])
        for hd in hlist:
            vt = vts[hd % 2]
            kpt = kpts[hd % 2]
            attnh = attnhs[hd % 2]
            nxt_items = []
            if hd + 1 < NH:
                nxt_items = proj_items(hd + 1)
                if hd + 2 < NH:
                    wdma(hd + 2, wb[(hd + 2) % 2])

            def headnorm(work, scale_c, out_b, n):
                for tt in range(n // 512):
                    p.op("S", lambda e, tt=tt: e.activation(
                            sq.ap, work.t[:, 512 * tt:512 * (tt + 1)], AF.Square),
                         reads=[work], writes=[sq])
                    p.mm(ROW.t[0:1, 0:512], C["onesb"].t[:, 0:1],
                         sq.ap, ROW, reads=[C["onesb"], sq])
                    p.op("S", lambda e: e.activation(
                            row.ap, ROW.t[0:1, 0:512],
                            AF.Ln, bias=C["epsb"].t[0:1, 0:1], scale=1.0 / DH),
                         reads=[ROW, C["epsb"]], writes=[row])
                    p.op("S", lambda e: e.activation(row.ap, row.ap,
                                                     AF.Exp, scale=-0.5),
                         reads=[row], writes=[row])
                    p.mm(BC.t[:, 0:512], C["ones"].t[0:1, :], row.ap,
                         BC, reads=[C["ones"], row])
                    p.op("V", lambda e, tt=tt: e.scalar_tensor_tensor(
                            out_b.t[:, 512 * tt:512 * (tt + 1)],
                            work.t[:, 512 * tt:512 * (tt + 1)],
                            C[scale_c].ap, BC.t[:, 0:512], ALU.mult, ALU.mult),
                         reads=[work, C[scale_c], BC], writes=[out_b])
            if hd == 0:
                tpx = tap("qworkh0", [128, T], F32)
                if tpx is not None:
                    p.dma("Yd", tpx.ap, q_work.ap, reads=[q_work], writes=[tpx])
            if upto == "qwork":
                continue
            headnorm(q_work, "qscale", q1, T)
            if hd == 0:
                tpx = tap("q1h0", [128, T], F32)
                if tpx is not None:
                    p.dma("Yd", tpx.ap, q1.ap, reads=[q1], writes=[tpx])
                tpx = tap("rowh0", [1, T], F32)
                if tpx is not None:
                    p.dma("Yd", tpx.ap, row.t[:, 0:T], reads=[row], writes=[tpx])
            headnorm(k_work, "kscale", k1, TH)

            def rope(src, rot, cos_c, sin_c, out_b, n):
                p.op("V", lambda e: e.tensor_copy(rot.t[0:64, 0:n], src.t[64:128, 0:n]),
                     reads=[src], writes=[rot])
                p.op("V", lambda e: e.tensor_copy(rot.t[64:128, 0:n], src.t[0:64, 0:n]),
                     reads=[src], writes=[rot])
                p.op("V", lambda e: e.tensor_tensor(t1.t[:, 0:n], src.t[:, 0:n],
                                                    C[cos_c].t[:, 0:n], ALU.mult),
                     reads=[src, C[cos_c]], writes=[t1])
                p.op("V", lambda e: e.tensor_tensor(t2.t[:, 0:n], rot.t[:, 0:n],
                                                    C[sin_c].t[:, 0:n], ALU.mult),
                     reads=[rot, C[sin_c]], writes=[t2])
                p.op("V", lambda e: e.tensor_tensor(out_b.t[:, 0:n], t1.t[:, 0:n],
                                                    t2.t[:, 0:n], ALU.add),
                     reads=[t1, t2], writes=[out_b])
            if upto == "norm":
                continue
            for _ in range(6):
                if nxt_items:
                    nxt_items.pop(0)()
            rope(q1, qr, "cosq", "sinq", q2, T)
            rope(k1, kr, "cosk", "sink", k2, TH)

            for j in range(12):
                sl = 128 * (j % 7)
                p.op("T", lambda e, j=j, sl=sl: e.transpose(
                        TP.t[:, sl:sl + 128], k2.t[:, 128 * j:128 * (j + 1)],
                        C["ident"].ap),
                     reads=[k2, C["ident"]], writes=[TP])
                p.op("S", lambda e, j=j, sl=sl, kpt=kpt: e.activation(
                        kpt.t[:, 128 * j:128 * (j + 1)], TP.t[:, sl:sl + 128],
                        AF.Copy, scale=gammaT.t[:, j:j + 1]),
                     reads=[TP, gammaT], writes=[kpt])

            def stA(qi):
                ks = 128 * qi
                par = qi % 2
                PSb = PS[par]
                p.mm(PSb.t[:, 0:512], q2.t[:, ks:ks + 128],
                     k2.t[:, ks:ks + 512], PSb, reads=[q2, k2])
                p.mm(PSb.t[:, 512:640], q2.t[:, ks:ks + 128],
                     k2.t[:, ks + 512:ks + 640], PSb, reads=[q2, k2])

            def stB(qi):
                par = qi % 2
                PSb, smb, pnb, omb, pomb = PS[par], sms_[par], pns[par], oms[par], poms[par]
                mcol, sumc, osum, rec, orec = (mcols[par], sumcs[par], osums[par],
                                               recs[par], orecs[par])
                p.op("V", lambda e, qi=qi, PSb=PSb, smb=smb: e.tensor_tensor(
                        smb.ap, PSb.ap, C["swa_cat"].t[:, 640 * qi:640 * (qi + 1)],
                        ALU.add),
                     reads=[PSb, C["swa_cat"]], writes=[smb])
                p.op("V", lambda e, smb=smb, mcol=mcol: e.reduce_max(
                        mcol.ap, smb.ap, axis=AX.X, negate=True),
                     reads=[smb], writes=[mcol])
                p.op("S", lambda e, smb=smb, pnb=pnb, mcol=mcol, sumc=sumc:
                     e.activation(pnb.ap, smb.ap, AF.Exp, bias=mcol.ap,
                                  accum_out=sumc.ap),
                     reads=[smb, mcol], writes=[pnb, sumc])
                p.op("V", lambda e, rec=rec, sumc=sumc: e.reciprocal(rec.ap, sumc.ap),
                     reads=[sumc], writes=[rec])
                p.op("V", lambda e, pnb=pnb, rec=rec: e.tensor_scalar(
                        pnb.ap, pnb.ap, rec.ap, None, ALU.mult),
                     reads=[pnb, rec], writes=[pnb])
                p.op("V", lambda e, qi=qi, smb=smb, omb=omb: e.tensor_tensor(
                        omb.ap, smb.t[:, 384:640],
                        C["om_cat"].t[:, 256 * qi:256 * (qi + 1)], ALU.add),
                     reads=[smb, C["om_cat"]], writes=[omb])
                p.op("S", lambda e, omb=omb, pomb=pomb, mcol=mcol, osum=osum:
                     e.activation(pomb.ap, omb.ap, AF.Exp, bias=mcol.ap,
                                  accum_out=osum.ap),
                     reads=[omb, mcol], writes=[pomb, osum])
                p.op("V", lambda e, orec=orec, osum=osum: e.reciprocal(
                        orec.ap, osum.ap),
                     reads=[osum], writes=[orec])
                p.op("V", lambda e, pomb=pomb, orec=orec: e.scalar_tensor_tensor(
                        pomb.ap, pomb.ap, orec.ap, C["d_cat"].ap,
                        ALU.mult, ALU.mult),
                     reads=[pomb, orec, C["d_cat"]], writes=[pomb])

            def stC(qi, hd):
                ks = 128 * qi
                par = qi % 2
                pnb, pomb, pTb = pns[par], poms[par], pTs[par]
                for b in range(5):
                    p.op("T", lambda e, b=b, pnb=pnb: e.transpose(
                            TP.t[:, 128 * b:128 * (b + 1)],
                            pnb.t[:, 128 * b:128 * (b + 1)], C["ident"].ap),
                         reads=[pnb, C["ident"]], writes=[TP])
                for oi in range(2):
                    p.op("T", lambda e, oi=oi, pomb=pomb: e.transpose(
                            TP.t[:, 640 + 128 * oi:640 + 128 * (oi + 1)],
                            pomb.t[:, 128 * oi:128 * (oi + 1)], C["ident"].ap),
                         reads=[pomb, C["ident"]], writes=[TP])
                p.op("V", lambda e, pTb=pTb: e.tensor_copy(pTb.ap, TP.ap),
                     reads=[TP], writes=[pTb])
                o0 = 256 * par
                p.mm_group(PO, [(PO.t[:, o0:o0 + 128],
                                 vt.t[:, 128 * (qi + b):128 * (qi + b + 1)],
                                 pTb.t[:, 128 * b:128 * (b + 1)]) for b in range(5)],
                           reads=[vt, pTb])
                p.mm_group(PO, [(PO.t[:, o0 + 128:o0 + 256],
                                 kpt.t[:, 128 * (qi + b):128 * (qi + b + 1)],
                                 pTb.t[:, 640 + 128 * (b - 3):640 + 128 * (b - 2)])
                                for b in (3, 4)],
                           reads=[kpt, pTb])
                p.op("S", lambda e, ks=ks, o0=o0, attnh=attnh: e.copy(
                        attnh.t[:, ks:ks + 128], PO.t[:, o0:o0 + 128]),
                     reads=[PO], writes=[attnh])
                p.op("S", lambda e, ks=ks, hd=hd, o0=o0: e.copy(
                        qmem[hd].t[:, ks:ks + 128], PO.t[:, o0 + 128:o0 + 256]),
                     reads=[PO], writes=[qmem[hd]])

            stA(0)
            stA(1)
            stB(0)
            for qi in range(8):
                if qi + 2 < 8:
                    stA(qi + 2)
                stC(qi, hd)
                for _ in range(3 if qi < 5 else 1):
                    if nxt_items:
                        nxt_items.pop(0)()
                if qi + 1 < 8:
                    stB(qi + 1)
            while nxt_items:
                nxt_items.pop(0)()

            p.dma("Yd", SC["attn"].t[hd, :, :], attnh.ap,
                  reads=[attnh], writes=[SC["attn"]])
            if tap_attn is not None:
                p.dma("Yd", tap_attn.t[hd, :, :], attnh.ap,
                      reads=[attnh], writes=[tap_attn])
            if tap_qm is not None:
                p.dma("Yd", tap_qm.t[hd, :, :], qmem[hd].ap,
                      reads=[qmem[hd]], writes=[tap_qm])
            if hd == 0:
                if tap_q2 is not None:
                    p.dma("Yd", tap_q2.ap, q2.ap, reads=[q2], writes=[tap_q2])
                if tap_k2 is not None:
                    p.dma("Yd", tap_k2.ap, k2.ap, reads=[k2], writes=[tap_k2])
                for nmt, bt, shp, dt in (("vth0", vt, [128, 1536], BF16),
                                         ("kpth0", kpt, [128, 1536], BF16)):
                    tpx = tap(nmt, shp, dt)
                    if tpx is not None:
                        p.dma("Yd", tpx.ap, bt.ap, reads=[bt], writes=[tpx])
        p.barrier()



def _p2(p, nc, I, C, hbuf, qmem, SC, rms2ss, tap):
    tap_m = tap("macc", [NDC, 128, T], F32)
    tap_xa = tap("xa", [NDC, 128, T], F32)
    tap_r = tap("r", [NDC, 128, T], F32)
    with ExitStack() as e2:
        w1b = [p.sb(f"p2w1{i}", [128, NDC * 128], BF16, es=e2) for i in range(2)]
        w2b = [p.sb(f"p2w2{i}", [128, 8 * 128], BF16, es=e2) for i in range(2)]
        wob = [p.sb(f"p2wo{i}", [128, NDC * 128], BF16, es=e2) for i in range(2)]
        macc = [p.sb(f"p2ma{m}", [128, T], F32, es=e2) for m in range(NDC)]
        poly = [p.sb(f"p2po{f}", [128, T], BF16, es=e2) for f in range(8)]
        scr = p.sb("p2scr", [128, 512], F32, es=e2)
        sqb = p.sb("p2sqb", [128, 512], BF16, es=e2)
        sg = p.sb("p2sg", [128, 512], F32, es=e2)
        t2b = sg
        xac = p.sb("p2xa", [128, T], F32, es=e2)
        rc = p.sb("p2rc", [128, T], F32, es=e2)
        xcs = [p.sb(f"p2xc{i}", [128, T], F32, es=e2) for i in range(2)]
        PH = [p.ps(f"p2ph{i}", [128, 512], F32, es=e2) for i in range(2)]
        PW = [p.ps(f"p2pw{i}", [128, 512], F32, es=e2) for i in range(2)]
        PA = [p.ps(f"p2pa{i}", [128, 512], F32, es=e2) for i in range(2)]
        RS = p.ps("p2rs", [1, T], F32, es=e2)

        def ldw1(idx, dst):
            p.dma("Gd", dst.t[:].rearrange("p (k c) -> p k c", c=128),
                  I["mw1"].t[:, 128 * idx:128 * (idx + 1)]
                  .rearrange("(k p) c -> p k c", p=128), writes=[dst])

        def ldw2(win, m, dst):
            p.dma("Gd", dst.t[:].rearrange("p (k c) -> p k c", c=128),
                  I["mw2"].t[1024 * win:1024 * (win + 1), 128 * m:128 * (m + 1)]
                  .rearrange("(k p) c -> p k c", p=128), writes=[dst])

        def ldwo(m, dst):
            p.dma("Gd", dst.t[:].rearrange("p (k c) -> p k c", c=128),
                  I["wo"].t[:, 128 * m:128 * (m + 1)]
                  .rearrange("(k p) c -> p k c", p=128), writes=[dst])

        for i in range(NDC):
            p.dma("Yd", hbuf[i].t[:, 0:T], SC["attn"].t[i, :, :],
                  reads=[SC["attn"]], writes=[hbuf[i]])
        ldw1(0, w1b[0])
        for win in range(8):
            for fs8 in range(8):
                idx = 8 * win + fs8
                wc = w1b[idx % 2]
                if idx + 1 < 64:
                    ldw1(idx + 1, w1b[(idx + 1) % 2])
                wv = wc.t[:].rearrange("p (k c) -> p k c", c=128)
                for th in range(2):
                    t0 = 512 * th
                    PHb = PH[th]
                    p.mm_group(PHb, [(PHb.ap, wv[:, k, :], qmem[k].t[:, t0:t0 + 512])
                                     for k in range(NDC)], reads=[wc] + qmem)
                    p.op("S", lambda e, PHb=PHb: e.activation(
                            scr.ap, PHb.ap, AF.Silu),
                         reads=[PHb], writes=[scr])
                    p.op("V", lambda e, fs8=fs8, t0=t0: e.scalar_tensor_tensor(
                            poly[fs8].t[:, t0:t0 + 512], scr.ap,
                            1.0, scr.ap, ALU.add, ALU.mult),
                         reads=[scr], writes=[poly[fs8]])
            if win == 0:
                ldw2(0, 0, w2b[0])
            for m in range(NDC):
                wc = w2b[m % 2]
                nwin, nm = (win, m + 1) if m + 1 < NDC else (win + 1, 0)
                if nwin < 8:
                    ldw2(nwin, nm, w2b[(m + 1) % 2])
                elif win == 7 and m == NDC - 1:
                    ldwo(0, wob[0])
                wv = wc.t[:].rearrange("p (k c) -> p k c", c=128)
                for th in range(2):
                    t0 = 512 * th
                    PWb = PW[th]
                    p.mm_group(PWb, [(PWb.ap, wv[:, k, :],
                                      poly[k].t[:, t0:t0 + 512])
                                     for k in range(8)], reads=[wc] + poly)
                    if win == 0:
                        p.op("V", lambda e, m=m, t0=t0, PWb=PWb: e.tensor_copy(
                                macc[m].t[:, t0:t0 + 512], PWb.ap),
                             reads=[PWb], writes=[macc[m]])
                    else:
                        p.op("V", lambda e, m=m, t0=t0, PWb=PWb: e.tensor_tensor(
                                macc[m].t[:, t0:t0 + 512],
                                macc[m].t[:, t0:t0 + 512], PWb.ap, ALU.add),
                             reads=[macc[m], PWb], writes=[macc[m]])
        p.dma("Yd", xcs[0].ap, I["x_f"].t[0:128, HALO:HALO + T], writes=[xcs[0]])
        for m in range(NDC):
            wc = wob[m % 2]
            if m + 1 < NDC:
                ldwo(m + 1, wob[(m + 1) % 2])
                p.dma("Yd", xcs[(m + 1) % 2].ap,
                      I["x_f"].t[128 * (m + 1):128 * (m + 2), HALO:HALO + T],
                      writes=[xcs[(m + 1) % 2]])
            wv = wc.t[:].rearrange("p (k c) -> p k c", c=128)
            xc = xcs[m % 2]
            for th in range(2):
                t0 = 512 * th
                PAb = PA[th]
                p.mm_group(PAb, [(PAb.ap, wv[:, k, :], hbuf[k].t[:, t0:t0 + 512])
                                 for k in range(NDC)], reads=[wc] + hbuf)
                p.op("S", lambda e, m=m, t0=t0: e.activation(
                        sg.ap, macc[m].t[:, t0:t0 + 512], AF.Sigmoid),
                     reads=[macc[m]], writes=[sg])
                p.op("V", lambda e, PAb=PAb: e.tensor_tensor(t2b.ap, sg.ap, PAb.ap,
                                                             ALU.mult),
                     reads=[sg, PAb], writes=[t2b])
                p.op("V", lambda e, xc=xc, t0=t0: e.tensor_tensor(
                        xac.t[:, t0:t0 + 512], xc.t[:, t0:t0 + 512], t2b.ap,
                        ALU.add),
                     reads=[xc, t2b], writes=[xac])
                p.op("V", lambda e, xc=xc, t0=t0: e.scalar_tensor_tensor(
                        rc.t[:, t0:t0 + 512], xc.t[:, t0:t0 + 512], 2.0, t2b.ap,
                        ALU.mult, ALU.add),
                     reads=[xc, t2b], writes=[rc])
                p.op("S", lambda e, t0=t0: e.activation(
                        sqb.ap, xac.t[:, t0:t0 + 512], AF.Square),
                     reads=[xac], writes=[sqb])
                p.mm(RS.t[:, t0:t0 + 512], C["onesb"].t[:, 0:1],
                     sqb.ap, RS, reads=[C["onesb"], sqb],
                     start=(m == 0), stop=(m == NDC - 1))
            p.dma("Yd", SC["xa"].t[m, :, :], xac.ap, reads=[xac], writes=[SC["xa"]])
            p.dma("Yd", SC["r"].t[m, :, :], rc.ap, reads=[rc], writes=[SC["r"]])
            if tap_m is not None:
                p.dma("Yd", tap_m.t[m, :, :], macc[m].ap,
                      reads=[macc[m]], writes=[tap_m])
            if tap_xa is not None:
                p.dma("Yd", tap_xa.t[m, :, :], xac.ap, reads=[xac], writes=[tap_xa])
            if tap_r is not None:
                p.dma("Yd", tap_r.t[m, :, :], rc.ap, reads=[rc], writes=[tap_r])
        p.op("V", lambda e: e.tensor_copy(rms2ss.ap, RS.ap),
             reads=[RS], writes=[rms2ss])
        p.barrier()



def _p3(p, nc, I, C, SC, rms2ss, tap):
    tap_y = tap("y", [NDC, 128, T], BF16)
    tap_rf = tap("rfin", [NDC, 128, T], F32)
    tap_x1 = tap("x1f", [NDC, 128, T], F32)
    NF3 = FFHP // 128
    with ExitStack() as e3:
        y = [p.sb(f"p3y{i}", [128, T], BF16, es=e3) for i in range(NDC)]
        g = [p.sb(f"p3g{f}", [128, T], BF16, es=e3) for f in range(NF3)]
        w1c = [p.sb(f"p3w1{i}", [128, NDC * 128], BF16, es=e3) for i in range(2)]
        w3c = [p.sb(f"p3w3{i}", [128, NDC * 128], BF16, es=e3) for i in range(2)]
        w2c = [p.sb(f"p3w2{i}", [128, NF3 * 128], BF16, es=e3) for i in range(2)]
        xc = [p.sb(f"p3xc{i}", [128, T], F32, es=e3) for i in range(2)]
        rfin = p.sb("p3rf", [128, T], F32, es=e3)
        scr = p.sb("p3scr", [128, T], F32, es=e3)
        sqb = p.sb("p3sqb", [128, 512], BF16, es=e3)
        rows = p.sb("p3rows", [1, 4 * T], F32, es=e3)
        R2 = rows.t[:, 0:T]; LS = rows.t[:, T:2*T]; LQ = rows.t[:, 2*T:3*T]
        MR = rows.t[:, 3*T:4*T]; R1 = R2
        bco = p.sb("p3bco", [128, T], F32, es=e3)
        mb = p.sb("p3mb", [128, T], F32, es=e3)
        x1b = p.sb("p3x1b", [128, T], BF16, es=e3)
        x1f = rfin
        BC = p.ps("p3bc", [128, 512], F32, es=e3)
        PH = [p.ps(f"p3ph{i}", [128, 512], F32, es=e3) for i in range(2)]
        LR = [p.ps(f"p3lr{i}", [1, 512], F32, es=e3) for i in range(4)]

        def bcast(src_slice, dst):
            for th in range(2):
                p.mm(BC.ap, C["ones"].t[0:1, :], src_slice[:, 512*th:512*(th+1)],
                     BC, reads=[C["ones"], rows])
                p.op("V", lambda e, th=th, dst=dst: e.tensor_copy(
                        dst.t[:, 512*th:512*(th+1)], BC.ap),
                     reads=[BC], writes=[dst])

        p.op("S", lambda e: e.activation(R2, rms2ss.ap, AF.Ln,
                                         bias=C["epsb"].t[0:1, 0:1], scale=1.0 / D),
             reads=[rms2ss, C["epsb"]], writes=[rows])
        p.op("S", lambda e: e.activation(R2, R2, AF.Exp, scale=-0.5),
             reads=[rows], writes=[rows])
        bcast(R2, bco)
        for i in range(NDC):
            xb = xc[i % 2]
            p.dma("Yd", xb.ap, SC["xa"].t[i, :, :], reads=[SC["xa"]], writes=[xb])
            p.op("V", lambda e, i=i, xb=xb: e.scalar_tensor_tensor(
                    y[i].ap, xb.ap, C["n2w"].t[:, i:i+1], bco.ap, ALU.mult, ALU.mult),
                 reads=[xb, C["n2w"], bco], writes=[y[i]])
            if tap_y is not None:
                p.dma("Yd", tap_y.t[i, :, :], y[i].ap, reads=[y[i]], writes=[tap_y])

        def ldw13(fs, b):
            for nm, dst in (("fw1", w1c[b]), ("fw3", w3c[b])):
                p.dma("Gd", dst.t[:].rearrange("p (k c) -> p k c", c=128),
                      I[nm].t[:, 128*fs:128*(fs+1)].rearrange("(k p) c -> p k c", p=128),
                      writes=[dst])
        ldw13(0, 0)
        for fs in range(NF3):
            if fs + 1 < NF3:
                ldw13(fs + 1, (fs + 1) % 2)
            wv1 = w1c[fs % 2].t[:].rearrange("p (k c) -> p k c", c=128)
            wv3 = w3c[fs % 2].t[:].rearrange("p (k c) -> p k c", c=128)
            for th in range(2):
                t0 = 512 * th
                p.mm_group(PH[0], [(PH[0].ap, wv1[:, k, :], y[k].t[:, t0:t0+512])
                                   for k in range(NDC)], reads=[w1c[fs % 2]] + y)
                p.op("S", lambda e, t0=t0: e.activation(scr.t[:, t0:t0+512],
                                                        PH[0].ap, AF.Silu),
                     reads=[PH[0]], writes=[scr])
                p.mm_group(PH[1], [(PH[1].ap, wv3[:, k, :], y[k].t[:, t0:t0+512])
                                   for k in range(NDC)], reads=[w3c[fs % 2]] + y)
                p.op("V", lambda e, fs=fs, t0=t0: e.tensor_tensor(
                        g[fs].t[:, t0:t0+512], scr.t[:, t0:t0+512], PH[1].ap,
                        ALU.mult),
                     reads=[scr, PH[1]], writes=[g[fs]])

        def ldw2(m, b):
            p.dma("Gd", w2c[b].t[:].rearrange("p (k c) -> p k c", c=128),
                  I["fw2"].t[:, 128*m:128*(m+1)].rearrange("(k p) c -> p k c", p=128),
                  writes=[w2c[b]])
        ldw2(0, 0)
        for m in range(NDC):
            if m + 1 < NDC:
                ldw2(m + 1, (m + 1) % 2)
            rb = xc[m % 2]
            p.dma("Yd", rb.ap, SC["r"].t[m, :, :], reads=[SC["r"]], writes=[rb])
            wv2 = w2c[m % 2].t[:].rearrange("p (k c) -> p k c", c=128)
            for th in range(2):
                t0 = 512 * th
                PHb = PH[th]
                p.mm_group(PHb, [(PHb.ap, wv2[:, k, :], g[k].t[:, t0:t0+512])
                                 for k in range(NF3)], reads=[w2c[m % 2]] + g)
                p.op("V", lambda e, t0=t0, rb=rb, PHb=PHb: e.tensor_tensor(
                        rfin.t[:, t0:t0+512], rb.t[:, t0:t0+512], PHb.ap, ALU.add),
                     reads=[rb, PHb], writes=[rfin])
                p.mm(LR[th].ap, C["ones"].t[:, 0:1], rfin.t[:, t0:t0+512],
                     LR[th], reads=[C["ones"], rfin],
                     start=(m == 0), stop=(m == NDC - 1))
                p.op("S", lambda e, t0=t0: e.activation(sqb.ap,
                                                        rfin.t[:, t0:t0+512],
                                                        AF.Square),
                     reads=[rfin], writes=[sqb])
                p.mm(LR[2+th].ap, C["onesb"].t[:, 0:1], sqb.ap,
                     LR[2+th], reads=[C["onesb"], sqb],
                     start=(m == 0), stop=(m == NDC - 1))
            p.dma("Yd", SC["r"].t[m, :, :], rfin.ap, reads=[rfin], writes=[SC["r"]])
            if tap_rf is not None:
                p.dma("Yd", tap_rf.t[m, :, :], rfin.ap, reads=[rfin], writes=[tap_rf])

        for th in range(2):
            p.op("V", lambda e, th=th: e.tensor_copy(LS[:, 512*th:512*(th+1)],
                                                     LR[th].ap),
                 reads=[LR[th]], writes=[rows])
            p.op("V", lambda e, th=th: e.tensor_copy(LQ[:, 512*th:512*(th+1)],
                                                     LR[2+th].ap),
                 reads=[LR[2+th]], writes=[rows])
        p.op("V", lambda e: e.tensor_scalar(MR, LS, 1.0 / D, None, ALU.mult),
             reads=[rows], writes=[rows])
        p.op("V", lambda e: e.tensor_scalar(LQ, LQ, 1.0 / D, None, ALU.mult),
             reads=[rows], writes=[rows])
        p.op("V", lambda e: e.tensor_tensor(R1, MR, MR, ALU.mult),
             reads=[rows], writes=[rows])
        p.op("V", lambda e: e.tensor_tensor(LQ, LQ, R1, ALU.subtract),
             reads=[rows], writes=[rows])
        p.op("S", lambda e: e.activation(R1, LQ, AF.Ln, bias=C["epsb"].t[0:1, 1:2]),
             reads=[rows, C["epsb"]], writes=[rows])
        p.op("S", lambda e: e.activation(R1, R1, AF.Exp, scale=-0.5),
             reads=[rows], writes=[rows])
        bcast(MR, mb)
        bcast(R1, bco)
        for m in range(NDC):
            rb = xc[m % 2]
            p.dma("Yd", rb.ap, SC["r"].t[m, :, :], reads=[SC["r"]], writes=[rb])
            p.op("G", lambda e, rb=rb: e.tensor_tensor(scr.ap, rb.ap, mb.ap,
                                                       ALU.subtract),
                 reads=[rb, mb], writes=[scr])
            p.op("V", lambda e: e.tensor_tensor(x1f.ap, scr.ap, bco.ap, ALU.mult),
                 reads=[scr, bco], writes=[x1f])
            p.op("S", lambda e: e.copy(x1b.ap, x1f.ap), reads=[x1f], writes=[x1b])
            p.dma("Yd", SC["x1f"].t[m, :, :], x1f.ap, reads=[x1f], writes=[SC["x1f"]])
            p.dma("Yd", SC["x1b"].t[m, :, :], x1b.ap, reads=[x1b], writes=[SC["x1b"]])
            if tap_x1 is not None:
                p.dma("Yd", tap_x1.t[m, :, :], x1f.ap, reads=[x1f], writes=[tap_x1])
        p.barrier()



def _p4(p, nc, I, C, SC, ln2s, tap):
    tap_w = tap("gatew", [2, 128, 32], F32)
    tap_r2 = tap("r2", [NDC, 128, T], F32)
    NF4 = DFF // 128
    with ExitStack() as e4:
        x1h = [p.sb(f"p4x{k}", [128, 512], BF16, es=e4) for k in range(NDC)]
        wrb = p.sb("p4wr", [128, NDC * NEXP], BF16, es=e4)
        bdb = p.sb("p4bd", [128, D], BF16, es=e4)
        atile = [p.sb(f"p4at{i}", [128, NDC * 128], BF16, es=e4) for i in range(1)]
        t_sb = p.sb("p4t", [48, 16 * 512], BF16, es=e4)
        gbp = [p.sb(f"p4gb{f}", [128, 1024], BF16, es=e4) for f in range(NF4 // 2)]
        ubp = [p.sb(f"p4ub{f}", [128, 1024], BF16, es=e4) for f in range(NF4 // 2)]
        def gb_ap(fs):
            return gbp[fs // 2].t[:, 512 * (fs % 2):512 * (fs % 2) + 512]
        def ub_ap(fs):
            return ubp[fs // 2].t[:, 512 * (fs % 2):512 * (fs % 2) + 512]
        bgc = p.sb("p4bgc", [48, DFF], BF16, es=e4)
        buc = p.sb("p4buc", [48, DFF], BF16, es=e4)
        adb = [p.sb(f"p4ad{i}", [128, NF4 * R], BF16, es=e4) for i in range(1)]
        wdc = [p.sb(f"p4wd{i}", [128, 22 * 128], BF16, es=e4) for i in range(2)]
        lgs = p.sb("p4lg", [128, 32], F32, es=e4)
        wgt = p.sb("p4wt", [128, 32], F32, es=e4)
        wtr = p.sb("p4wtr", [8, 512], F32, es=e4)
        wrows = p.sb("p4wrow", [1, 512], F32, es=e4)
        wsb = p.sb("p4wsb", [16, 512], F32, es=e4)
        dtt = p.sb("p4dtt", [16, 512], BF16, es=e4)
        dT_all = p.sb("p4dta", [128, 512], BF16, es=e4)
        scr1 = p.sb("p4s1", [128, 512], F32, es=e4)
        scr2 = p.sb("p4s2", [128, 512], F32, es=e4)
        scr3 = None
        heb = p.sb("p4he", [128, 512], BF16, es=e4)
        scr3 = scr1
        xfc = [p.sb("p4xf0", [128, 512], F32, es=e4), scr2]
        r2c = p.sb("p4r2", [128, 512], F32, es=e4)
        col = p.sb("p4col", [128, 4], F32, es=e4)

        p.dma("Gd", wrb.t[:].rearrange("p (k c) -> p k c", c=NEXP),
              I["wr"].ap.rearrange("(k p) c -> p k c", p=128), writes=[wrb])
        p.dma("Gd", bdb.ap, I["bd"].ap, writes=[bdb])

        for hi, t0 in enumerate((0, 512)):
            for k in range(NDC):
                p.dma("Yd", x1h[k].ap, SC["x1b"].t[k, :, t0:t0 + 512],
                      reads=[SC["x1b"]], writes=[x1h[k]])

            with ExitStack() as eR:
                PT4 = p.ps(f"p4pt{hi}", [128, 32], F32, es=eR)
                TRP = p.ps(f"p4trp{hi}", [8, 512], F32, es=eR)
                wrv = wrb.t[:].rearrange("p (k c) -> p k c", c=NEXP)
                for tt in range(4):
                    p.mm_group(PT4, [(PT4.t[:, 8 * tt:8 * (tt + 1)],
                                      x1h[k].t[:, 128 * tt:128 * (tt + 1)],
                                      wrv[:, k, :]) for k in range(NDC)],
                               reads=[wrb] + x1h)
                p.op("V", lambda e: e.tensor_copy(lgs.ap, PT4.ap),
                     reads=[PT4], writes=[lgs])
                for tt in range(4):
                    sl = lgs.t[:, 8 * tt:8 * (tt + 1)]
                    wl = wgt.t[:, 8 * tt:8 * (tt + 1)]
                    p.op("V", lambda e, sl=sl: e.reduce_max(col.t[:, 0:1], sl, axis=AX.X),
                         reads=[lgs], writes=[col])
                    p.op("V", lambda e: e.tensor_scalar_mul(col.t[:, 3:4],
                                                            col.t[:, 0:1], -1.0),
                         reads=[col], writes=[col])
                    p.op("S", lambda e, sl=sl, wl=wl: e.activation(
                            wl, sl, AF.Exp, bias=col.t[:, 3:4],
                            accum_out=col.t[:, 2:3]),
                         reads=[lgs, col], writes=[wgt, col])
                    p.op("V", lambda e: e.reciprocal(col.t[:, 2:3], col.t[:, 2:3]),
                         reads=[col], writes=[col])
                    p.op("V", lambda e, sl=sl: e.tensor_scalar(
                            scr1.t[:, 0:8], sl, col.t[:, 0:1], None, ALU.is_equal),
                         reads=[lgs, col], writes=[scr1])
                    p.op("V", lambda e, sl=sl: e.scalar_tensor_tensor(
                            scr1.t[:, 0:8], scr1.t[:, 0:8], NEG, sl,
                            ALU.mult, ALU.add),
                         reads=[scr1, lgs], writes=[scr1])
                    p.op("V", lambda e: e.reduce_max(col.t[:, 1:2], scr1.t[:, 0:8],
                                                     axis=AX.X),
                         reads=[scr1], writes=[col])
                    p.op("V", lambda e, sl=sl: e.tensor_scalar(
                            scr1.t[:, 0:8], sl, col.t[:, 1:2], None, ALU.is_ge),
                         reads=[lgs, col], writes=[scr1])
                    p.op("V", lambda e, wl=wl: e.scalar_tensor_tensor(
                            wl, wl, col.t[:, 2:3], scr1.t[:, 0:8],
                            ALU.mult, ALU.mult),
                         reads=[wgt, col, scr1], writes=[wgt])
                if tap_w is not None:
                    p.dma("Yd", tap_w.t[hi, :, :], wgt.ap, reads=[wgt], writes=[tap_w])
                for tt in range(4):
                    p.op("T", lambda e, tt=tt: e.transpose(
                            TRP.t[:, 128 * tt:128 * (tt + 1)],
                            wgt.t[:, 8 * tt:8 * (tt + 1)], C["identf"].ap),
                         reads=[wgt, C["identf"]], writes=[TRP])
                p.op("V", lambda e: e.tensor_scalar_mul(wtr.ap, TRP.t[0:8, :],
                                                        SCALING),
                     reads=[TRP], writes=[wtr])
                p.barrier(("T", "V", "S"))

            with ExitStack() as eE:
                PAW = [p.ps(f"p4pa{hi}_{i}", [128, 1024], F32, es=eE)
                       for i in range(2)]
                PUW = p.ps(f"p4pu{hi}", [128, 1024], F32, es=eE)
                PD = p.ps(f"p4pd{hi}", [16, 512], F32, es=eE)
                for ti in range(8):
                    p.dma("Gd", atile[0].t[:].rearrange("p (k c) -> p k c", c=128),
                          I["a_all"].t[:, 128 * ti:128 * (ti + 1)]
                          .rearrange("(k p) c -> p k c", p=128),
                          writes=[atile[0]])
                    av = atile[0].t[:].rearrange("p (k c) -> p k c", c=128)
                    PAb = PAW[ti % 2]
                    p.mm_group(PAb, [(PAb.t[:, 0:512], av[:, k, :], x1h[k].ap)
                                     for k in range(NDC)], reads=[atile[0]] + x1h)
                    base_slot = 0 if ti < 4 else 8
                    uu = ti % 4
                    for half_e, pbase in ((0, 0), (1, 64)):
                        slot = base_slot + 2 * uu + half_e
                        for dbase in (0, 32):
                            p.op("S", lambda e, slot=slot, pbase=pbase, PAb=PAb,
                                 dbase=dbase:
                                 e.activation(
                                     t_sb.t[dbase:dbase + 16,
                                            512 * slot:512 * (slot + 1)],
                                     PAb.t[pbase:pbase + 16, 0:512], AF.Copy,
                                     scale=SCALING),
                                 reads=[PAb], writes=[t_sb])

                with ExitStack() as eB:
                    wgc = [p.sb(f"p4wg{hi}_{i}", [128, NDC * 128], BF16, es=eB)
                           for i in range(2)]
                    wuc = [p.sb(f"p4wu{hi}_{i}", [128, NDC * 128], BF16, es=eB)
                           for i in range(2)]
                    def ldg(fs, b):
                        for nm, dst in (("wg", wgc[b]), ("wu", wuc[b])):
                            p.dma("Gd", dst.t[:].rearrange("p (k c) -> p k c", c=128),
                                  I[nm].t[:, 128 * fs:128 * (fs + 1)]
                                  .rearrange("(k p) c -> p k c", p=128), writes=[dst])
                    ldg(0, 0)
                    for fs in range(NF4):
                        if fs + 1 < NF4:
                            ldg(fs + 1, (fs + 1) % 2)
                        gv = wgc[fs % 2].t[:].rearrange("p (k c) -> p k c", c=128)
                        uv = wuc[fs % 2].t[:].rearrange("p (k c) -> p k c", c=128)
                        PAb = PAW[fs % 2]
                        p.mm_group(PAb, [(PAb.t[:, 0:512], gv[:, k, :], x1h[k].ap)
                                         for k in range(NDC)], reads=[wgc[fs % 2]] + x1h)
                        p.op("S", lambda e, fs=fs, PAb=PAb: e.copy(
                                gb_ap(fs), PAb.t[:, 0:512]),
                             reads=[PAb], writes=[gbp[fs // 2]])
                        p.mm_group(PUW, [(PUW.t[:, 0:512], uv[:, k, :], x1h[k].ap)
                                         for k in range(NDC)], reads=[wuc[fs % 2]] + x1h)
                        p.op("S", lambda e, fs=fs: e.copy(ub_ap(fs), PUW.t[:, 0:512]),
                             reads=[PUW], writes=[ubp[fs // 2]])
                    p.barrier(("T", "V", "S", "Gd"))

                def ldcombo(ex, b):
                    p.dma("Gd", bgc.t[0:16, :], I["bg"].t[ex, :, :], writes=[bgc])
                    p.dma("Gd", bgc.t[32:48, :], I["bg"].t[ex, :, :], writes=[bgc])
                    p.dma("Gd", buc.t[0:16, :], I["bu"].t[ex, :, :], writes=[buc])
                    p.dma("Gd", buc.t[32:48, :], I["bu"].t[ex, :, :], writes=[buc])
                    p.dma("Gd", adb[b].t[:].rearrange("p (f r) -> p f r", r=R),
                          I["ad"].t[ex, :, :].rearrange("(f p) r -> p f r", p=128),
                          writes=[adb[b]])
                NP4 = NF4 // 2
                with ExitStack() as eX:
                    s1 = [p.sb(f"p4s1{hi}_{i}", [128, 1024], BF16, es=eX)
                          for i in range(2)]
                    s2 = [p.sb(f"p4s2{hi}_{i}", [128, 1024], BF16, es=eX)
                          for i in range(2)]
                    s3 = [p.sb(f"p4s3{hi}_{i}", [128, 1024], BF16, es=eX)
                          for i in range(2)]
                    hebs = [p.sb(f"p4he{hi}_{i}", [128, 1024], BF16, es=eX)
                            for i in range(2)]
                    bbscr = p.sb(f"p4bbs{hi}", [128, 1024], BF16, es=eX)
                    for ex in range(NEXP):
                        ldcombo(ex, 0)
                        ab = adb[0].t[:].rearrange("p (f r) -> p f r", r=R)
                        def tgv(b):
                            return t_sb.t[b:b + 16, 512 * ex:512 * (ex + 1)]
                        def tuv(b):
                            return t_sb.t[b:b + 16,
                                          512 * (8 + ex):512 * (9 + ex)]

                        def mms_g(j):
                            par = j % 2
                            for h2 in range(2):
                                fs = 2 * j + h2
                                b = 32 * h2
                                p.op("T", lambda e, h2=h2, fs=fs, b=b, par=par:
                                     e.matmul(PAW[par].t[:, 512 * h2:512 * (h2 + 1)],
                                              bgc.t[b:b + 16, 128 * fs:128 * (fs + 1)],
                                              tgv(b), start=True, stop=True,
                                              tile_position=(b, 0)),
                                     reads=[bgc, t_sb], writes=[PAW[par]])

                        def mms_u(j):
                            for h2 in range(2):
                                fs = 2 * j + h2
                                b = 32 * h2
                                p.op("T", lambda e, h2=h2, fs=fs, b=b:
                                     e.matmul(PUW.t[:, 512 * h2:512 * (h2 + 1)],
                                              buc.t[b:b + 16, 128 * fs:128 * (fs + 1)],
                                              tuv(b), start=True, stop=True,
                                              tile_position=(b, 0)),
                                     reads=[buc, t_sb], writes=[PUW])

                        def elem(j):
                            par = j % 2
                            p.op("V", lambda e, j=j, par=par: e.tensor_tensor(
                                    s1[par].ap, gbp[j].ap, PAW[par].ap, ALU.add),
                                 reads=[gbp[j], PAW[par]], writes=[s1[par]])
                            p.op("S", lambda e, par=par: e.activation(
                                    s2[par].ap, s1[par].ap, AF.Silu),
                                 reads=[s1[par]], writes=[s2[par]])
                            p.op("V", lambda e, j=j, par=par: e.tensor_tensor(
                                    s3[par].ap, ubp[j].ap, PUW.ap, ALU.add),
                                 reads=[ubp[j], PUW], writes=[s3[par]])
                            p.op("V", lambda e, par=par: e.tensor_tensor(
                                    hebs[par].ap, s2[par].ap, s3[par].ap, ALU.mult),
                                 reads=[s2[par], s3[par]], writes=[hebs[par]])
                            for h2 in range(2):
                                fs = 2 * j + h2
                                p.mm(PD.ap, ab[:, fs, :],
                                     hebs[par].t[:, 512 * h2:512 * (h2 + 1)], PD,
                                     reads=[adb[0], hebs[par]],
                                     start=(fs == 0), stop=(fs == NF4 - 1))

                        def bb_make(j):
                            p.op("S", lambda e, j=j: e.activation(
                                    bbscr.ap, gbp[j].ap, AF.Silu),
                                 reads=[gbp[j]], writes=[bbscr])
                            p.op("V", lambda e, j=j: e.tensor_tensor(
                                    gbp[j].ap, bbscr.ap, ubp[j].ap, ALU.mult),
                                 reads=[bbscr, ubp[j]], writes=[gbp[j]])
                        mms_g(0)
                        mms_u(0)
                        for j in range(1, NP4):
                            mms_g(j)
                            elem(j - 1)
                            mms_u(j)
                            if ex == NEXP - 1 and j >= 2:
                                bb_make(j - 2)
                        elem(NP4 - 1)
                        if ex == NEXP - 1:
                            bb_make(NP4 - 2)
                            bb_make(NP4 - 1)
                        for tt in range(4):
                            p.dma("Yd", wrows.t[:, 128 * tt:128 * (tt + 1)],
                                  wtr.t[ex:ex + 1, 128 * tt:128 * (tt + 1)],
                                  reads=[wtr], writes=[wrows])
                        p.mm(PUW.t[0:16, 0:512], C["ones"].t[0:1, 0:16],
                             wrows.ap, PUW, reads=[C["ones"], wrows])
                        p.op("V", lambda e: e.tensor_copy(wsb.ap, PUW.t[0:16, 0:512]),
                             reads=[PUW], writes=[wsb])
                        p.op("V", lambda e: e.tensor_tensor(dtt.ap, PD.ap, wsb.ap,
                                                            ALU.mult),
                             reads=[PD, wsb], writes=[dtt])
                        p.dma("Yd", dT_all.t[16 * ex:16 * (ex + 1), :], dtt.ap,
                              reads=[dtt], writes=[dT_all])

                p.barrier(("T", "V", "S", "G"))

            with ExitStack() as eF:
                PF = [p.ps(f"p4pf{hi}_{i}", [128, 512], F32, es=eF) for i in range(2)]
                LR2 = [p.ps(f"p4lr{hi}_{i}", [1, 512], F32, es=eF) for i in range(2)]
                def ldwd(m, b):
                    p.dma("Gd", wdc[b].t[:].rearrange("p (k c) -> p k c", c=128),
                          I["wd"].t[2816 * b:2816 * (b + 1), 128 * m:128 * (m + 1)]
                          .rearrange("(k p) c -> p k c", p=128), writes=[wdc[b]])
                ldwd(0, 0)
                ldwd(0, 1)
                p.dma("Yd", xfc[0].ap, SC["x1f"].t[0, :, t0:t0 + 512],
                      reads=[SC["x1f"]], writes=[xfc[0]])
                for m in range(NDC):
                    if m + 1 < NDC:
                        p.dma("Yd", xfc[(m + 1) % 2].ap,
                              SC["x1f"].t[m + 1, :, t0:t0 + 512],
                              reads=[SC["x1f"]], writes=[xfc[(m + 1) % 2]])
                    dvA = wdc[0].t[:].rearrange("p (k c) -> p k c", c=128)
                    dvB = wdc[1].t[:].rearrange("p (k c) -> p k c", c=128)
                    PFb = PF[m % 2]
                    p.mm_group(PFb, [(PFb.ap, dvA[:, k, :], gb_ap(k))
                                     for k in range(22)],
                               reads=[wdc[0]] + gbp[0:11], start=True, stop=False)
                    if m + 1 < NDC:
                        ldwd(m + 1, 0)
                    mmsB = [(PFb.ap, dvB[:, k, :], gb_ap(22 + k))
                            for k in range(22)]
                    mmsB.append((PFb.ap, bdb.t[:, 128 * m:128 * (m + 1)],
                                 dT_all.ap))
                    p.mm_group(PFb, mmsB, reads=[wdc[1], bdb, dT_all] + gbp[11:],
                               start=False, stop=True)
                    if m + 1 < NDC:
                        ldwd(m + 1, 1)
                    xb = xfc[m % 2]
                    p.op("V", lambda e, xb=xb, PFb=PFb: e.tensor_tensor(
                            r2c.ap, xb.ap, PFb.ap, ALU.add),
                         reads=[xb, PFb], writes=[r2c])
                    p.mm(LR2[0].ap, C["ones"].t[:, 0:1], r2c.ap, LR2[0],
                         reads=[C["ones"], r2c], start=(m == 0), stop=(m == NDC - 1))
                    p.op("S", lambda e: e.activation(scr1.ap, r2c.ap, AF.Square),
                         reads=[r2c], writes=[scr1])
                    p.mm(LR2[1].ap, C["ones"].t[:, 0:1], scr1.ap, LR2[1],
                         reads=[C["ones"], scr1], start=(m == 0), stop=(m == NDC - 1))
                    p.dma("Yd", SC["r2"].t[m, :, t0:t0 + 512], r2c.ap,
                          reads=[r2c], writes=[SC["r2"]])
                    if tap_r2 is not None:
                        p.dma("Yd", tap_r2.t[m, :, t0:t0 + 512], r2c.ap,
                              reads=[r2c], writes=[tap_r2])
                p.op("V", lambda e, t0=t0: e.tensor_copy(
                        ln2s.t[:, t0:t0 + 512], LR2[0].ap),
                     reads=[LR2[0]], writes=[ln2s])
                p.op("V", lambda e, t0=t0: e.tensor_copy(
                        ln2s.t[:, T + t0:T + t0 + 512], LR2[1].ap),
                     reads=[LR2[1]], writes=[ln2s])
                p.barrier(("T", "V", "S"))
        p.barrier()



def _p5(p, nc, I, C, SC, ln2s, out, tap):
    with ExitStack() as e5:
        rows = p.sb("p5rows", [1, 3 * T], F32, es=e5)
        MR = rows.t[:, 0:T]; VR = rows.t[:, T:2*T]; RS = rows.t[:, 2*T:3*T]
        mb = p.sb("p5mb", [128, T], F32, es=e5)
        bco = p.sb("p5bco", [128, T], F32, es=e5)
        rc = [p.sb(f"p5rc{i}", [128, T], F32, es=e5) for i in range(2)]
        oc = p.sb("p5oc", [128, T], F32, es=e5)
        scr = p.sb("p5scr", [128, T], F32, es=e5)
        BC = p.ps("p5bc", [128, 512], F32, es=e5)

        p.op("V", lambda e: e.tensor_scalar(MR, ln2s.t[:, 0:T], 1.0 / D, None,
                                            ALU.mult),
             reads=[ln2s], writes=[rows])
        p.op("V", lambda e: e.tensor_scalar(VR, ln2s.t[:, T:2*T], 1.0 / D, None,
                                            ALU.mult),
             reads=[ln2s], writes=[rows])
        p.op("V", lambda e: e.tensor_tensor(RS, MR, MR, ALU.mult),
             reads=[rows], writes=[rows])
        p.op("V", lambda e: e.tensor_tensor(VR, VR, RS, ALU.subtract),
             reads=[rows], writes=[rows])
        p.op("S", lambda e: e.activation(RS, VR, AF.Ln, bias=C["epsb"].t[0:1, 1:2]),
             reads=[rows, C["epsb"]], writes=[rows])
        p.op("S", lambda e: e.activation(RS, RS, AF.Exp, scale=-0.5),
             reads=[rows], writes=[rows])
        for src_sl, dst in ((MR, mb), (RS, bco)):
            for th in range(2):
                p.mm(BC.ap, C["ones"].t[0:1, :], src_sl[:, 512*th:512*(th+1)],
                     BC, reads=[C["ones"], rows])
                p.op("V", lambda e, th=th, dst=dst: e.tensor_copy(
                        dst.t[:, 512*th:512*(th+1)], BC.ap),
                     reads=[BC], writes=[dst])
        for m in range(NDC):
            rb = rc[m % 2]
            p.dma("Yd", rb.ap, SC["r2"].t[m, :, :], reads=[SC["r2"]], writes=[rb])
            p.op("V", lambda e, rb=rb: e.tensor_tensor(scr.ap, rb.ap, mb.ap,
                                                       ALU.subtract),
                 reads=[rb, mb], writes=[scr])
            p.op("V", lambda e: e.tensor_tensor(oc.ap, scr.ap, bco.ap, ALU.mult),
                 reads=[scr, bco], writes=[oc])
            p.dma("Yd", out.t[m, :, :], oc.ap, reads=[oc], writes=[out])
        p.barrier()



def kernel(**inputs):
    shared, per_core = prep_inputs(inputs)
    nc, _ = build(phases=("P0", "P1", "P2", "P3", "P4", "P5"))
    in_maps = [dict(shared, **pc) for pc in per_core]
    res = run_bass_kernel_spmd(nc, in_maps, core_ids=list(range(NCORES)))
    outs = [np.asarray(res.results[c]["out"]).transpose(2, 0, 1).reshape(T, D)
            for c in range(NCORES)]
    return np.concatenate(outs, axis=0).reshape(1, S, D).astype(np.float32)


# version 17
# speedup vs baseline: 1.0541x; 1.0137x over previous
import numpy as np
import ml_dtypes
from contextlib import ExitStack

import concourse.bass as bass
import concourse.mybir as mybir
from concourse.bass_utils import run_bass_kernel_spmd

F32 = mybir.dt.float32
BF16 = mybir.dt.bfloat16
AF = mybir.ActivationFunctionType
ALU = mybir.AluOpType
AX = mybir.AxisListType
BFD = ml_dtypes.bfloat16

NCORES = 8
S = 8192
D = 2048
T = 1024
HALO = 512
TH = T + HALO
NH = 16
DH = 128
NDC = D // 128
W_SWA = 512
W_OM = 64
DECAY = 0.9
GGH = 64
MEMF = 8192
FFH = 5461
FFHP = 5504
DFF = 5632
NEXP = 8
R = 16
SCALING = 2.0
EPS_RMS = 1e-6
EPS_LN = 1e-5
SCALE = DH ** -0.5
NEG = -1e30


ENGINE_OF = {"T": "tensor", "V": "vector", "S": "scalar", "Sd": "scalar",
             "G": "gpsimd", "Gd": "gpsimd", "Yd": "sync"}

NSLOT = 8

class Q:
    def __init__(self, name, step):
        self.name = name
        self.step = step
        self.is_dma = name.endswith("d")
        if self.is_dma:
            self.slot_cnt = [0] * NSLOT
            self.issue = 0
        else:
            self.cnt = 0

class Buf:
    __slots__ = ("t", "w", "readers", "name")
    def __init__(self, t, name=""):
        self.t = t
        self.name = name
        self.w = None
        self.readers = {}
    def __getitem__(self, idx):
        return self.t[idx]
    @property
    def ap(self):
        return self.t[:]

class Prog:
    def __init__(self, nc, es):
        self.nc = nc
        self.es = es
        self.queues = {n: Q(n, 16 if n.endswith("d") else 1)
                       for n in ("T", "V", "S", "G", "Yd", "Gd", "Sd")}
        self.sems = {}
        for n, q in self.queues.items():
            if q.is_dma:
                for s in range(NSLOT):
                    self.sems[f"{n}{s}"] = es.enter_context(nc.semaphore(f"sem_{n}{s}"))
            else:
                self.sems[n] = es.enter_context(nc.semaphore(f"sem_{n}"))
        self.streams = {e: [] for e in ("tensor", "vector", "scalar", "gpsimd", "sync")}
        self.hw = {e: {} for e in self.streams}

    def sb(self, name, shape, dt, es=None):
        return Buf((es or self.es).enter_context(
            self.nc.sbuf_tensor(name, list(shape), dt)), name)

    def ps(self, name, shape, dt=F32, es=None):
        return Buf((es or self.es).enter_context(
            self.nc.psum_tensor(name, list(shape), dt)), name)

    def dram(self, name, shape, dt):
        return Buf(self.nc.dram_tensor(name, list(shape), dt), name)

    def _edges(self, qname, reads, writes, skip_self_T=True):
        eng = ENGINE_OF[qname]
        need = {}
        def upd(e):
            if e is None:
                return
            sk, c = e
            if sk == "T" and qname == "T" and skip_self_T:
                return
            if need.get(sk, -1) < c:
                need[sk] = c
        for b in reads:
            upd(b.w)
        for b in writes:
            upd(b.w)
            for sk, c in b.readers.items():
                upd((sk, c))
        hw = self.hw[eng]
        out = []
        for sk in sorted(need):
            c = need[sk]
            if c <= 0 or hw.get(sk, 0) >= c:
                continue
            out.append((sk, c))
            hw[sk] = c
        return out

    def _push(self, qname, waits, emit, incsem):
        self.streams[ENGINE_OF[qname]].append((waits, emit, incsem))

    def _record(self, semkey, val, reads, writes):
        for b in writes:
            b.w = (semkey, val)
            b.readers = {}
        for b in reads:
            if b.readers.get(semkey, -1) < val:
                b.readers[semkey] = val

    def op(self, qname, emit, reads=(), writes=()):
        q = self.queues[qname]
        assert not q.is_dma
        waits = self._edges(qname, reads, writes)
        q.cnt += 1
        self._push(qname, waits, emit, (qname, 1))
        self._record(qname, q.cnt, reads, writes)

    def dma(self, qname, dst_ap, src_ap, reads=(), writes=()):
        q = self.queues[qname]
        assert q.is_dma
        slot = q.issue % NSLOT
        q.issue += 1
        semkey = f"{qname}{slot}"
        waits = []
        prior = q.slot_cnt[slot] * 16
        eng = ENGINE_OF[qname]
        if prior > 0 and self.hw[eng].get(semkey, 0) < prior:
            waits.append((semkey, prior))
            self.hw[eng][semkey] = prior
        waits += self._edges(qname, reads, writes)
        q.slot_cnt[slot] += 1
        self._push(qname, waits,
                   lambda e: e.dma_start(dst_ap, src_ap), (semkey, 16))
        self._record(semkey, q.slot_cnt[slot] * 16, reads, writes)

    def mm(self, out_ap, lhsT_ap, rhs_ap, psum_buf, reads=(), start=True, stop=True):
        self.op("T", lambda e: e.matmul(out_ap, lhsT_ap, rhs_ap, start=start, stop=stop),
                reads=list(reads), writes=[psum_buf])

    def mm_group(self, psum_buf, mms, reads=(), start=True, stop=True):
        waits = self._edges("T", list(reads), [psum_buf])
        q = self.queues["T"]
        q.cnt += 1
        n = len(mms)
        def emit(e, mms=tuple(mms), n=n, start=start, stop=stop):
            ins = None
            for i, (o, l, r) in enumerate(mms):
                ins = e.matmul(o, l, r, start=(start and i == 0),
                               stop=(stop and i == n - 1))
            return ins
        self._push("T", waits, emit, ("T", 1))
        self._record("T", q.cnt, list(reads), [psum_buf])

    def _cur_semvals(self):
        vals = {}
        for n, q in self.queues.items():
            if q.is_dma:
                for s in range(NSLOT):
                    if q.slot_cnt[s] > 0:
                        vals[f"{n}{s}"] = q.slot_cnt[s] * 16
            else:
                if q.cnt > 0:
                    vals[n] = q.cnt
        return vals

    def wait_all(self, qname):
        eng = ENGINE_OF[qname]
        hw = self.hw[eng]
        waits = []
        own = {qname} if not self.queues[qname].is_dma else \
            {f"{qname}{s}" for s in range(NSLOT)}
        for sk, c in sorted(self._cur_semvals().items()):
            if sk in own:
                continue
            if hw.get(sk, 0) < c:
                waits.append((sk, c))
                hw[sk] = c
        if waits:
            self._push(qname, waits, None, None)

    def barrier(self, qnames=("T", "V", "S", "Yd", "Gd", "Sd")):
        for qa in qnames:
            self.wait_all(qa)

    def finalize(self, block):
        def mk(engname):
            items = self.streams[engname]
            def body(eng):
                for waits, emit, incsem in items:
                    for sk, c in waits:
                        eng.wait_ge(self.sems[sk], c)
                    if emit is not None:
                        ins = emit(eng)
                        if incsem is not None:
                            ins.then_inc(self.sems[incsem[0]], incsem[1])
            return body
        block.tensor(mk("tensor"))
        block.vector(mk("vector"))
        block.scalar(mk("scalar"))
        block.gpsimd(mk("gpsimd"))
        block.sync(mk("sync"))



def _rope_tables():
    inv_freq = 1.0 / (10000.0 ** (np.arange(0, DH, 2, dtype=np.float64) / DH))
    t = np.arange(-HALO, S, dtype=np.float64)
    freqs = t[:, None] * inv_freq[None, :]
    cos = np.cos(freqs)
    sin = np.sin(freqs)
    cosT = np.concatenate([cos.T, cos.T], axis=0)
    sinT = np.concatenate([-sin.T, sin.T], axis=0)
    return cosT, sinT

def _masks(core):
    i = np.arange(128)[:, None]
    j = np.arange(128)[None, :]
    m_far = np.where(j > i, 0.0, NEG)
    m_causal = np.where(j <= i, 0.0, NEG)
    rel4 = i - j
    rel3 = i + 128 - j
    m_om4 = np.where((rel4 >= 0) & (rel4 < W_OM), 0.0, NEG)
    m_om3 = np.where((rel3 >= 0) & (rel3 < W_OM), 0.0, NEG)
    d_om4 = np.where((rel4 >= 0) & (rel4 < W_OM),
                     DECAY ** np.clip(rel4, 0, 255), 0.0)
    d_om3 = np.where((rel3 >= 0) & (rel3 < W_OM),
                     DECAY ** np.clip(rel3, 0, 255), 0.0)
    swa_cat = np.zeros((8, 128, 640))
    om_cat = np.zeros((8, 128, 256))
    for qi in range(8):
        for b in range(5):
            blk = np.zeros((128, 128))
            if b == 0:
                blk = blk + m_far
            elif b == 4:
                blk = blk + m_causal
            if core == 0 and (qi + b) < 4:
                blk = blk + NEG
            swa_cat[qi, :, 128*b:128*(b+1)] = blk
        om_cat[qi, :, 0:128] = m_om3 + (NEG if (core == 0 and qi == 0) else 0.0)
        om_cat[qi, :, 128:256] = m_om4
    np.clip(swa_cat, -3e38, 0, out=swa_cat)
    np.clip(om_cat, -3e38, 0, out=om_cat)
    d_cat = np.concatenate([d_om3, d_om4], axis=1)
    return dict(
        swa_cat=np.ascontiguousarray(
            swa_cat.transpose(1, 0, 2).reshape(128, 8 * 640)).astype(BFD),
        om_cat=np.ascontiguousarray(
            om_cat.transpose(1, 0, 2).reshape(128, 8 * 256)).astype(BFD),
        d_cat=d_cat.astype(np.float32))

def prep_inputs(inp):
    x = np.asarray(inp["L0_attn_x"])[0]
    shared = {}
    wqkv = np.asarray(inp["L0_attn_w_qkv"])
    cols = []
    for h in range(NH):
        cols.append(wqkv[:, h*DH:(h+1)*DH])
        cols.append(wqkv[:, D + h*DH:D + (h+1)*DH])
        cols.append(wqkv[:, 2*D + h*DH:2*D + (h+1)*DH])
    shared["wqkv_p"] = np.ascontiguousarray(np.concatenate(cols, axis=1))
    shared["qscale"] = np.ascontiguousarray(np.asarray(inp["L0_attn_q_scale"]).reshape(DH, 1))
    shared["kscale"] = np.ascontiguousarray(np.asarray(inp["L0_attn_k_scale"]).reshape(DH, 1))
    shared["n1w"] = np.ascontiguousarray(np.asarray(inp["L0_attn_norm1_w"]).reshape(NDC, 128).T)
    shared["n2w"] = np.ascontiguousarray(np.asarray(inp["L0_attn_norm2_w"]).reshape(NDC, 128).T)
    shared["gg1"] = np.asarray(inp["L0_attn_gg_w1"])
    shared["gg2"] = np.asarray(inp["L0_attn_gg_w2"])
    shared["wo"] = np.asarray(inp["L0_attn_w_o"])
    shared["mw1"] = np.asarray(inp["L0_attn_mem_w1"])
    shared["mw2"] = np.asarray(inp["L0_attn_mem_w2"])
    z = np.zeros((D, FFHP - FFH), np.float32)
    shared["fw1"] = np.ascontiguousarray(np.concatenate([np.asarray(inp["L0_attn_ffn_w1"]), z], axis=1))
    shared["fw3"] = np.ascontiguousarray(np.concatenate([np.asarray(inp["L0_attn_ffn_w3"]), z], axis=1))
    shared["fw2"] = np.ascontiguousarray(np.concatenate(
        [np.asarray(inp["L0_attn_ffn_w2"]), np.zeros((FFHP - FFH, D), np.float32)], axis=0))
    shared["wg"] = np.ascontiguousarray(np.asarray(inp["L0_ffn_Wg"]).T)
    shared["wu"] = np.ascontiguousarray(np.asarray(inp["L0_ffn_Wu"]).T)
    shared["wd"] = np.ascontiguousarray(np.asarray(inp["L0_ffn_Wd"]).T)
    shared["wr"] = np.ascontiguousarray(np.asarray(inp["L0_ffn_Wr"]).T)
    Ag = np.asarray(inp["L0_ffn_Ag"])
    Au = np.asarray(inp["L0_ffn_Au"])
    a_all = np.zeros((D, 8 * 128), np.float32)
    for t in range(4):
        a_all[:, 128*t + 0:128*t + 16] = Ag[2*t].T
        a_all[:, 128*t + 64:128*t + 80] = Ag[2*t + 1].T
        base = 512 + 128*t
        a_all[:, base + 0:base + 16] = Au[2*t].T
        a_all[:, base + 64:base + 80] = Au[2*t + 1].T
    shared["a_all"] = a_all
    shared["bg"] = np.ascontiguousarray(np.asarray(inp["L0_ffn_Bg"]).transpose(0, 2, 1))
    shared["bu"] = np.ascontiguousarray(np.asarray(inp["L0_ffn_Bu"]).transpose(0, 2, 1))
    shared["ad"] = np.ascontiguousarray(np.asarray(inp["L0_ffn_Ad"]).transpose(0, 2, 1))
    shared["bd"] = np.ascontiguousarray(
        np.asarray(inp["L0_ffn_Bd"]).transpose(0, 2, 1).reshape(NEXP * R, D))
    shared["ident"] = np.eye(128, dtype=BFD)
    shared["identf"] = np.eye(128, dtype=np.float32)
    shared["ones"] = np.ones((128, 128), np.float32)
    shared["onesb"] = np.ones((128, 128), dtype=BFD)
    epsb = np.zeros((128, 2), np.float32); epsb[:, 0] = EPS_RMS; epsb[:, 1] = EPS_LN
    shared['epsb'] = epsb

    cosT, sinT = _rope_tables()
    per_core = []
    for c in range(NCORES):
        gs = c * T
        m = {}
        lo = gs - HALO
        if lo < 0:
            xs = np.concatenate([np.zeros((HALO, D), np.float32),
                                 x[0:gs + T]], axis=0)
        else:
            xs = x[lo:gs + T]
        m["x_f"] = np.ascontiguousarray(xs.T)
        m["cosq"] = np.ascontiguousarray(cosT[:, HALO + gs: HALO + gs + T] * SCALE).astype(BFD)
        m["sinq"] = np.ascontiguousarray(sinT[:, HALO + gs: HALO + gs + T] * SCALE).astype(BFD)
        m["cosk"] = np.ascontiguousarray(cosT[:, gs: gs + TH]).astype(BFD)
        m["sink"] = np.ascontiguousarray(sinT[:, gs: gs + TH]).astype(BFD)
        m.update(_masks(c))
        per_core.append(m)
    return shared, per_core



IN_SPECS = [
    ("x_f", [D, TH], F32), ("wqkv_p", [D, 3 * D], F32),
    ("qscale", [DH, 1], F32), ("kscale", [DH, 1], F32),
    ("n1w", [128, NDC], F32), ("n2w", [128, NDC], F32),
    ("gg1", [D, GGH], F32), ("gg2", [GGH, 1], F32),
    ("wo", [D, D], F32), ("mw1", [D, MEMF], F32), ("mw2", [MEMF, D], F32),
    ("fw1", [D, FFHP], F32), ("fw3", [D, FFHP], F32), ("fw2", [FFHP, D], F32),
    ("wg", [D, DFF], F32), ("wu", [D, DFF], F32), ("wd", [DFF, D], F32),
    ("wr", [D, NEXP], F32), ("a_all", [D, 1024], F32),
    ("bg", [NEXP, R, DFF], F32), ("bu", [NEXP, R, DFF], F32),
    ("ad", [NEXP, DFF, R], F32), ("bd", [NEXP * R, D], F32),
    ("ident", [128, 128], BF16), ("identf", [128, 128], F32),
    ("onesb", [128, 128], BF16),
    ("ones", [128, 128], F32),
    ("swa_cat", [128, 8 * 640], BF16), ("om_cat", [128, 8 * 256], BF16),
    ("d_cat", [128, 256], F32),
    ("cosq", [128, T], BF16), ("sinq", [128, T], BF16),
    ("cosk", [128, TH], BF16), ("sink", [128, TH], BF16),
    ("epsb", [128, 2], F32),
]

CONST_NAMES = ["ident", "identf", "ones", "onesb", "n1w", "n2w", "qscale", "kscale", "epsb"]
P1_CONST_NAMES = ["swa_cat", "om_cat", "d_cat",
                  "cosq", "sinq", "cosk", "sink"]

def build(phases=("P0",), debug_taps=()):
    nc = bass.Bass("TRN2", target_bir_lowering=False)
    es = ExitStack()
    p = Prog(nc, es)
    dbg = {}

    I = {nm: Buf(nc.dram_tensor(nm, list(shape), dt, kind="ExternalInput"), nm)
         for nm, shape, dt in IN_SPECS}
    out = Buf(nc.dram_tensor("out", [NDC, 128, T], F32, kind="ExternalOutput"), "out")

    SC = {
        "attn": p.dram("sc_attn", [NH, 128, T], BF16),
        "xa": p.dram("sc_xa", [NDC, 128, T], F32),
        "r": p.dram("sc_r", [NDC, 128, T], F32),
        "x1b": p.dram("sc_x1b", [NDC, 128, T], BF16),
        "x1f": p.dram("sc_x1f", [NDC, 128, T], F32),
        "r2": p.dram("sc_r2", [NDC, 128, T], F32),
    }

    def tap(name, shape, dt=F32):
        if name in debug_taps:
            b = Buf(nc.dram_tensor("dbg_" + name, list(shape), dt,
                                   kind="ExternalOutput"), "dbg_" + name)
            dbg[name] = b
            return b
        return None

    with es:
        C = {}
        for nm in CONST_NAMES:
            shape = dict((n, s) for n, s, _ in IN_SPECS)[nm]
            dt = dict((n, d) for n, s, d in IN_SPECS)[nm]
            C[nm] = p.sb("c_" + nm, shape, dt)
            p.dma("Yd", C[nm].ap, I[nm].ap, writes=[C[nm]])

        es_p03 = es.enter_context(ExitStack())
        rms2ss = p.sb("rms2ss", [1, T], F32, es=es_p03)

        with ExitStack() as es_p02:
            qmem = [p.sb(f"qmem{h}", [128, T], BF16, es=es_p02) for h in range(NH)]
            hbuf = [p.sb(f"h{i}", [128, TH], BF16, es=es_p02) for i in range(NDC)]
            gammaT = p.sb("gammaT", [128, 12], F32, es=es_p02)

            if "P0" in phases:
                _p0(p, nc, I, C, hbuf, gammaT, tap)
            p1sel = [ph for ph in phases if ph.startswith("P1")]
            if p1sel:
                parts = p1sel[0].split(":")
                upto = parts[1] if len(parts) > 1 else "full"
                nh = int(parts[2]) if len(parts) > 2 else NH
                _p1(p, nc, I, C, hbuf, gammaT, qmem, SC, tap,
                    heads=range(nh), upto=upto)

            if "P2" in phases:
                _p2(p, nc, I, C, hbuf, qmem, SC, rms2ss, tap)

        if "P3" in phases:
            _p3(p, nc, I, C, SC, rms2ss, tap)
        es_p03.close()

        ln2s = p.sb("ln2s", [1, 2 * T], F32)
        if "P4" in phases:
            _p4(p, nc, I, C, SC, ln2s, tap)
        if "P5" in phases:
            _p5(p, nc, I, C, SC, ln2s, out, tap)

        if "OUT_STUB" in phases:
            with ExitStack() as e2:
                z = p.sb("zstub", [128, D], F32, es=e2)
                p.op("V", lambda e: e.memset(z.ap, 0.0), writes=[z])
                for i in range(T // 128):
                    p.dma("Yd", out.t[i * 128:(i + 1) * 128, :], z.ap,
                          reads=[z], writes=[out])

        p.barrier()

        with nc.Block() as block:
            p.finalize(block)
    return nc, dbg



def _p0(p, nc, I, C, hbuf, gammaT, tap):
    with ExitStack() as e0:
        xs = [p.sb(f"p0x{i}", [128, TH], F32, es=e0) for i in range(2)]
        sqs = [p.sb(f"p0sq{i}", [128, TH], BF16, es=e0) for i in range(2)]
        rstd_row = p.sb("p0rstd", [1, TH], F32, es=e0)
        rstd_sb = p.sb("p0rbs", [128, TH], F32, es=e0)
        gg1b = p.sb("p0gg1", [128, NDC * GGH], BF16, es=e0)
        gg2b = p.sb("p0gg2", [GGH, 1], BF16, es=e0)
        gsil = p.sb("p0gs", [GGH, TH], BF16, es=e0)

        p.dma("Gd", gg1b.t[:].rearrange("p (k c) -> p k c", c=GGH),
              I["gg1"].ap.rearrange("(k p) c -> p k c", p=128), writes=[gg1b])
        p.dma("Gd", gg2b.ap, I["gg2"].ap, writes=[gg2b])

        with ExitStack() as eA:
            ssum = p.ps("p0ss", [1, TH], F32, es=eA)
            rstd_b = p.ps("p0rb", [128, TH], F32, es=eA)
            for i in range(NDC):
                xb, sq = xs[i % 2], sqs[i % 2]
                p.dma("Yd", xb.ap, I["x_f"].t[i * 128:(i + 1) * 128, :], writes=[xb])
                p.op("S", lambda e, xb=xb, sq=sq: e.activation(sq.ap, xb.ap, AF.Square),
                     reads=[xb], writes=[sq])
                for t0 in range(0, TH, 512):
                    p.mm(ssum.t[:, t0:t0 + 512], C["onesb"].t[:, 0:1],
                         sq.t[:, t0:t0 + 512], ssum, reads=[C["onesb"], sq],
                         start=(i == 0), stop=(i == NDC - 1))
            p.op("S", lambda e: e.activation(rstd_row.ap, ssum.ap, AF.Ln,
                                             bias=C['epsb'].t[0:1, 0:1], scale=1.0 / D),
                 reads=[ssum], writes=[rstd_row])
            p.op("S", lambda e: e.activation(rstd_row.ap, rstd_row.ap, AF.Exp,
                                             scale=-0.5),
                 reads=[rstd_row], writes=[rstd_row])
            for t0 in range(0, TH, 512):
                p.mm(rstd_b.t[:, t0:t0 + 512], C["ones"].t[0:1, :],
                     rstd_row.t[:, t0:t0 + 512], rstd_b,
                     reads=[C["ones"], rstd_row], start=True, stop=True)
            p.op("V", lambda e: e.tensor_copy(rstd_sb.ap, rstd_b.ap),
                 reads=[rstd_b], writes=[rstd_sb])
            for i in range(NDC):
                xb = xs[i % 2]
                p.dma("Yd", xb.ap, I["x_f"].t[i * 128:(i + 1) * 128, :], writes=[xb])
                p.op("V", lambda e, i=i, xb=xb: e.scalar_tensor_tensor(
                        hbuf[i].ap, xb.ap, C["n1w"].t[:, i:i + 1], rstd_sb.ap,
                        ALU.mult, ALU.mult),
                     reads=[xb, C["n1w"], rstd_sb], writes=[hbuf[i]])
            p.barrier()
        with ExitStack() as eB:
            gpre = p.ps("p0gp", [GGH, TH], F32, es=eB)
            grow = p.ps("p0gr", [128, 12], F32, es=eB)
            gg1v = gg1b.t[:].rearrange("p (k c) -> p k c", c=GGH)
            for t0 in range(0, TH, 512):
                p.mm_group(gpre,
                           [(gpre.t[:, t0:t0 + 512], gg1v[:, i, :],
                             hbuf[i].t[:, t0:t0 + 512]) for i in range(NDC)],
                           reads=[gg1b] + hbuf)
            p.op("S", lambda e: e.activation(gsil.ap, gpre.ap, AF.Silu),
                 reads=[gpre], writes=[gsil])
            for j in range(12):
                p.mm(grow.t[:, j:j + 1], gsil.t[:, j * 128:(j + 1) * 128], gg2b.ap,
                     grow, reads=[gsil, gg2b], start=True, stop=True)
            p.op("S", lambda e: e.activation(gammaT.ap, grow.ap, AF.Sigmoid),
                 reads=[grow], writes=[gammaT])

            tp = tap("h", [NDC, 128, TH], BF16)
            if tp is not None:
                for i in range(NDC):
                    p.dma("Yd", tp.t[i, :, :], hbuf[i].ap, reads=[hbuf[i]], writes=[tp])
            tp = tap("gammaT", [128, 12])
            if tp is not None:
                p.dma("Yd", tp.ap, gammaT.ap, reads=[gammaT], writes=[tp])
            p.barrier()


def _p1(p, nc, I, C, hbuf, gammaT, qmem, SC, tap, heads=None, upto="full"):
    with ExitStack() as e1:
        ispec = {nm: (s, d) for nm, s, d in IN_SPECS}
        for nm in P1_CONST_NAMES:
            C[nm] = p.sb("c_" + nm, ispec[nm][0], ispec[nm][1], es=e1)
            p.dma("Yd", C[nm].ap, I[nm].ap, writes=[C[nm]])
        wb = [p.sb(f"p1w{i}", [128, NDC * 384], BF16, es=e1) for i in range(2)]
        q_work = p.sb("p1qw", [128, T], F32, es=e1)
        k_work = p.sb("p1kw", [128, TH], F32, es=e1)
        row = p.sb("p1row", [1, 512], F32, es=e1)
        rowb = p.sb("p1rowb", [1, 512], BF16, es=e1)
        q1 = p.sb("p1q1", [128, T], F32, es=e1)
        k1 = p.sb("p1k1", [128, TH], F32, es=e1)
        t1 = p.sb("p1t1", [128, TH], F32, es=e1)
        t2 = p.sb("p1t2", [128, TH], F32, es=e1)
        sq = p.sb("p1sqb", [128, 512], BF16, es=e1)
        qr = t2
        kr = t2
        q2 = p.sb("p1q2", [128, T], BF16, es=e1)
        k2 = p.sb("p1k2", [128, TH], BF16, es=e1)
        vts = [p.sb(f"p1vt{i}", [128, 12 * 128], BF16, es=e1) for i in range(2)]
        kpts = [p.sb(f"p1kpt{i}", [128, 12 * 128], BF16, es=e1) for i in range(2)]
        sms_ = [p.sb(f"p1sm{i}", [128, 640], F32, es=e1) for i in range(2)]
        pns = [p.sb(f"p1pn{i}", [128, 640], BF16, es=e1) for i in range(2)]
        oms = [p.sb(f"p1om{i}", [128, 256], F32, es=e1) for i in range(2)]
        poms = [p.sb(f"p1pom{i}", [128, 256], BF16, es=e1) for i in range(2)]
        pTs = [p.sb(f"p1pT{i}", [128, 896], BF16, es=e1) for i in range(2)]
        attnhs = [p.sb(f"p1ah{i}", [128, T], BF16, es=e1) for i in range(2)]
        mcols = [p.sb(f"p1m{i}", [128, 1], F32, es=e1) for i in range(2)]
        sumcs = [p.sb(f"p1su{i}", [128, 1], F32, es=e1) for i in range(2)]
        osums = [p.sb(f"p1os{i}", [128, 1], F32, es=e1) for i in range(2)]
        recs = [p.sb(f"p1re{i}", [128, 1], F32, es=e1) for i in range(2)]
        orecs = [p.sb(f"p1or{i}", [128, 1], F32, es=e1) for i in range(2)]
        PJ = [p.ps(f"p1pj{i}", [128, 512], F32, es=e1) for i in range(2)]
        PS = [p.ps(f"p1ps{i}", [128, 640], F32, es=e1) for i in range(2)]
        TP = p.ps("p1ptp", [128, 896], BF16, es=e1)
        PO = p.ps("p1po", [128, 512], F32, es=e1)
        ROW = PJ[0]
        BC = PJ[1]

        def wdma(hd, dst):
            src = I["wqkv_p"].t[:, 384 * hd:384 * (hd + 1)].rearrange(
                "(k p) c -> p k c", p=128)
            p.dma("Gd", dst.t[:].rearrange("p (k c) -> p k c", c=384), src,
                  writes=[dst])

        wdma(0, wb[0])
        tap_q2 = tap("q2h0", [128, T], BF16)
        tap_k2 = tap("k2h0", [128, TH], BF16)
        tap_attn = tap("attn", [NH, 128, T], BF16)
        tap_qm = tap("qmemt", [NH, 128, T], BF16)

        def proj_items(hd):
            wcur = wb[hd % 2]
            wv = wcur.t[:].rearrange("p (k c) -> p k c", c=384)
            vt = vts[hd % 2]
            items = []
            pjc = [0]
            def nxt():
                b = PJ[pjc[0] % 2]
                pjc[0] += 1
                return b
            for tt in range(2):
                def it(tt=tt):
                    PJb = nxt()
                    p.mm_group(PJb, [(PJb.ap, wv[:, k, 0:128],
                                      hbuf[k].t[:, HALO + 512 * tt:HALO + 512 * (tt + 1)])
                                     for k in range(NDC)], reads=[wcur] + hbuf)
                    p.op("S", lambda e, tt=tt, PJb=PJb: e.copy(
                            q_work.t[:, 512 * tt:512 * (tt + 1)], PJb.ap),
                         reads=[PJb], writes=[q_work])
                items.append(it)
            for tt in range(3):
                def it(tt=tt):
                    PJb = nxt()
                    p.mm_group(PJb, [(PJb.ap, wv[:, k, 128:256],
                                      hbuf[k].t[:, 512 * tt:512 * (tt + 1)])
                                     for k in range(NDC)], reads=[wcur] + hbuf)
                    p.op("S", lambda e, tt=tt, PJb=PJb: e.copy(
                            k_work.t[:, 512 * tt:512 * (tt + 1)], PJb.ap),
                         reads=[PJb], writes=[k_work])
                items.append(it)
            for j in range(12):
                def it(j=j):
                    PJb = nxt()
                    p.mm_group(PJb, [(PJb.t[:, 0:128],
                                      hbuf[k].t[:, 128 * j:128 * (j + 1)],
                                      wv[:, k, 256:384]) for k in range(NDC)],
                               reads=[wcur] + hbuf)
                    p.op("S", lambda e, j=j, PJb=PJb, vt=vt: e.copy(
                            vt.t[:, 128 * j:128 * (j + 1)], PJb.t[:, 0:128]),
                         reads=[PJb], writes=[vt])
                items.append(it)
            return items

        def proj(hd):
            for it in proj_items(hd):
                it()

        hlist = list(heads if heads is not None else range(NH))
        proj(hlist[0])
        if len(hlist) > 1:
            wdma(hlist[1], wb[hlist[1] % 2])
        for hd in hlist:
            vt = vts[hd % 2]
            kpt = kpts[hd % 2]
            attnh = attnhs[hd % 2]
            nxt_items = []
            if hd + 1 < NH:
                nxt_items = proj_items(hd + 1)
                if hd + 2 < NH:
                    wdma(hd + 2, wb[(hd + 2) % 2])

            def headnorm(work, scale_c, out_b, n):
                for tt in range(n // 512):
                    p.op("S", lambda e, tt=tt: e.activation(
                            sq.ap, work.t[:, 512 * tt:512 * (tt + 1)], AF.Square),
                         reads=[work], writes=[sq])
                    p.mm(ROW.t[0:1, 0:512], C["onesb"].t[:, 0:1],
                         sq.ap, ROW, reads=[C["onesb"], sq])
                    p.op("S", lambda e: e.activation(
                            row.ap, ROW.t[0:1, 0:512],
                            AF.Ln, bias=C["epsb"].t[0:1, 0:1], scale=1.0 / DH),
                         reads=[ROW, C["epsb"]], writes=[row])
                    p.op("S", lambda e: e.activation(rowb.ap, row.ap,
                                                     AF.Exp, scale=-0.5),
                         reads=[row], writes=[rowb])
                    p.mm(BC.t[:, 0:512], C["onesb"].t[0:1, :], rowb.ap,
                         BC, reads=[C["onesb"], rowb])
                    p.op("V", lambda e, tt=tt: e.scalar_tensor_tensor(
                            out_b.t[:, 512 * tt:512 * (tt + 1)],
                            work.t[:, 512 * tt:512 * (tt + 1)],
                            C[scale_c].ap, BC.t[:, 0:512], ALU.mult, ALU.mult),
                         reads=[work, C[scale_c], BC], writes=[out_b])
            if hd == 0:
                tpx = tap("qworkh0", [128, T], F32)
                if tpx is not None:
                    p.dma("Yd", tpx.ap, q_work.ap, reads=[q_work], writes=[tpx])
            if upto == "qwork":
                continue
            headnorm(q_work, "qscale", q1, T)
            if hd == 0:
                tpx = tap("q1h0", [128, T], F32)
                if tpx is not None:
                    p.dma("Yd", tpx.ap, q1.ap, reads=[q1], writes=[tpx])
                tpx = tap("rowh0", [1, T], F32)
                if tpx is not None:
                    p.dma("Yd", tpx.ap, row.t[:, 0:T], reads=[row], writes=[tpx])
            headnorm(k_work, "kscale", k1, TH)

            def rope(src, rot, cos_c, sin_c, out_b, n):
                p.op("V", lambda e: e.tensor_copy(rot.t[0:64, 0:n], src.t[64:128, 0:n]),
                     reads=[src], writes=[rot])
                p.op("V", lambda e: e.tensor_copy(rot.t[64:128, 0:n], src.t[0:64, 0:n]),
                     reads=[src], writes=[rot])
                p.op("V", lambda e: e.tensor_tensor(t1.t[:, 0:n], src.t[:, 0:n],
                                                    C[cos_c].t[:, 0:n], ALU.mult),
                     reads=[src, C[cos_c]], writes=[t1])
                p.op("V", lambda e: e.tensor_tensor(t2.t[:, 0:n], rot.t[:, 0:n],
                                                    C[sin_c].t[:, 0:n], ALU.mult),
                     reads=[rot, C[sin_c]], writes=[t2])
                p.op("V", lambda e: e.tensor_tensor(out_b.t[:, 0:n], t1.t[:, 0:n],
                                                    t2.t[:, 0:n], ALU.add),
                     reads=[t1, t2], writes=[out_b])
            if upto == "norm":
                continue
            for _ in range(6):
                if nxt_items:
                    nxt_items.pop(0)()
            rope(q1, qr, "cosq", "sinq", q2, T)
            rope(k1, kr, "cosk", "sink", k2, TH)

            for j in range(12):
                sl = 128 * (j % 7)
                p.op("T", lambda e, j=j, sl=sl: e.transpose(
                        TP.t[:, sl:sl + 128], k2.t[:, 128 * j:128 * (j + 1)],
                        C["ident"].ap),
                     reads=[k2, C["ident"]], writes=[TP])
                p.op("S", lambda e, j=j, sl=sl, kpt=kpt: e.activation(
                        kpt.t[:, 128 * j:128 * (j + 1)], TP.t[:, sl:sl + 128],
                        AF.Copy, scale=gammaT.t[:, j:j + 1]),
                     reads=[TP, gammaT], writes=[kpt])

            def stA(qi):
                ks = 128 * qi
                par = qi % 2
                PSb = PS[par]
                p.mm(PSb.t[:, 0:512], q2.t[:, ks:ks + 128],
                     k2.t[:, ks:ks + 512], PSb, reads=[q2, k2])
                p.mm(PSb.t[:, 512:640], q2.t[:, ks:ks + 128],
                     k2.t[:, ks + 512:ks + 640], PSb, reads=[q2, k2])

            def stB(qi):
                par = qi % 2
                PSb, smb, pnb, omb, pomb = PS[par], sms_[par], pns[par], oms[par], poms[par]
                mcol, sumc, osum, rec, orec = (mcols[par], sumcs[par], osums[par],
                                               recs[par], orecs[par])
                p.op("V", lambda e, qi=qi, PSb=PSb, smb=smb: e.tensor_tensor(
                        smb.ap, PSb.ap, C["swa_cat"].t[:, 640 * qi:640 * (qi + 1)],
                        ALU.add),
                     reads=[PSb, C["swa_cat"]], writes=[smb])
                p.op("V", lambda e, smb=smb, mcol=mcol: e.reduce_max(
                        mcol.ap, smb.ap, axis=AX.X, negate=True),
                     reads=[smb], writes=[mcol])
                p.op("S", lambda e, smb=smb, pnb=pnb, mcol=mcol, sumc=sumc:
                     e.activation(pnb.ap, smb.ap, AF.Exp, bias=mcol.ap,
                                  accum_out=sumc.ap),
                     reads=[smb, mcol], writes=[pnb, sumc])
                p.op("V", lambda e, rec=rec, sumc=sumc: e.reciprocal(rec.ap, sumc.ap),
                     reads=[sumc], writes=[rec])
                p.op("V", lambda e, pnb=pnb, rec=rec: e.tensor_scalar(
                        pnb.ap, pnb.ap, rec.ap, None, ALU.mult),
                     reads=[pnb, rec], writes=[pnb])
                p.op("V", lambda e, qi=qi, smb=smb, omb=omb: e.tensor_tensor(
                        omb.ap, smb.t[:, 384:640],
                        C["om_cat"].t[:, 256 * qi:256 * (qi + 1)], ALU.add),
                     reads=[smb, C["om_cat"]], writes=[omb])
                p.op("S", lambda e, omb=omb, pomb=pomb, mcol=mcol, osum=osum:
                     e.activation(pomb.ap, omb.ap, AF.Exp, bias=mcol.ap,
                                  accum_out=osum.ap),
                     reads=[omb, mcol], writes=[pomb, osum])
                p.op("V", lambda e, orec=orec, osum=osum: e.reciprocal(
                        orec.ap, osum.ap),
                     reads=[osum], writes=[orec])
                p.op("V", lambda e, pomb=pomb, orec=orec: e.scalar_tensor_tensor(
                        pomb.ap, pomb.ap, orec.ap, C["d_cat"].ap,
                        ALU.mult, ALU.mult),
                     reads=[pomb, orec, C["d_cat"]], writes=[pomb])

            def stC(qi, hd):
                ks = 128 * qi
                par = qi % 2
                pnb, pomb, pTb = pns[par], poms[par], pTs[par]
                for b in range(5):
                    p.op("T", lambda e, b=b, pnb=pnb: e.transpose(
                            TP.t[:, 128 * b:128 * (b + 1)],
                            pnb.t[:, 128 * b:128 * (b + 1)], C["ident"].ap),
                         reads=[pnb, C["ident"]], writes=[TP])
                for oi in range(2):
                    p.op("T", lambda e, oi=oi, pomb=pomb: e.transpose(
                            TP.t[:, 640 + 128 * oi:640 + 128 * (oi + 1)],
                            pomb.t[:, 128 * oi:128 * (oi + 1)], C["ident"].ap),
                         reads=[pomb, C["ident"]], writes=[TP])
                p.op("V", lambda e, pTb=pTb: e.tensor_copy(pTb.ap, TP.ap),
                     reads=[TP], writes=[pTb])
                o0 = 256 * par
                p.mm_group(PO, [(PO.t[:, o0:o0 + 128],
                                 vt.t[:, 128 * (qi + b):128 * (qi + b + 1)],
                                 pTb.t[:, 128 * b:128 * (b + 1)]) for b in range(5)],
                           reads=[vt, pTb])
                p.mm_group(PO, [(PO.t[:, o0 + 128:o0 + 256],
                                 kpt.t[:, 128 * (qi + b):128 * (qi + b + 1)],
                                 pTb.t[:, 640 + 128 * (b - 3):640 + 128 * (b - 2)])
                                for b in (3, 4)],
                           reads=[kpt, pTb])
                p.op("S", lambda e, ks=ks, o0=o0, attnh=attnh: e.copy(
                        attnh.t[:, ks:ks + 128], PO.t[:, o0:o0 + 128]),
                     reads=[PO], writes=[attnh])
                p.op("S", lambda e, ks=ks, hd=hd, o0=o0: e.copy(
                        qmem[hd].t[:, ks:ks + 128], PO.t[:, o0 + 128:o0 + 256]),
                     reads=[PO], writes=[qmem[hd]])

            stA(0)
            stA(1)
            stB(0)
            for qi in range(8):
                if qi + 2 < 8:
                    stA(qi + 2)
                stC(qi, hd)
                for _ in range(3 if qi < 5 else 1):
                    if nxt_items:
                        nxt_items.pop(0)()
                if qi + 1 < 8:
                    stB(qi + 1)
            while nxt_items:
                nxt_items.pop(0)()

            p.dma("Yd", SC["attn"].t[hd, :, :], attnh.ap,
                  reads=[attnh], writes=[SC["attn"]])
            if tap_attn is not None:
                p.dma("Yd", tap_attn.t[hd, :, :], attnh.ap,
                      reads=[attnh], writes=[tap_attn])
            if tap_qm is not None:
                p.dma("Yd", tap_qm.t[hd, :, :], qmem[hd].ap,
                      reads=[qmem[hd]], writes=[tap_qm])
            if hd == 0:
                if tap_q2 is not None:
                    p.dma("Yd", tap_q2.ap, q2.ap, reads=[q2], writes=[tap_q2])
                if tap_k2 is not None:
                    p.dma("Yd", tap_k2.ap, k2.ap, reads=[k2], writes=[tap_k2])
                for nmt, bt, shp, dt in (("vth0", vt, [128, 1536], BF16),
                                         ("kpth0", kpt, [128, 1536], BF16)):
                    tpx = tap(nmt, shp, dt)
                    if tpx is not None:
                        p.dma("Yd", tpx.ap, bt.ap, reads=[bt], writes=[tpx])
        p.barrier()



def _p2(p, nc, I, C, hbuf, qmem, SC, rms2ss, tap):
    tap_m = tap("macc", [NDC, 128, T], F32)
    tap_xa = tap("xa", [NDC, 128, T], F32)
    tap_r = tap("r", [NDC, 128, T], F32)
    with ExitStack() as e2:
        w1b = [p.sb(f"p2w1{i}", [128, NDC * 128], BF16, es=e2) for i in range(2)]
        w2b = [p.sb(f"p2w2{i}", [128, 8 * 128], BF16, es=e2) for i in range(2)]
        wob = [p.sb(f"p2wo{i}", [128, NDC * 128], BF16, es=e2) for i in range(2)]
        macc = [p.sb(f"p2ma{m}", [128, T], F32, es=e2) for m in range(NDC)]
        poly = [p.sb(f"p2po{f}", [128, T], BF16, es=e2) for f in range(8)]
        scr = p.sb("p2scr", [128, 512], F32, es=e2)
        sqb = p.sb("p2sqb", [128, 512], BF16, es=e2)
        sg = p.sb("p2sg", [128, 512], F32, es=e2)
        t2b = sg
        xac = p.sb("p2xa", [128, T], F32, es=e2)
        rc = p.sb("p2rc", [128, T], F32, es=e2)
        xcs = [p.sb(f"p2xc{i}", [128, T], F32, es=e2) for i in range(2)]
        PH = [p.ps(f"p2ph{i}", [128, 512], F32, es=e2) for i in range(2)]
        PW = [p.ps(f"p2pw{i}", [128, 512], F32, es=e2) for i in range(2)]
        PA = [p.ps(f"p2pa{i}", [128, 512], F32, es=e2) for i in range(2)]
        RS = p.ps("p2rs", [1, T], F32, es=e2)

        def ldw1(idx, dst):
            p.dma("Gd", dst.t[:].rearrange("p (k c) -> p k c", c=128),
                  I["mw1"].t[:, 128 * idx:128 * (idx + 1)]
                  .rearrange("(k p) c -> p k c", p=128), writes=[dst])

        def ldw2(win, m, dst):
            p.dma("Gd", dst.t[:].rearrange("p (k c) -> p k c", c=128),
                  I["mw2"].t[1024 * win:1024 * (win + 1), 128 * m:128 * (m + 1)]
                  .rearrange("(k p) c -> p k c", p=128), writes=[dst])

        def ldwo(m, dst):
            p.dma("Gd", dst.t[:].rearrange("p (k c) -> p k c", c=128),
                  I["wo"].t[:, 128 * m:128 * (m + 1)]
                  .rearrange("(k p) c -> p k c", p=128), writes=[dst])

        for i in range(NDC):
            p.dma("Yd", hbuf[i].t[:, 0:T], SC["attn"].t[i, :, :],
                  reads=[SC["attn"]], writes=[hbuf[i]])
        ldw1(0, w1b[0])
        for win in range(8):
            for fs8 in range(8):
                idx = 8 * win + fs8
                wc = w1b[idx % 2]
                if idx + 1 < 64:
                    ldw1(idx + 1, w1b[(idx + 1) % 2])
                wv = wc.t[:].rearrange("p (k c) -> p k c", c=128)
                for th in range(2):
                    t0 = 512 * th
                    PHb = PH[th]
                    p.mm_group(PHb, [(PHb.ap, wv[:, k, :], qmem[k].t[:, t0:t0 + 512])
                                     for k in range(NDC)], reads=[wc] + qmem)
                    p.op("S", lambda e, PHb=PHb: e.activation(
                            scr.ap, PHb.ap, AF.Silu),
                         reads=[PHb], writes=[scr])
                    p.op("V", lambda e, fs8=fs8, t0=t0: e.scalar_tensor_tensor(
                            poly[fs8].t[:, t0:t0 + 512], scr.ap,
                            1.0, scr.ap, ALU.add, ALU.mult),
                         reads=[scr], writes=[poly[fs8]])
            if win == 0:
                ldw2(0, 0, w2b[0])
            for m in range(NDC):
                wc = w2b[m % 2]
                nwin, nm = (win, m + 1) if m + 1 < NDC else (win + 1, 0)
                if nwin < 8:
                    ldw2(nwin, nm, w2b[(m + 1) % 2])
                elif win == 7 and m == NDC - 1:
                    ldwo(0, wob[0])
                wv = wc.t[:].rearrange("p (k c) -> p k c", c=128)
                for th in range(2):
                    t0 = 512 * th
                    PWb = PW[th]
                    p.mm_group(PWb, [(PWb.ap, wv[:, k, :],
                                      poly[k].t[:, t0:t0 + 512])
                                     for k in range(8)], reads=[wc] + poly)
                    if win == 0:
                        p.op("V", lambda e, m=m, t0=t0, PWb=PWb: e.tensor_copy(
                                macc[m].t[:, t0:t0 + 512], PWb.ap),
                             reads=[PWb], writes=[macc[m]])
                    else:
                        p.op("V", lambda e, m=m, t0=t0, PWb=PWb: e.tensor_tensor(
                                macc[m].t[:, t0:t0 + 512],
                                macc[m].t[:, t0:t0 + 512], PWb.ap, ALU.add),
                             reads=[macc[m], PWb], writes=[macc[m]])
        p.dma("Yd", xcs[0].ap, I["x_f"].t[0:128, HALO:HALO + T], writes=[xcs[0]])
        for m in range(NDC):
            wc = wob[m % 2]
            if m + 1 < NDC:
                ldwo(m + 1, wob[(m + 1) % 2])
                p.dma("Yd", xcs[(m + 1) % 2].ap,
                      I["x_f"].t[128 * (m + 1):128 * (m + 2), HALO:HALO + T],
                      writes=[xcs[(m + 1) % 2]])
            wv = wc.t[:].rearrange("p (k c) -> p k c", c=128)
            xc = xcs[m % 2]
            for th in range(2):
                t0 = 512 * th
                PAb = PA[th]
                p.mm_group(PAb, [(PAb.ap, wv[:, k, :], hbuf[k].t[:, t0:t0 + 512])
                                 for k in range(NDC)], reads=[wc] + hbuf)
                p.op("S", lambda e, m=m, t0=t0: e.activation(
                        sg.ap, macc[m].t[:, t0:t0 + 512], AF.Sigmoid),
                     reads=[macc[m]], writes=[sg])
                p.op("V", lambda e, PAb=PAb: e.tensor_tensor(t2b.ap, sg.ap, PAb.ap,
                                                             ALU.mult),
                     reads=[sg, PAb], writes=[t2b])
                p.op("V", lambda e, xc=xc, t0=t0: e.tensor_tensor(
                        xac.t[:, t0:t0 + 512], xc.t[:, t0:t0 + 512], t2b.ap,
                        ALU.add),
                     reads=[xc, t2b], writes=[xac])
                p.op("V", lambda e, xc=xc, t0=t0: e.scalar_tensor_tensor(
                        rc.t[:, t0:t0 + 512], xc.t[:, t0:t0 + 512], 2.0, t2b.ap,
                        ALU.mult, ALU.add),
                     reads=[xc, t2b], writes=[rc])
                p.op("S", lambda e, t0=t0: e.activation(
                        sqb.ap, xac.t[:, t0:t0 + 512], AF.Square),
                     reads=[xac], writes=[sqb])
                p.mm(RS.t[:, t0:t0 + 512], C["onesb"].t[:, 0:1],
                     sqb.ap, RS, reads=[C["onesb"], sqb],
                     start=(m == 0), stop=(m == NDC - 1))
            p.dma("Yd", SC["xa"].t[m, :, :], xac.ap, reads=[xac], writes=[SC["xa"]])
            p.dma("Yd", SC["r"].t[m, :, :], rc.ap, reads=[rc], writes=[SC["r"]])
            if tap_m is not None:
                p.dma("Yd", tap_m.t[m, :, :], macc[m].ap,
                      reads=[macc[m]], writes=[tap_m])
            if tap_xa is not None:
                p.dma("Yd", tap_xa.t[m, :, :], xac.ap, reads=[xac], writes=[tap_xa])
            if tap_r is not None:
                p.dma("Yd", tap_r.t[m, :, :], rc.ap, reads=[rc], writes=[tap_r])
        p.op("V", lambda e: e.tensor_copy(rms2ss.ap, RS.ap),
             reads=[RS], writes=[rms2ss])
        p.barrier()



def _p3(p, nc, I, C, SC, rms2ss, tap):
    tap_y = tap("y", [NDC, 128, T], BF16)
    tap_rf = tap("rfin", [NDC, 128, T], F32)
    tap_x1 = tap("x1f", [NDC, 128, T], F32)
    NF3 = FFHP // 128
    with ExitStack() as e3:
        y = [p.sb(f"p3y{i}", [128, T], BF16, es=e3) for i in range(NDC)]
        g = [p.sb(f"p3g{f}", [128, T], BF16, es=e3) for f in range(NF3)]
        w1c = [p.sb(f"p3w1{i}", [128, NDC * 128], BF16, es=e3) for i in range(2)]
        w3c = [p.sb(f"p3w3{i}", [128, NDC * 128], BF16, es=e3) for i in range(2)]
        w2c = [p.sb(f"p3w2{i}", [128, NF3 * 128], BF16, es=e3) for i in range(2)]
        xc = [p.sb(f"p3xc{i}", [128, T], F32, es=e3) for i in range(2)]
        rfin = p.sb("p3rf", [128, T], F32, es=e3)
        scr = p.sb("p3scr", [128, T], F32, es=e3)
        sqb = p.sb("p3sqb", [128, 512], BF16, es=e3)
        rows = p.sb("p3rows", [1, 4 * T], F32, es=e3)
        R2 = rows.t[:, 0:T]; LS = rows.t[:, T:2*T]; LQ = rows.t[:, 2*T:3*T]
        MR = rows.t[:, 3*T:4*T]; R1 = R2
        bco = p.sb("p3bco", [128, T], F32, es=e3)
        mb = p.sb("p3mb", [128, T], F32, es=e3)
        x1b = p.sb("p3x1b", [128, T], BF16, es=e3)
        x1f = rfin
        BC = p.ps("p3bc", [128, 512], F32, es=e3)
        PH = [p.ps(f"p3ph{i}", [128, 512], F32, es=e3) for i in range(2)]
        LR = [p.ps(f"p3lr{i}", [1, 512], F32, es=e3) for i in range(4)]

        def bcast(src_slice, dst):
            for th in range(2):
                p.mm(BC.ap, C["ones"].t[0:1, :], src_slice[:, 512*th:512*(th+1)],
                     BC, reads=[C["ones"], rows])
                p.op("V", lambda e, th=th, dst=dst: e.tensor_copy(
                        dst.t[:, 512*th:512*(th+1)], BC.ap),
                     reads=[BC], writes=[dst])

        p.op("S", lambda e: e.activation(R2, rms2ss.ap, AF.Ln,
                                         bias=C["epsb"].t[0:1, 0:1], scale=1.0 / D),
             reads=[rms2ss, C["epsb"]], writes=[rows])
        p.op("S", lambda e: e.activation(R2, R2, AF.Exp, scale=-0.5),
             reads=[rows], writes=[rows])
        bcast(R2, bco)
        for i in range(NDC):
            xb = xc[i % 2]
            p.dma("Yd", xb.ap, SC["xa"].t[i, :, :], reads=[SC["xa"]], writes=[xb])
            p.op("V", lambda e, i=i, xb=xb: e.scalar_tensor_tensor(
                    y[i].ap, xb.ap, C["n2w"].t[:, i:i+1], bco.ap, ALU.mult, ALU.mult),
                 reads=[xb, C["n2w"], bco], writes=[y[i]])
            if tap_y is not None:
                p.dma("Yd", tap_y.t[i, :, :], y[i].ap, reads=[y[i]], writes=[tap_y])

        def ldw13(fs, b):
            for nm, dst in (("fw1", w1c[b]), ("fw3", w3c[b])):
                p.dma("Gd", dst.t[:].rearrange("p (k c) -> p k c", c=128),
                      I[nm].t[:, 128*fs:128*(fs+1)].rearrange("(k p) c -> p k c", p=128),
                      writes=[dst])
        ldw13(0, 0)
        for fs in range(NF3):
            if fs + 1 < NF3:
                ldw13(fs + 1, (fs + 1) % 2)
            wv1 = w1c[fs % 2].t[:].rearrange("p (k c) -> p k c", c=128)
            wv3 = w3c[fs % 2].t[:].rearrange("p (k c) -> p k c", c=128)
            for th in range(2):
                t0 = 512 * th
                p.mm_group(PH[0], [(PH[0].ap, wv1[:, k, :], y[k].t[:, t0:t0+512])
                                   for k in range(NDC)], reads=[w1c[fs % 2]] + y)
                p.op("S", lambda e, t0=t0: e.activation(scr.t[:, t0:t0+512],
                                                        PH[0].ap, AF.Silu),
                     reads=[PH[0]], writes=[scr])
                p.mm_group(PH[1], [(PH[1].ap, wv3[:, k, :], y[k].t[:, t0:t0+512])
                                   for k in range(NDC)], reads=[w3c[fs % 2]] + y)
                p.op("V", lambda e, fs=fs, t0=t0: e.tensor_tensor(
                        g[fs].t[:, t0:t0+512], scr.t[:, t0:t0+512], PH[1].ap,
                        ALU.mult),
                     reads=[scr, PH[1]], writes=[g[fs]])

        def ldw2(m, b):
            p.dma("Gd", w2c[b].t[:].rearrange("p (k c) -> p k c", c=128),
                  I["fw2"].t[:, 128*m:128*(m+1)].rearrange("(k p) c -> p k c", p=128),
                  writes=[w2c[b]])
        ldw2(0, 0)
        for m in range(NDC):
            if m + 1 < NDC:
                ldw2(m + 1, (m + 1) % 2)
            rb = xc[m % 2]
            p.dma("Yd", rb.ap, SC["r"].t[m, :, :], reads=[SC["r"]], writes=[rb])
            wv2 = w2c[m % 2].t[:].rearrange("p (k c) -> p k c", c=128)
            for th in range(2):
                t0 = 512 * th
                PHb = PH[th]
                p.mm_group(PHb, [(PHb.ap, wv2[:, k, :], g[k].t[:, t0:t0+512])
                                 for k in range(NF3)], reads=[w2c[m % 2]] + g)
                p.op("V", lambda e, t0=t0, rb=rb, PHb=PHb: e.tensor_tensor(
                        rfin.t[:, t0:t0+512], rb.t[:, t0:t0+512], PHb.ap, ALU.add),
                     reads=[rb, PHb], writes=[rfin])
                p.mm(LR[th].ap, C["ones"].t[:, 0:1], rfin.t[:, t0:t0+512],
                     LR[th], reads=[C["ones"], rfin],
                     start=(m == 0), stop=(m == NDC - 1))
                p.op("S", lambda e, t0=t0: e.activation(sqb.ap,
                                                        rfin.t[:, t0:t0+512],
                                                        AF.Square),
                     reads=[rfin], writes=[sqb])
                p.mm(LR[2+th].ap, C["onesb"].t[:, 0:1], sqb.ap,
                     LR[2+th], reads=[C["onesb"], sqb],
                     start=(m == 0), stop=(m == NDC - 1))
            p.dma("Yd", SC["r"].t[m, :, :], rfin.ap, reads=[rfin], writes=[SC["r"]])
            if tap_rf is not None:
                p.dma("Yd", tap_rf.t[m, :, :], rfin.ap, reads=[rfin], writes=[tap_rf])

        for th in range(2):
            p.op("V", lambda e, th=th: e.tensor_copy(LS[:, 512*th:512*(th+1)],
                                                     LR[th].ap),
                 reads=[LR[th]], writes=[rows])
            p.op("V", lambda e, th=th: e.tensor_copy(LQ[:, 512*th:512*(th+1)],
                                                     LR[2+th].ap),
                 reads=[LR[2+th]], writes=[rows])
        p.op("V", lambda e: e.tensor_scalar(MR, LS, 1.0 / D, None, ALU.mult),
             reads=[rows], writes=[rows])
        p.op("V", lambda e: e.tensor_scalar(LQ, LQ, 1.0 / D, None, ALU.mult),
             reads=[rows], writes=[rows])
        p.op("V", lambda e: e.tensor_tensor(R1, MR, MR, ALU.mult),
             reads=[rows], writes=[rows])
        p.op("V", lambda e: e.tensor_tensor(LQ, LQ, R1, ALU.subtract),
             reads=[rows], writes=[rows])
        p.op("S", lambda e: e.activation(R1, LQ, AF.Ln, bias=C["epsb"].t[0:1, 1:2]),
             reads=[rows, C["epsb"]], writes=[rows])
        p.op("S", lambda e: e.activation(R1, R1, AF.Exp, scale=-0.5),
             reads=[rows], writes=[rows])
        bcast(MR, mb)
        bcast(R1, bco)
        for m in range(NDC):
            rb = xc[m % 2]
            p.dma("Yd", rb.ap, SC["r"].t[m, :, :], reads=[SC["r"]], writes=[rb])
            p.op("G", lambda e, rb=rb: e.tensor_tensor(scr.ap, rb.ap, mb.ap,
                                                       ALU.subtract),
                 reads=[rb, mb], writes=[scr])
            p.op("V", lambda e: e.tensor_tensor(x1f.ap, scr.ap, bco.ap, ALU.mult),
                 reads=[scr, bco], writes=[x1f])
            p.op("S", lambda e: e.copy(x1b.ap, x1f.ap), reads=[x1f], writes=[x1b])
            p.dma("Yd", SC["x1f"].t[m, :, :], x1f.ap, reads=[x1f], writes=[SC["x1f"]])
            p.dma("Yd", SC["x1b"].t[m, :, :], x1b.ap, reads=[x1b], writes=[SC["x1b"]])
            if tap_x1 is not None:
                p.dma("Yd", tap_x1.t[m, :, :], x1f.ap, reads=[x1f], writes=[tap_x1])
        p.barrier()



def _p4(p, nc, I, C, SC, ln2s, tap):
    tap_w = tap("gatew", [2, 128, 32], F32)
    tap_r2 = tap("r2", [NDC, 128, T], F32)
    NF4 = DFF // 128
    with ExitStack() as e4:
        x1h = [p.sb(f"p4x{k}", [128, 512], BF16, es=e4) for k in range(NDC)]
        wrb = p.sb("p4wr", [128, NDC * NEXP], BF16, es=e4)
        bdb = p.sb("p4bd", [128, D], BF16, es=e4)
        atile = [p.sb(f"p4at{i}", [128, NDC * 128], BF16, es=e4) for i in range(1)]
        t_sb = p.sb("p4t", [16, 16 * 512], BF16, es=e4)
        gbp = [p.sb(f"p4gb{f}", [128, 1024], BF16, es=e4) for f in range(NF4 // 2)]
        ubp = [p.sb(f"p4ub{f}", [128, 1024], BF16, es=e4) for f in range(NF4 // 2)]
        def gb_ap(fs):
            return gbp[fs // 2].t[:, 512 * (fs % 2):512 * (fs % 2) + 512]
        def ub_ap(fs):
            return ubp[fs // 2].t[:, 512 * (fs % 2):512 * (fs % 2) + 512]
        bgc = p.sb("p4bgc", [16, DFF], BF16, es=e4)
        buc = p.sb("p4buc", [16, DFF], BF16, es=e4)
        adb = [p.sb(f"p4ad{i}", [128, NF4 * R], BF16, es=e4) for i in range(1)]
        wdc = [p.sb(f"p4wd{i}", [128, 22 * 128], BF16, es=e4) for i in range(2)]
        lgs = p.sb("p4lg", [128, 32], F32, es=e4)
        wgt = p.sb("p4wt", [128, 32], F32, es=e4)
        wtr = p.sb("p4wtr", [8, 512], F32, es=e4)
        wrows = p.sb("p4wrow", [1, 512], F32, es=e4)
        wsb = p.sb("p4wsb", [16, 512], F32, es=e4)
        dtt = p.sb("p4dtt", [16, 512], BF16, es=e4)
        dT_all = p.sb("p4dta", [128, 512], BF16, es=e4)
        scr1 = p.sb("p4s1", [128, 512], F32, es=e4)
        scr2 = p.sb("p4s2", [128, 512], F32, es=e4)
        scr3 = None
        heb = p.sb("p4he", [128, 512], BF16, es=e4)
        scr3 = scr1
        xfc = [p.sb("p4xf0", [128, 512], F32, es=e4), scr2]
        r2c = p.sb("p4r2", [128, 512], F32, es=e4)
        col = p.sb("p4col", [128, 4], F32, es=e4)

        p.dma("Gd", wrb.t[:].rearrange("p (k c) -> p k c", c=NEXP),
              I["wr"].ap.rearrange("(k p) c -> p k c", p=128), writes=[wrb])
        p.dma("Gd", bdb.ap, I["bd"].ap, writes=[bdb])

        for hi, t0 in enumerate((0, 512)):
            for k in range(NDC):
                p.dma("Yd", x1h[k].ap, SC["x1b"].t[k, :, t0:t0 + 512],
                      reads=[SC["x1b"]], writes=[x1h[k]])

            with ExitStack() as eR:
                PT4 = p.ps(f"p4pt{hi}", [128, 32], F32, es=eR)
                TRP = p.ps(f"p4trp{hi}", [8, 512], F32, es=eR)
                wrv = wrb.t[:].rearrange("p (k c) -> p k c", c=NEXP)
                for tt in range(4):
                    p.mm_group(PT4, [(PT4.t[:, 8 * tt:8 * (tt + 1)],
                                      x1h[k].t[:, 128 * tt:128 * (tt + 1)],
                                      wrv[:, k, :]) for k in range(NDC)],
                               reads=[wrb] + x1h)
                p.op("V", lambda e: e.tensor_copy(lgs.ap, PT4.ap),
                     reads=[PT4], writes=[lgs])
                for tt in range(4):
                    sl = lgs.t[:, 8 * tt:8 * (tt + 1)]
                    wl = wgt.t[:, 8 * tt:8 * (tt + 1)]
                    p.op("V", lambda e, sl=sl: e.reduce_max(col.t[:, 0:1], sl, axis=AX.X),
                         reads=[lgs], writes=[col])
                    p.op("V", lambda e: e.tensor_scalar_mul(col.t[:, 3:4],
                                                            col.t[:, 0:1], -1.0),
                         reads=[col], writes=[col])
                    p.op("S", lambda e, sl=sl, wl=wl: e.activation(
                            wl, sl, AF.Exp, bias=col.t[:, 3:4],
                            accum_out=col.t[:, 2:3]),
                         reads=[lgs, col], writes=[wgt, col])
                    p.op("V", lambda e: e.reciprocal(col.t[:, 2:3], col.t[:, 2:3]),
                         reads=[col], writes=[col])
                    p.op("V", lambda e, sl=sl: e.tensor_scalar(
                            scr1.t[:, 0:8], sl, col.t[:, 0:1], None, ALU.is_equal),
                         reads=[lgs, col], writes=[scr1])
                    p.op("V", lambda e, sl=sl: e.scalar_tensor_tensor(
                            scr1.t[:, 0:8], scr1.t[:, 0:8], NEG, sl,
                            ALU.mult, ALU.add),
                         reads=[scr1, lgs], writes=[scr1])
                    p.op("V", lambda e: e.reduce_max(col.t[:, 1:2], scr1.t[:, 0:8],
                                                     axis=AX.X),
                         reads=[scr1], writes=[col])
                    p.op("V", lambda e, sl=sl: e.tensor_scalar(
                            scr1.t[:, 0:8], sl, col.t[:, 1:2], None, ALU.is_ge),
                         reads=[lgs, col], writes=[scr1])
                    p.op("V", lambda e, wl=wl: e.scalar_tensor_tensor(
                            wl, wl, col.t[:, 2:3], scr1.t[:, 0:8],
                            ALU.mult, ALU.mult),
                         reads=[wgt, col, scr1], writes=[wgt])
                if tap_w is not None:
                    p.dma("Yd", tap_w.t[hi, :, :], wgt.ap, reads=[wgt], writes=[tap_w])
                for tt in range(4):
                    p.op("T", lambda e, tt=tt: e.transpose(
                            TRP.t[:, 128 * tt:128 * (tt + 1)],
                            wgt.t[:, 8 * tt:8 * (tt + 1)], C["identf"].ap),
                         reads=[wgt, C["identf"]], writes=[TRP])
                p.op("V", lambda e: e.tensor_scalar_mul(wtr.ap, TRP.t[0:8, :],
                                                        SCALING),
                     reads=[TRP], writes=[wtr])
                p.barrier(("T", "V", "S"))

            with ExitStack() as eE:
                PAW = [p.ps(f"p4pa{hi}_{i}", [128, 1024], F32, es=eE)
                       for i in range(2)]
                PUW = p.ps(f"p4pu{hi}", [128, 1024], F32, es=eE)
                PD = p.ps(f"p4pd{hi}", [16, 512], F32, es=eE)
                for ti in range(8):
                    p.dma("Gd", atile[0].t[:].rearrange("p (k c) -> p k c", c=128),
                          I["a_all"].t[:, 128 * ti:128 * (ti + 1)]
                          .rearrange("(k p) c -> p k c", p=128),
                          writes=[atile[0]])
                    av = atile[0].t[:].rearrange("p (k c) -> p k c", c=128)
                    PAb = PAW[ti % 2]
                    p.mm_group(PAb, [(PAb.t[:, 0:512], av[:, k, :], x1h[k].ap)
                                     for k in range(NDC)], reads=[atile[0]] + x1h)
                    base_slot = 0 if ti < 4 else 8
                    uu = ti % 4
                    for half_e, pbase in ((0, 0), (1, 64)):
                        slot = base_slot + 2 * uu + half_e
                        p.op("S", lambda e, slot=slot, pbase=pbase, PAb=PAb:
                             e.activation(t_sb.t[:, 512 * slot:512 * (slot + 1)],
                                          PAb.t[pbase:pbase + 16, 0:512], AF.Copy,
                                          scale=SCALING),
                             reads=[PAb], writes=[t_sb])

                with ExitStack() as eB:
                    wgc = [p.sb(f"p4wg{hi}_{i}", [128, NDC * 128], BF16, es=eB)
                           for i in range(2)]
                    wuc = [p.sb(f"p4wu{hi}_{i}", [128, NDC * 128], BF16, es=eB)
                           for i in range(2)]
                    def ldg(fs, b):
                        for nm, dst in (("wg", wgc[b]), ("wu", wuc[b])):
                            p.dma("Gd", dst.t[:].rearrange("p (k c) -> p k c", c=128),
                                  I[nm].t[:, 128 * fs:128 * (fs + 1)]
                                  .rearrange("(k p) c -> p k c", p=128), writes=[dst])
                    ldg(0, 0)
                    for fs in range(NF4):
                        if fs + 1 < NF4:
                            ldg(fs + 1, (fs + 1) % 2)
                        gv = wgc[fs % 2].t[:].rearrange("p (k c) -> p k c", c=128)
                        uv = wuc[fs % 2].t[:].rearrange("p (k c) -> p k c", c=128)
                        PAb = PAW[fs % 2]
                        p.mm_group(PAb, [(PAb.t[:, 0:512], gv[:, k, :], x1h[k].ap)
                                         for k in range(NDC)], reads=[wgc[fs % 2]] + x1h)
                        p.op("S", lambda e, fs=fs, PAb=PAb: e.copy(
                                gb_ap(fs), PAb.t[:, 0:512]),
                             reads=[PAb], writes=[gbp[fs // 2]])
                        p.mm_group(PUW, [(PUW.t[:, 0:512], uv[:, k, :], x1h[k].ap)
                                         for k in range(NDC)], reads=[wuc[fs % 2]] + x1h)
                        p.op("S", lambda e, fs=fs: e.copy(ub_ap(fs), PUW.t[:, 0:512]),
                             reads=[PUW], writes=[ubp[fs // 2]])
                    p.barrier(("T", "V", "S", "Gd"))

                def ldcombo(ex, b):
                    p.dma("Gd", bgc.ap, I["bg"].t[ex, :, :], writes=[bgc])
                    p.dma("Gd", buc.ap, I["bu"].t[ex, :, :], writes=[buc])
                    p.dma("Gd", adb[b].t[:].rearrange("p (f r) -> p f r", r=R),
                          I["ad"].t[ex, :, :].rearrange("(f p) r -> p f r", p=128),
                          writes=[adb[b]])
                NP4 = NF4 // 2
                with ExitStack() as eX:
                    s1 = [p.sb(f"p4s1{hi}_{i}", [128, 1024], BF16, es=eX)
                          for i in range(2)]
                    s2 = [p.sb(f"p4s2{hi}_{i}", [128, 1024], BF16, es=eX)
                          for i in range(2)]
                    s3 = [p.sb(f"p4s3{hi}_{i}", [128, 1024], BF16, es=eX)
                          for i in range(2)]
                    hebs = [p.sb(f"p4he{hi}_{i}", [128, 1024], BF16, es=eX)
                            for i in range(2)]
                    bbscr = p.sb(f"p4bbs{hi}", [128, 1024], BF16, es=eX)
                    for ex in range(NEXP):
                        ldcombo(ex, 0)
                        ab = adb[0].t[:].rearrange("p (f r) -> p f r", r=R)
                        tg = t_sb.t[:, 512 * ex:512 * (ex + 1)]
                        tu = t_sb.t[:, 512 * (8 + ex):512 * (9 + ex)]

                        def mms_g(j):
                            par = j % 2
                            for h2 in range(2):
                                fs = 2 * j + h2
                                p.mm(PAW[par].t[:, 512 * h2:512 * (h2 + 1)],
                                     bgc.t[:, 128 * fs:128 * (fs + 1)],
                                     tg, PAW[par], reads=[bgc, t_sb])

                        def mms_u(j):
                            for h2 in range(2):
                                fs = 2 * j + h2
                                p.mm(PUW.t[:, 512 * h2:512 * (h2 + 1)],
                                     buc.t[:, 128 * fs:128 * (fs + 1)],
                                     tu, PUW, reads=[buc, t_sb])

                        def elem(j):
                            par = j % 2
                            p.op("V", lambda e, j=j, par=par: e.tensor_tensor(
                                    s1[par].ap, gbp[j].ap, PAW[par].ap, ALU.add),
                                 reads=[gbp[j], PAW[par]], writes=[s1[par]])
                            p.op("S", lambda e, par=par: e.activation(
                                    s2[par].ap, s1[par].ap, AF.Silu),
                                 reads=[s1[par]], writes=[s2[par]])
                            p.op("V", lambda e, j=j, par=par: e.tensor_tensor(
                                    s3[par].ap, ubp[j].ap, PUW.ap, ALU.add),
                                 reads=[ubp[j], PUW], writes=[s3[par]])
                            p.op("V", lambda e, par=par: e.tensor_tensor(
                                    hebs[par].ap, s2[par].ap, s3[par].ap, ALU.mult),
                                 reads=[s2[par], s3[par]], writes=[hebs[par]])
                            for h2 in range(2):
                                fs = 2 * j + h2
                                p.mm(PD.ap, ab[:, fs, :],
                                     hebs[par].t[:, 512 * h2:512 * (h2 + 1)], PD,
                                     reads=[adb[0], hebs[par]],
                                     start=(fs == 0), stop=(fs == NF4 - 1))

                        def bb_make(j):
                            p.op("S", lambda e, j=j: e.activation(
                                    bbscr.ap, gbp[j].ap, AF.Silu),
                                 reads=[gbp[j]], writes=[bbscr])
                            p.op("V", lambda e, j=j: e.tensor_tensor(
                                    gbp[j].ap, bbscr.ap, ubp[j].ap, ALU.mult),
                                 reads=[bbscr, ubp[j]], writes=[gbp[j]])
                        mms_g(0)
                        mms_u(0)
                        for j in range(1, NP4):
                            mms_g(j)
                            elem(j - 1)
                            mms_u(j)
                            if ex == NEXP - 1 and j >= 2:
                                bb_make(j - 2)
                        elem(NP4 - 1)
                        if ex == NEXP - 1:
                            bb_make(NP4 - 2)
                            bb_make(NP4 - 1)
                        for tt in range(4):
                            p.dma("Yd", wrows.t[:, 128 * tt:128 * (tt + 1)],
                                  wtr.t[ex:ex + 1, 128 * tt:128 * (tt + 1)],
                                  reads=[wtr], writes=[wrows])
                        p.mm(PUW.t[0:16, 0:512], C["ones"].t[0:1, 0:16],
                             wrows.ap, PUW, reads=[C["ones"], wrows])
                        p.op("V", lambda e: e.tensor_copy(wsb.ap, PUW.t[0:16, 0:512]),
                             reads=[PUW], writes=[wsb])
                        p.op("V", lambda e: e.tensor_tensor(dtt.ap, PD.ap, wsb.ap,
                                                            ALU.mult),
                             reads=[PD, wsb], writes=[dtt])
                        p.dma("Yd", dT_all.t[16 * ex:16 * (ex + 1), :], dtt.ap,
                              reads=[dtt], writes=[dT_all])

                p.barrier(("T", "V", "S", "G"))

            with ExitStack() as eF:
                PF = [p.ps(f"p4pf{hi}_{i}", [128, 512], F32, es=eF) for i in range(2)]
                LR2 = [p.ps(f"p4lr{hi}_{i}", [1, 512], F32, es=eF) for i in range(2)]
                def ldwd(m, b):
                    p.dma("Gd", wdc[b].t[:].rearrange("p (k c) -> p k c", c=128),
                          I["wd"].t[2816 * b:2816 * (b + 1), 128 * m:128 * (m + 1)]
                          .rearrange("(k p) c -> p k c", p=128), writes=[wdc[b]])
                ldwd(0, 0)
                ldwd(0, 1)
                p.dma("Yd", xfc[0].ap, SC["x1f"].t[0, :, t0:t0 + 512],
                      reads=[SC["x1f"]], writes=[xfc[0]])
                for m in range(NDC):
                    if m + 1 < NDC:
                        p.dma("Yd", xfc[(m + 1) % 2].ap,
                              SC["x1f"].t[m + 1, :, t0:t0 + 512],
                              reads=[SC["x1f"]], writes=[xfc[(m + 1) % 2]])
                    dvA = wdc[0].t[:].rearrange("p (k c) -> p k c", c=128)
                    dvB = wdc[1].t[:].rearrange("p (k c) -> p k c", c=128)
                    PFb = PF[m % 2]
                    p.mm_group(PFb, [(PFb.ap, dvA[:, k, :], gb_ap(k))
                                     for k in range(22)],
                               reads=[wdc[0]] + gbp[0:11], start=True, stop=False)
                    if m + 1 < NDC:
                        ldwd(m + 1, 0)
                    mmsB = [(PFb.ap, dvB[:, k, :], gb_ap(22 + k))
                            for k in range(22)]
                    mmsB.append((PFb.ap, bdb.t[:, 128 * m:128 * (m + 1)],
                                 dT_all.ap))
                    p.mm_group(PFb, mmsB, reads=[wdc[1], bdb, dT_all] + gbp[11:],
                               start=False, stop=True)
                    if m + 1 < NDC:
                        ldwd(m + 1, 1)
                    xb = xfc[m % 2]
                    p.op("V", lambda e, xb=xb, PFb=PFb: e.tensor_tensor(
                            r2c.ap, xb.ap, PFb.ap, ALU.add),
                         reads=[xb, PFb], writes=[r2c])
                    p.mm(LR2[0].ap, C["ones"].t[:, 0:1], r2c.ap, LR2[0],
                         reads=[C["ones"], r2c], start=(m == 0), stop=(m == NDC - 1))
                    p.op("S", lambda e: e.activation(scr1.ap, r2c.ap, AF.Square),
                         reads=[r2c], writes=[scr1])
                    p.mm(LR2[1].ap, C["ones"].t[:, 0:1], scr1.ap, LR2[1],
                         reads=[C["ones"], scr1], start=(m == 0), stop=(m == NDC - 1))
                    p.dma("Yd", SC["r2"].t[m, :, t0:t0 + 512], r2c.ap,
                          reads=[r2c], writes=[SC["r2"]])
                    if tap_r2 is not None:
                        p.dma("Yd", tap_r2.t[m, :, t0:t0 + 512], r2c.ap,
                              reads=[r2c], writes=[tap_r2])
                p.op("V", lambda e, t0=t0: e.tensor_copy(
                        ln2s.t[:, t0:t0 + 512], LR2[0].ap),
                     reads=[LR2[0]], writes=[ln2s])
                p.op("V", lambda e, t0=t0: e.tensor_copy(
                        ln2s.t[:, T + t0:T + t0 + 512], LR2[1].ap),
                     reads=[LR2[1]], writes=[ln2s])
                p.barrier(("T", "V", "S"))
        p.barrier()



def _p5(p, nc, I, C, SC, ln2s, out, tap):
    with ExitStack() as e5:
        rows = p.sb("p5rows", [1, 3 * T], F32, es=e5)
        MR = rows.t[:, 0:T]; VR = rows.t[:, T:2*T]; RS = rows.t[:, 2*T:3*T]
        mb = p.sb("p5mb", [128, T], F32, es=e5)
        bco = p.sb("p5bco", [128, T], F32, es=e5)
        rc = [p.sb(f"p5rc{i}", [128, T], F32, es=e5) for i in range(2)]
        oc = p.sb("p5oc", [128, T], F32, es=e5)
        scr = p.sb("p5scr", [128, T], F32, es=e5)
        BC = p.ps("p5bc", [128, 512], F32, es=e5)

        p.op("V", lambda e: e.tensor_scalar(MR, ln2s.t[:, 0:T], 1.0 / D, None,
                                            ALU.mult),
             reads=[ln2s], writes=[rows])
        p.op("V", lambda e: e.tensor_scalar(VR, ln2s.t[:, T:2*T], 1.0 / D, None,
                                            ALU.mult),
             reads=[ln2s], writes=[rows])
        p.op("V", lambda e: e.tensor_tensor(RS, MR, MR, ALU.mult),
             reads=[rows], writes=[rows])
        p.op("V", lambda e: e.tensor_tensor(VR, VR, RS, ALU.subtract),
             reads=[rows], writes=[rows])
        p.op("S", lambda e: e.activation(RS, VR, AF.Ln, bias=C["epsb"].t[0:1, 1:2]),
             reads=[rows, C["epsb"]], writes=[rows])
        p.op("S", lambda e: e.activation(RS, RS, AF.Exp, scale=-0.5),
             reads=[rows], writes=[rows])
        for src_sl, dst in ((MR, mb), (RS, bco)):
            for th in range(2):
                p.mm(BC.ap, C["ones"].t[0:1, :], src_sl[:, 512*th:512*(th+1)],
                     BC, reads=[C["ones"], rows])
                p.op("V", lambda e, th=th, dst=dst: e.tensor_copy(
                        dst.t[:, 512*th:512*(th+1)], BC.ap),
                     reads=[BC], writes=[dst])
        for m in range(NDC):
            rb = rc[m % 2]
            p.dma("Yd", rb.ap, SC["r2"].t[m, :, :], reads=[SC["r2"]], writes=[rb])
            p.op("V", lambda e, rb=rb: e.tensor_tensor(scr.ap, rb.ap, mb.ap,
                                                       ALU.subtract),
                 reads=[rb, mb], writes=[scr])
            p.op("V", lambda e: e.tensor_tensor(oc.ap, scr.ap, bco.ap, ALU.mult),
                 reads=[scr, bco], writes=[oc])
            p.dma("Yd", out.t[m, :, :], oc.ap, reads=[oc], writes=[out])
        p.barrier()



def kernel(**inputs):
    shared, per_core = prep_inputs(inputs)
    nc, _ = build(phases=("P0", "P1", "P2", "P3", "P4", "P5"))
    in_maps = [dict(shared, **pc) for pc in per_core]
    res = run_bass_kernel_spmd(nc, in_maps, core_ids=list(range(NCORES)))
    outs = [np.asarray(res.results[c]["out"]).transpose(2, 0, 1).reshape(T, D)
            for c in range(NCORES)]
    return np.concatenate(outs, axis=0).reshape(1, S, D).astype(np.float32)


# version 18
# speedup vs baseline: 1.0892x; 1.0333x over previous
import numpy as np
import ml_dtypes
from contextlib import ExitStack

import concourse.bass as bass
import concourse.mybir as mybir
from concourse.bass_utils import run_bass_kernel_spmd

F32 = mybir.dt.float32
BF16 = mybir.dt.bfloat16
AF = mybir.ActivationFunctionType
ALU = mybir.AluOpType
AX = mybir.AxisListType
BFD = ml_dtypes.bfloat16

NCORES = 8
S = 8192
D = 2048
T = 1024
HALO = 512
TH = T + HALO
NH = 16
DH = 128
NDC = D // 128
W_SWA = 512
W_OM = 64
DECAY = 0.9
GGH = 64
MEMF = 8192
FFH = 5461
FFHP = 5504
DFF = 5632
NEXP = 8
R = 16
SCALING = 2.0
EPS_RMS = 1e-6
EPS_LN = 1e-5
SCALE = DH ** -0.5
NEG = -1e30


ENGINE_OF = {"T": "tensor", "V": "vector", "S": "scalar", "Sd": "scalar",
             "G": "gpsimd", "Gd": "gpsimd", "Yd": "sync"}

NSLOT = 8

class Q:
    def __init__(self, name, step):
        self.name = name
        self.step = step
        self.is_dma = name.endswith("d")
        if self.is_dma:
            self.slot_cnt = [0] * NSLOT
            self.issue = 0
        else:
            self.cnt = 0

class Buf:
    __slots__ = ("t", "w", "readers", "name")
    def __init__(self, t, name=""):
        self.t = t
        self.name = name
        self.w = None
        self.readers = {}
    def __getitem__(self, idx):
        return self.t[idx]
    @property
    def ap(self):
        return self.t[:]

class Prog:
    def __init__(self, nc, es):
        self.nc = nc
        self.es = es
        self.queues = {n: Q(n, 16 if n.endswith("d") else 1)
                       for n in ("T", "V", "S", "G", "Yd", "Gd", "Sd")}
        self.sems = {}
        for n, q in self.queues.items():
            if q.is_dma:
                for s in range(NSLOT):
                    self.sems[f"{n}{s}"] = es.enter_context(nc.semaphore(f"sem_{n}{s}"))
            else:
                self.sems[n] = es.enter_context(nc.semaphore(f"sem_{n}"))
        self.streams = {e: [] for e in ("tensor", "vector", "scalar", "gpsimd", "sync")}
        self.hw = {e: {} for e in self.streams}

    def sb(self, name, shape, dt, es=None):
        return Buf((es or self.es).enter_context(
            self.nc.sbuf_tensor(name, list(shape), dt)), name)

    def ps(self, name, shape, dt=F32, es=None):
        return Buf((es or self.es).enter_context(
            self.nc.psum_tensor(name, list(shape), dt)), name)

    def dram(self, name, shape, dt):
        return Buf(self.nc.dram_tensor(name, list(shape), dt), name)

    def _edges(self, qname, reads, writes, skip_self_T=True):
        eng = ENGINE_OF[qname]
        need = {}
        def upd(e):
            if e is None:
                return
            sk, c = e
            if sk == "T" and qname == "T" and skip_self_T:
                return
            if need.get(sk, -1) < c:
                need[sk] = c
        for b in reads:
            upd(b.w)
        for b in writes:
            upd(b.w)
            for sk, c in b.readers.items():
                upd((sk, c))
        hw = self.hw[eng]
        out = []
        for sk in sorted(need):
            c = need[sk]
            if c <= 0 or hw.get(sk, 0) >= c:
                continue
            out.append((sk, c))
            hw[sk] = c
        return out

    def _push(self, qname, waits, emit, incsem):
        self.streams[ENGINE_OF[qname]].append((waits, emit, incsem))

    def _record(self, semkey, val, reads, writes):
        for b in writes:
            b.w = (semkey, val)
            b.readers = {}
        for b in reads:
            if b.readers.get(semkey, -1) < val:
                b.readers[semkey] = val

    def op(self, qname, emit, reads=(), writes=()):
        q = self.queues[qname]
        assert not q.is_dma
        waits = self._edges(qname, reads, writes)
        q.cnt += 1
        self._push(qname, waits, emit, (qname, 1))
        self._record(qname, q.cnt, reads, writes)

    def dma(self, qname, dst_ap, src_ap, reads=(), writes=()):
        q = self.queues[qname]
        assert q.is_dma
        slot = q.issue % NSLOT
        q.issue += 1
        semkey = f"{qname}{slot}"
        waits = []
        prior = q.slot_cnt[slot] * 16
        eng = ENGINE_OF[qname]
        if prior > 0 and self.hw[eng].get(semkey, 0) < prior:
            waits.append((semkey, prior))
            self.hw[eng][semkey] = prior
        waits += self._edges(qname, reads, writes)
        q.slot_cnt[slot] += 1
        self._push(qname, waits,
                   lambda e: e.dma_start(dst_ap, src_ap), (semkey, 16))
        self._record(semkey, q.slot_cnt[slot] * 16, reads, writes)

    def mm(self, out_ap, lhsT_ap, rhs_ap, psum_buf, reads=(), start=True, stop=True):
        self.op("T", lambda e: e.matmul(out_ap, lhsT_ap, rhs_ap, start=start, stop=stop),
                reads=list(reads), writes=[psum_buf])

    def mm_group(self, psum_buf, mms, reads=(), start=True, stop=True):
        waits = self._edges("T", list(reads), [psum_buf])
        q = self.queues["T"]
        q.cnt += 1
        n = len(mms)
        def emit(e, mms=tuple(mms), n=n, start=start, stop=stop):
            ins = None
            for i, (o, l, r) in enumerate(mms):
                ins = e.matmul(o, l, r, start=(start and i == 0),
                               stop=(stop and i == n - 1))
            return ins
        self._push("T", waits, emit, ("T", 1))
        self._record("T", q.cnt, list(reads), [psum_buf])

    def _cur_semvals(self):
        vals = {}
        for n, q in self.queues.items():
            if q.is_dma:
                for s in range(NSLOT):
                    if q.slot_cnt[s] > 0:
                        vals[f"{n}{s}"] = q.slot_cnt[s] * 16
            else:
                if q.cnt > 0:
                    vals[n] = q.cnt
        return vals

    def wait_all(self, qname):
        eng = ENGINE_OF[qname]
        hw = self.hw[eng]
        waits = []
        own = {qname} if not self.queues[qname].is_dma else \
            {f"{qname}{s}" for s in range(NSLOT)}
        for sk, c in sorted(self._cur_semvals().items()):
            if sk in own:
                continue
            if hw.get(sk, 0) < c:
                waits.append((sk, c))
                hw[sk] = c
        if waits:
            self._push(qname, waits, None, None)

    def barrier(self, qnames=("T", "V", "S", "Yd", "Gd", "Sd")):
        for qa in qnames:
            self.wait_all(qa)

    def finalize(self, block):
        def mk(engname):
            items = self.streams[engname]
            def body(eng):
                for waits, emit, incsem in items:
                    for sk, c in waits:
                        eng.wait_ge(self.sems[sk], c)
                    if emit is not None:
                        ins = emit(eng)
                        if incsem is not None:
                            ins.then_inc(self.sems[incsem[0]], incsem[1])
            return body
        block.tensor(mk("tensor"))
        block.vector(mk("vector"))
        block.scalar(mk("scalar"))
        block.gpsimd(mk("gpsimd"))
        block.sync(mk("sync"))



def _rope_tables():
    inv_freq = 1.0 / (10000.0 ** (np.arange(0, DH, 2, dtype=np.float64) / DH))
    t = np.arange(-HALO, S, dtype=np.float64)
    freqs = t[:, None] * inv_freq[None, :]
    cos = np.cos(freqs)
    sin = np.sin(freqs)
    cosT = np.concatenate([cos.T, cos.T], axis=0)
    sinT = np.concatenate([-sin.T, sin.T], axis=0)
    return cosT, sinT

def _masks(core):
    i = np.arange(128)[:, None]
    j = np.arange(128)[None, :]
    m_far = np.where(j > i, 0.0, NEG)
    m_causal = np.where(j <= i, 0.0, NEG)
    rel4 = i - j
    rel3 = i + 128 - j
    m_om4 = np.where((rel4 >= 0) & (rel4 < W_OM), 0.0, NEG)
    m_om3 = np.where((rel3 >= 0) & (rel3 < W_OM), 0.0, NEG)
    d_om4 = np.where((rel4 >= 0) & (rel4 < W_OM),
                     DECAY ** np.clip(rel4, 0, 255), 0.0)
    d_om3 = np.where((rel3 >= 0) & (rel3 < W_OM),
                     DECAY ** np.clip(rel3, 0, 255), 0.0)
    swa_cat = np.zeros((8, 128, 640))
    om_cat = np.zeros((8, 128, 256))
    for qi in range(8):
        for b in range(5):
            blk = np.zeros((128, 128))
            if b == 0:
                blk = blk + m_far
            elif b == 4:
                blk = blk + m_causal
            if core == 0 and (qi + b) < 4:
                blk = blk + NEG
            swa_cat[qi, :, 128*b:128*(b+1)] = blk
        om_cat[qi, :, 0:128] = m_om3 + (NEG if (core == 0 and qi == 0) else 0.0)
        om_cat[qi, :, 128:256] = m_om4
    np.clip(swa_cat, -3e38, 0, out=swa_cat)
    np.clip(om_cat, -3e38, 0, out=om_cat)
    d_cat = np.concatenate([d_om3, d_om4], axis=1)
    return dict(
        swa_cat=np.ascontiguousarray(
            swa_cat.transpose(1, 0, 2).reshape(128, 8 * 640)).astype(BFD),
        om_cat=np.ascontiguousarray(
            om_cat.transpose(1, 0, 2).reshape(128, 8 * 256)).astype(BFD),
        d_cat=d_cat.astype(np.float32))

def prep_inputs(inp):
    x = np.asarray(inp["L0_attn_x"])[0]
    shared = {}
    wqkv = np.asarray(inp["L0_attn_w_qkv"])
    cols = []
    for h in range(NH):
        cols.append(wqkv[:, h*DH:(h+1)*DH])
        cols.append(wqkv[:, D + h*DH:D + (h+1)*DH])
        cols.append(wqkv[:, 2*D + h*DH:2*D + (h+1)*DH])
    shared["wqkv_p"] = np.ascontiguousarray(np.concatenate(cols, axis=1))
    shared["qscale"] = np.ascontiguousarray(np.asarray(inp["L0_attn_q_scale"]).reshape(DH, 1))
    shared["kscale"] = np.ascontiguousarray(np.asarray(inp["L0_attn_k_scale"]).reshape(DH, 1))
    shared["n1w"] = np.ascontiguousarray(np.asarray(inp["L0_attn_norm1_w"]).reshape(NDC, 128).T)
    shared["n2w"] = np.ascontiguousarray(np.asarray(inp["L0_attn_norm2_w"]).reshape(NDC, 128).T)
    shared["gg1"] = np.asarray(inp["L0_attn_gg_w1"])
    shared["gg2"] = np.asarray(inp["L0_attn_gg_w2"])
    shared["wo"] = np.asarray(inp["L0_attn_w_o"])
    shared["mw1"] = np.asarray(inp["L0_attn_mem_w1"])
    shared["mw2"] = np.asarray(inp["L0_attn_mem_w2"])
    z = np.zeros((D, FFHP - FFH), np.float32)
    shared["fw1"] = np.ascontiguousarray(np.concatenate([np.asarray(inp["L0_attn_ffn_w1"]), z], axis=1))
    shared["fw3"] = np.ascontiguousarray(np.concatenate([np.asarray(inp["L0_attn_ffn_w3"]), z], axis=1))
    shared["fw2"] = np.ascontiguousarray(np.concatenate(
        [np.asarray(inp["L0_attn_ffn_w2"]), np.zeros((FFHP - FFH, D), np.float32)], axis=0))
    shared["wg"] = np.ascontiguousarray(np.asarray(inp["L0_ffn_Wg"]).T)
    shared["wu"] = np.ascontiguousarray(np.asarray(inp["L0_ffn_Wu"]).T)
    shared["wd"] = np.ascontiguousarray(np.asarray(inp["L0_ffn_Wd"]).T)
    shared["wr"] = np.ascontiguousarray(np.asarray(inp["L0_ffn_Wr"]).T)
    Ag = np.asarray(inp["L0_ffn_Ag"])
    Au = np.asarray(inp["L0_ffn_Au"])
    a_all = np.zeros((D, 8 * 128), np.float32)
    for t in range(4):
        a_all[:, 128*t + 0:128*t + 16] = Ag[2*t].T
        a_all[:, 128*t + 64:128*t + 80] = Ag[2*t + 1].T
        base = 512 + 128*t
        a_all[:, base + 0:base + 16] = Au[2*t].T
        a_all[:, base + 64:base + 80] = Au[2*t + 1].T
    shared["a_all"] = a_all
    shared["bg"] = np.ascontiguousarray(np.asarray(inp["L0_ffn_Bg"]).transpose(0, 2, 1))
    shared["bu"] = np.ascontiguousarray(np.asarray(inp["L0_ffn_Bu"]).transpose(0, 2, 1))
    shared["ad"] = np.ascontiguousarray(np.asarray(inp["L0_ffn_Ad"]).transpose(0, 2, 1))
    shared["bd"] = np.ascontiguousarray(
        np.asarray(inp["L0_ffn_Bd"]).transpose(0, 2, 1).reshape(NEXP * R, D))
    shared["ident"] = np.eye(128, dtype=BFD)
    shared["identf"] = np.eye(128, dtype=np.float32)
    shared["ones"] = np.ones((128, 128), np.float32)
    shared["onesb"] = np.ones((128, 128), dtype=BFD)
    epsb = np.zeros((128, 2), np.float32); epsb[:, 0] = EPS_RMS; epsb[:, 1] = EPS_LN
    shared['epsb'] = epsb

    cosT, sinT = _rope_tables()
    per_core = []
    for c in range(NCORES):
        gs = c * T
        m = {}
        lo = gs - HALO
        if lo < 0:
            xs = np.concatenate([np.zeros((HALO, D), np.float32),
                                 x[0:gs + T]], axis=0)
        else:
            xs = x[lo:gs + T]
        m["x_f"] = np.ascontiguousarray(xs.T)
        m["cosq"] = np.ascontiguousarray(cosT[:, HALO + gs: HALO + gs + T] * SCALE).astype(BFD)
        m["sinq"] = np.ascontiguousarray(sinT[:, HALO + gs: HALO + gs + T] * SCALE).astype(BFD)
        m["cosk"] = np.ascontiguousarray(cosT[:, gs: gs + TH]).astype(BFD)
        m["sink"] = np.ascontiguousarray(sinT[:, gs: gs + TH]).astype(BFD)
        m.update(_masks(c))
        per_core.append(m)
    return shared, per_core



IN_SPECS = [
    ("x_f", [D, TH], F32), ("wqkv_p", [D, 3 * D], F32),
    ("qscale", [DH, 1], F32), ("kscale", [DH, 1], F32),
    ("n1w", [128, NDC], F32), ("n2w", [128, NDC], F32),
    ("gg1", [D, GGH], F32), ("gg2", [GGH, 1], F32),
    ("wo", [D, D], F32), ("mw1", [D, MEMF], F32), ("mw2", [MEMF, D], F32),
    ("fw1", [D, FFHP], F32), ("fw3", [D, FFHP], F32), ("fw2", [FFHP, D], F32),
    ("wg", [D, DFF], F32), ("wu", [D, DFF], F32), ("wd", [DFF, D], F32),
    ("wr", [D, NEXP], F32), ("a_all", [D, 1024], F32),
    ("bg", [NEXP, R, DFF], F32), ("bu", [NEXP, R, DFF], F32),
    ("ad", [NEXP, DFF, R], F32), ("bd", [NEXP * R, D], F32),
    ("ident", [128, 128], BF16), ("identf", [128, 128], F32),
    ("onesb", [128, 128], BF16),
    ("ones", [128, 128], F32),
    ("swa_cat", [128, 8 * 640], BF16), ("om_cat", [128, 8 * 256], BF16),
    ("d_cat", [128, 256], F32),
    ("cosq", [128, T], BF16), ("sinq", [128, T], BF16),
    ("cosk", [128, TH], BF16), ("sink", [128, TH], BF16),
    ("epsb", [128, 2], F32),
]

CONST_NAMES = ["ident", "identf", "ones", "onesb", "n1w", "n2w", "qscale", "kscale", "epsb"]
P1_CONST_NAMES = ["swa_cat", "om_cat", "d_cat",
                  "cosq", "sinq", "cosk", "sink"]

def build(phases=("P0",), debug_taps=()):
    nc = bass.Bass("TRN2", target_bir_lowering=False)
    es = ExitStack()
    p = Prog(nc, es)
    dbg = {}

    I = {nm: Buf(nc.dram_tensor(nm, list(shape), dt, kind="ExternalInput"), nm)
         for nm, shape, dt in IN_SPECS}
    out = Buf(nc.dram_tensor("out", [NDC, 128, T], F32, kind="ExternalOutput"), "out")

    SC = {
        "attn": p.dram("sc_attn", [NH, 128, T], BF16),
        "xa": p.dram("sc_xa", [NDC, 128, T], F32),
        "r": p.dram("sc_r", [NDC, 128, T], F32),
        "x1b": p.dram("sc_x1b", [NDC, 128, T], BF16),
        "x1f": p.dram("sc_x1f", [NDC, 128, T], F32),
        "r2": p.dram("sc_r2", [NDC, 128, T], F32),
    }

    def tap(name, shape, dt=F32):
        if name in debug_taps:
            b = Buf(nc.dram_tensor("dbg_" + name, list(shape), dt,
                                   kind="ExternalOutput"), "dbg_" + name)
            dbg[name] = b
            return b
        return None

    with es:
        C = {}
        for nm in CONST_NAMES:
            shape = dict((n, s) for n, s, _ in IN_SPECS)[nm]
            dt = dict((n, d) for n, s, d in IN_SPECS)[nm]
            C[nm] = p.sb("c_" + nm, shape, dt)
            p.dma("Yd", C[nm].ap, I[nm].ap, writes=[C[nm]])

        es_p03 = es.enter_context(ExitStack())
        rms2ss = p.sb("rms2ss", [1, T], F32, es=es_p03)

        with ExitStack() as es_p02:
            qmem = [p.sb(f"qmem{h}", [128, T], BF16, es=es_p02) for h in range(NH)]
            hbuf = [p.sb(f"h{i}", [128, TH], BF16, es=es_p02) for i in range(NDC)]
            gammaT = p.sb("gammaT", [128, 12], F32, es=es_p02)

            if "P0" in phases:
                _p0(p, nc, I, C, hbuf, gammaT, tap)
            p1sel = [ph for ph in phases if ph.startswith("P1")]
            if p1sel:
                parts = p1sel[0].split(":")
                upto = parts[1] if len(parts) > 1 else "full"
                nh = int(parts[2]) if len(parts) > 2 else NH
                _p1(p, nc, I, C, hbuf, gammaT, qmem, SC, tap,
                    heads=range(nh), upto=upto)

            if "P2" in phases:
                _p2(p, nc, I, C, hbuf, qmem, SC, rms2ss, tap)

        if "P3" in phases:
            _p3(p, nc, I, C, SC, rms2ss, tap)
        es_p03.close()

        ln2s = p.sb("ln2s", [1, 2 * T], F32)
        if "P4" in phases:
            _p4(p, nc, I, C, SC, ln2s, tap)
        if "P5" in phases:
            _p5(p, nc, I, C, SC, ln2s, out, tap)

        if "OUT_STUB" in phases:
            with ExitStack() as e2:
                z = p.sb("zstub", [128, D], F32, es=e2)
                p.op("V", lambda e: e.memset(z.ap, 0.0), writes=[z])
                for i in range(T // 128):
                    p.dma("Yd", out.t[i * 128:(i + 1) * 128, :], z.ap,
                          reads=[z], writes=[out])

        p.barrier()

        with nc.Block() as block:
            p.finalize(block)
    return nc, dbg



def _p0(p, nc, I, C, hbuf, gammaT, tap):
    with ExitStack() as e0:
        xs = [p.sb(f"p0x{i}", [128, TH], F32, es=e0) for i in range(2)]
        sqs = [p.sb(f"p0sq{i}", [128, TH], BF16, es=e0) for i in range(2)]
        rstd_row = p.sb("p0rstd", [1, TH], F32, es=e0)
        rstd_sb = p.sb("p0rbs", [128, TH], F32, es=e0)
        gg1b = p.sb("p0gg1", [128, NDC * GGH], BF16, es=e0)
        gg2b = p.sb("p0gg2", [GGH, 1], BF16, es=e0)
        gsil = p.sb("p0gs", [GGH, TH], BF16, es=e0)

        p.dma("Gd", gg1b.t[:].rearrange("p (k c) -> p k c", c=GGH),
              I["gg1"].ap.rearrange("(k p) c -> p k c", p=128), writes=[gg1b])
        p.dma("Gd", gg2b.ap, I["gg2"].ap, writes=[gg2b])

        with ExitStack() as eA:
            ssum = p.ps("p0ss", [1, TH], F32, es=eA)
            rstd_b = p.ps("p0rb", [128, TH], F32, es=eA)
            for i in range(NDC):
                xb, sq = xs[i % 2], sqs[i % 2]
                p.dma("Yd", xb.ap, I["x_f"].t[i * 128:(i + 1) * 128, :], writes=[xb])
                p.op("S", lambda e, xb=xb, sq=sq: e.activation(sq.ap, xb.ap, AF.Square),
                     reads=[xb], writes=[sq])
                for t0 in range(0, TH, 512):
                    p.mm(ssum.t[:, t0:t0 + 512], C["onesb"].t[:, 0:1],
                         sq.t[:, t0:t0 + 512], ssum, reads=[C["onesb"], sq],
                         start=(i == 0), stop=(i == NDC - 1))
            p.op("S", lambda e: e.activation(rstd_row.ap, ssum.ap, AF.Ln,
                                             bias=C['epsb'].t[0:1, 0:1], scale=1.0 / D),
                 reads=[ssum], writes=[rstd_row])
            p.op("S", lambda e: e.activation(rstd_row.ap, rstd_row.ap, AF.Exp,
                                             scale=-0.5),
                 reads=[rstd_row], writes=[rstd_row])
            for t0 in range(0, TH, 512):
                p.mm(rstd_b.t[:, t0:t0 + 512], C["ones"].t[0:1, :],
                     rstd_row.t[:, t0:t0 + 512], rstd_b,
                     reads=[C["ones"], rstd_row], start=True, stop=True)
            p.op("V", lambda e: e.tensor_copy(rstd_sb.ap, rstd_b.ap),
                 reads=[rstd_b], writes=[rstd_sb])
            for i in range(NDC):
                xb = xs[i % 2]
                p.dma("Yd", xb.ap, I["x_f"].t[i * 128:(i + 1) * 128, :], writes=[xb])
                p.op("V", lambda e, i=i, xb=xb: e.scalar_tensor_tensor(
                        hbuf[i].ap, xb.ap, C["n1w"].t[:, i:i + 1], rstd_sb.ap,
                        ALU.mult, ALU.mult),
                     reads=[xb, C["n1w"], rstd_sb], writes=[hbuf[i]])
            p.barrier()
        with ExitStack() as eB:
            gpre = p.ps("p0gp", [GGH, TH], F32, es=eB)
            grow = p.ps("p0gr", [128, 12], F32, es=eB)
            gg1v = gg1b.t[:].rearrange("p (k c) -> p k c", c=GGH)
            for t0 in range(0, TH, 512):
                p.mm_group(gpre,
                           [(gpre.t[:, t0:t0 + 512], gg1v[:, i, :],
                             hbuf[i].t[:, t0:t0 + 512]) for i in range(NDC)],
                           reads=[gg1b] + hbuf)
            p.op("S", lambda e: e.activation(gsil.ap, gpre.ap, AF.Silu),
                 reads=[gpre], writes=[gsil])
            for j in range(12):
                p.mm(grow.t[:, j:j + 1], gsil.t[:, j * 128:(j + 1) * 128], gg2b.ap,
                     grow, reads=[gsil, gg2b], start=True, stop=True)
            p.op("S", lambda e: e.activation(gammaT.ap, grow.ap, AF.Sigmoid),
                 reads=[grow], writes=[gammaT])

            tp = tap("h", [NDC, 128, TH], BF16)
            if tp is not None:
                for i in range(NDC):
                    p.dma("Yd", tp.t[i, :, :], hbuf[i].ap, reads=[hbuf[i]], writes=[tp])
            tp = tap("gammaT", [128, 12])
            if tp is not None:
                p.dma("Yd", tp.ap, gammaT.ap, reads=[gammaT], writes=[tp])
            p.barrier()


def _p1(p, nc, I, C, hbuf, gammaT, qmem, SC, tap, heads=None, upto="full"):
    with ExitStack() as e1:
        ispec = {nm: (s, d) for nm, s, d in IN_SPECS}
        for nm in P1_CONST_NAMES:
            C[nm] = p.sb("c_" + nm, ispec[nm][0], ispec[nm][1], es=e1)
            p.dma("Yd", C[nm].ap, I[nm].ap, writes=[C[nm]])
        wb = [p.sb(f"p1w{i}", [128, NDC * 384], BF16, es=e1) for i in range(2)]
        q_work = p.sb("p1qw", [128, T], F32, es=e1)
        k_work = p.sb("p1kw", [128, TH], F32, es=e1)
        row = p.sb("p1row", [1, 512], F32, es=e1)
        rowb = p.sb("p1rowb", [1, 512], BF16, es=e1)
        q1 = p.sb("p1q1", [128, T], F32, es=e1)
        k1 = p.sb("p1k1", [128, TH], F32, es=e1)
        t1 = p.sb("p1t1", [128, TH], F32, es=e1)
        t2 = p.sb("p1t2", [128, TH], F32, es=e1)
        sq = p.sb("p1sqb", [128, 512], BF16, es=e1)
        qr = t2
        kr = t2
        q2 = p.sb("p1q2", [128, T], BF16, es=e1)
        k2 = p.sb("p1k2", [128, TH], BF16, es=e1)
        vts = [p.sb(f"p1vt{i}", [128, 12 * 128], BF16, es=e1) for i in range(2)]
        kpts = [p.sb(f"p1kpt{i}", [128, 12 * 128], BF16, es=e1) for i in range(2)]
        sms_ = [p.sb(f"p1sm{i}", [128, 640], F32, es=e1) for i in range(2)]
        pns = [p.sb(f"p1pn{i}", [128, 640], BF16, es=e1) for i in range(2)]
        oms = [p.sb(f"p1om{i}", [128, 256], F32, es=e1) for i in range(2)]
        poms = [p.sb(f"p1pom{i}", [128, 256], BF16, es=e1) for i in range(2)]
        pTs = [p.sb(f"p1pT{i}", [128, 896], BF16, es=e1) for i in range(2)]
        attnhs = [p.sb(f"p1ah{i}", [128, T], BF16, es=e1) for i in range(2)]
        mcols = [p.sb(f"p1m{i}", [128, 1], F32, es=e1) for i in range(2)]
        sumcs = [p.sb(f"p1su{i}", [128, 1], F32, es=e1) for i in range(2)]
        osums = [p.sb(f"p1os{i}", [128, 1], F32, es=e1) for i in range(2)]
        recs = [p.sb(f"p1re{i}", [128, 1], F32, es=e1) for i in range(2)]
        orecs = [p.sb(f"p1or{i}", [128, 1], F32, es=e1) for i in range(2)]
        PJ = [p.ps(f"p1pj{i}", [128, 512], F32, es=e1) for i in range(2)]
        PS = [p.ps(f"p1ps{i}", [128, 640], F32, es=e1) for i in range(2)]
        TP = p.ps("p1ptp", [128, 896], BF16, es=e1)
        PO = p.ps("p1po", [128, 512], F32, es=e1)
        ROW = PJ[0]
        BC = PJ[1]

        def wdma(hd, dst):
            src = I["wqkv_p"].t[:, 384 * hd:384 * (hd + 1)].rearrange(
                "(k p) c -> p k c", p=128)
            p.dma("Gd", dst.t[:].rearrange("p (k c) -> p k c", c=384), src,
                  writes=[dst])

        wdma(0, wb[0])
        tap_q2 = tap("q2h0", [128, T], BF16)
        tap_k2 = tap("k2h0", [128, TH], BF16)
        tap_attn = tap("attn", [NH, 128, T], BF16)
        tap_qm = tap("qmemt", [NH, 128, T], BF16)

        def proj_items(hd):
            wcur = wb[hd % 2]
            wv = wcur.t[:].rearrange("p (k c) -> p k c", c=384)
            vt = vts[hd % 2]
            items = []
            pjc = [0]
            def nxt():
                b = PJ[pjc[0] % 2]
                pjc[0] += 1
                return b
            for tt in range(2):
                def it(tt=tt):
                    PJb = nxt()
                    p.mm_group(PJb, [(PJb.ap, wv[:, k, 0:128],
                                      hbuf[k].t[:, HALO + 512 * tt:HALO + 512 * (tt + 1)])
                                     for k in range(NDC)], reads=[wcur] + hbuf)
                    p.op("S", lambda e, tt=tt, PJb=PJb: e.copy(
                            q_work.t[:, 512 * tt:512 * (tt + 1)], PJb.ap),
                         reads=[PJb], writes=[q_work])
                items.append(it)
            for tt in range(3):
                def it(tt=tt):
                    PJb = nxt()
                    p.mm_group(PJb, [(PJb.ap, wv[:, k, 128:256],
                                      hbuf[k].t[:, 512 * tt:512 * (tt + 1)])
                                     for k in range(NDC)], reads=[wcur] + hbuf)
                    p.op("S", lambda e, tt=tt, PJb=PJb: e.copy(
                            k_work.t[:, 512 * tt:512 * (tt + 1)], PJb.ap),
                         reads=[PJb], writes=[k_work])
                items.append(it)
            for j in range(12):
                def it(j=j):
                    PJb = nxt()
                    p.mm_group(PJb, [(PJb.t[:, 0:128],
                                      hbuf[k].t[:, 128 * j:128 * (j + 1)],
                                      wv[:, k, 256:384]) for k in range(NDC)],
                               reads=[wcur] + hbuf)
                    p.op("S", lambda e, j=j, PJb=PJb, vt=vt: e.copy(
                            vt.t[:, 128 * j:128 * (j + 1)], PJb.t[:, 0:128]),
                         reads=[PJb], writes=[vt])
                items.append(it)
            return items

        def proj(hd):
            for it in proj_items(hd):
                it()

        hlist = list(heads if heads is not None else range(NH))
        proj(hlist[0])
        if len(hlist) > 1:
            wdma(hlist[1], wb[hlist[1] % 2])
        for hd in hlist:
            vt = vts[hd % 2]
            kpt = kpts[hd % 2]
            attnh = attnhs[hd % 2]
            nxt_items = []
            if hd + 1 < NH:
                nxt_items = proj_items(hd + 1)
                if hd + 2 < NH:
                    wdma(hd + 2, wb[(hd + 2) % 2])

            def headnorm(work, scale_c, out_b, n):
                for tt in range(n // 512):
                    p.op("S", lambda e, tt=tt: e.activation(
                            sq.ap, work.t[:, 512 * tt:512 * (tt + 1)], AF.Square),
                         reads=[work], writes=[sq])
                    p.mm(ROW.t[0:1, 0:512], C["onesb"].t[:, 0:1],
                         sq.ap, ROW, reads=[C["onesb"], sq])
                    p.op("S", lambda e: e.activation(
                            row.ap, ROW.t[0:1, 0:512],
                            AF.Ln, bias=C["epsb"].t[0:1, 0:1], scale=1.0 / DH),
                         reads=[ROW, C["epsb"]], writes=[row])
                    p.op("S", lambda e: e.activation(rowb.ap, row.ap,
                                                     AF.Exp, scale=-0.5),
                         reads=[row], writes=[rowb])
                    p.mm(BC.t[:, 0:512], C["onesb"].t[0:1, :], rowb.ap,
                         BC, reads=[C["onesb"], rowb])
                    p.op("V", lambda e, tt=tt: e.scalar_tensor_tensor(
                            out_b.t[:, 512 * tt:512 * (tt + 1)],
                            work.t[:, 512 * tt:512 * (tt + 1)],
                            C[scale_c].ap, BC.t[:, 0:512], ALU.mult, ALU.mult),
                         reads=[work, C[scale_c], BC], writes=[out_b])
            if hd == 0:
                tpx = tap("qworkh0", [128, T], F32)
                if tpx is not None:
                    p.dma("Yd", tpx.ap, q_work.ap, reads=[q_work], writes=[tpx])
            if upto == "qwork":
                continue
            headnorm(q_work, "qscale", q1, T)
            if hd == 0:
                tpx = tap("q1h0", [128, T], F32)
                if tpx is not None:
                    p.dma("Yd", tpx.ap, q1.ap, reads=[q1], writes=[tpx])
                tpx = tap("rowh0", [1, T], F32)
                if tpx is not None:
                    p.dma("Yd", tpx.ap, row.t[:, 0:T], reads=[row], writes=[tpx])
            headnorm(k_work, "kscale", k1, TH)

            def rope(src, rot, cos_c, sin_c, out_b, n):
                p.op("V", lambda e: e.tensor_copy(rot.t[0:64, 0:n], src.t[64:128, 0:n]),
                     reads=[src], writes=[rot])
                p.op("V", lambda e: e.tensor_copy(rot.t[64:128, 0:n], src.t[0:64, 0:n]),
                     reads=[src], writes=[rot])
                p.op("V", lambda e: e.tensor_tensor(t1.t[:, 0:n], src.t[:, 0:n],
                                                    C[cos_c].t[:, 0:n], ALU.mult),
                     reads=[src, C[cos_c]], writes=[t1])
                p.op("V", lambda e: e.tensor_tensor(t2.t[:, 0:n], rot.t[:, 0:n],
                                                    C[sin_c].t[:, 0:n], ALU.mult),
                     reads=[rot, C[sin_c]], writes=[t2])
                p.op("V", lambda e: e.tensor_tensor(out_b.t[:, 0:n], t1.t[:, 0:n],
                                                    t2.t[:, 0:n], ALU.add),
                     reads=[t1, t2], writes=[out_b])
            if upto == "norm":
                continue
            for _ in range(6):
                if nxt_items:
                    nxt_items.pop(0)()
            rope(q1, qr, "cosq", "sinq", q2, T)
            rope(k1, kr, "cosk", "sink", k2, TH)

            for j in range(12):
                sl = 128 * (j % 7)
                p.op("T", lambda e, j=j, sl=sl: e.transpose(
                        TP.t[:, sl:sl + 128], k2.t[:, 128 * j:128 * (j + 1)],
                        C["ident"].ap),
                     reads=[k2, C["ident"]], writes=[TP])
                p.op("S", lambda e, j=j, sl=sl, kpt=kpt: e.activation(
                        kpt.t[:, 128 * j:128 * (j + 1)], TP.t[:, sl:sl + 128],
                        AF.Copy, scale=gammaT.t[:, j:j + 1]),
                     reads=[TP, gammaT], writes=[kpt])

            def stA(qi):
                ks = 128 * qi
                par = qi % 2
                PSb = PS[par]
                p.mm(PSb.t[:, 0:512], q2.t[:, ks:ks + 128],
                     k2.t[:, ks:ks + 512], PSb, reads=[q2, k2])
                p.mm(PSb.t[:, 512:640], q2.t[:, ks:ks + 128],
                     k2.t[:, ks + 512:ks + 640], PSb, reads=[q2, k2])

            def stB(qi):
                par = qi % 2
                PSb, smb, pnb, omb, pomb = PS[par], sms_[par], pns[par], oms[par], poms[par]
                mcol, sumc, osum, rec, orec = (mcols[par], sumcs[par], osums[par],
                                               recs[par], orecs[par])
                p.op("V", lambda e, qi=qi, PSb=PSb, smb=smb: e.tensor_tensor(
                        smb.ap, PSb.ap, C["swa_cat"].t[:, 640 * qi:640 * (qi + 1)],
                        ALU.add),
                     reads=[PSb, C["swa_cat"]], writes=[smb])
                p.op("V", lambda e, smb=smb, mcol=mcol: e.reduce_max(
                        mcol.ap, smb.ap, axis=AX.X, negate=True),
                     reads=[smb], writes=[mcol])
                p.op("S", lambda e, smb=smb, pnb=pnb, mcol=mcol, sumc=sumc:
                     e.activation(pnb.ap, smb.ap, AF.Exp, bias=mcol.ap,
                                  accum_out=sumc.ap),
                     reads=[smb, mcol], writes=[pnb, sumc])
                p.op("V", lambda e, rec=rec, sumc=sumc: e.reciprocal(rec.ap, sumc.ap),
                     reads=[sumc], writes=[rec])
                p.op("V", lambda e, pnb=pnb, rec=rec: e.tensor_scalar(
                        pnb.ap, pnb.ap, rec.ap, None, ALU.mult),
                     reads=[pnb, rec], writes=[pnb])
                p.op("V", lambda e, qi=qi, smb=smb, omb=omb: e.tensor_tensor(
                        omb.ap, smb.t[:, 384:640],
                        C["om_cat"].t[:, 256 * qi:256 * (qi + 1)], ALU.add),
                     reads=[smb, C["om_cat"]], writes=[omb])
                p.op("S", lambda e, omb=omb, pomb=pomb, mcol=mcol, osum=osum:
                     e.activation(pomb.ap, omb.ap, AF.Exp, bias=mcol.ap,
                                  accum_out=osum.ap),
                     reads=[omb, mcol], writes=[pomb, osum])
                p.op("V", lambda e, orec=orec, osum=osum: e.reciprocal(
                        orec.ap, osum.ap),
                     reads=[osum], writes=[orec])
                p.op("V", lambda e, pomb=pomb, orec=orec: e.scalar_tensor_tensor(
                        pomb.ap, pomb.ap, orec.ap, C["d_cat"].ap,
                        ALU.mult, ALU.mult),
                     reads=[pomb, orec, C["d_cat"]], writes=[pomb])

            def stC(qi, hd):
                ks = 128 * qi
                par = qi % 2
                pnb, pomb, pTb = pns[par], poms[par], pTs[par]
                for b in range(5):
                    p.op("T", lambda e, b=b, pnb=pnb: e.transpose(
                            TP.t[:, 128 * b:128 * (b + 1)],
                            pnb.t[:, 128 * b:128 * (b + 1)], C["ident"].ap),
                         reads=[pnb, C["ident"]], writes=[TP])
                for oi in range(2):
                    p.op("T", lambda e, oi=oi, pomb=pomb: e.transpose(
                            TP.t[:, 640 + 128 * oi:640 + 128 * (oi + 1)],
                            pomb.t[:, 128 * oi:128 * (oi + 1)], C["ident"].ap),
                         reads=[pomb, C["ident"]], writes=[TP])
                p.op("V", lambda e, pTb=pTb: e.tensor_copy(pTb.ap, TP.ap),
                     reads=[TP], writes=[pTb])
                o0 = 256 * par
                p.mm_group(PO, [(PO.t[:, o0:o0 + 128],
                                 vt.t[:, 128 * (qi + b):128 * (qi + b + 1)],
                                 pTb.t[:, 128 * b:128 * (b + 1)]) for b in range(5)],
                           reads=[vt, pTb])
                p.mm_group(PO, [(PO.t[:, o0 + 128:o0 + 256],
                                 kpt.t[:, 128 * (qi + b):128 * (qi + b + 1)],
                                 pTb.t[:, 640 + 128 * (b - 3):640 + 128 * (b - 2)])
                                for b in (3, 4)],
                           reads=[kpt, pTb])
                p.op("S", lambda e, ks=ks, o0=o0, attnh=attnh: e.copy(
                        attnh.t[:, ks:ks + 128], PO.t[:, o0:o0 + 128]),
                     reads=[PO], writes=[attnh])
                p.op("S", lambda e, ks=ks, hd=hd, o0=o0: e.copy(
                        qmem[hd].t[:, ks:ks + 128], PO.t[:, o0 + 128:o0 + 256]),
                     reads=[PO], writes=[qmem[hd]])

            stA(0)
            stB(0)
            stA(1)
            for qi in range(8):
                if qi + 1 < 8:
                    stB(qi + 1)
                if qi + 2 < 8:
                    stA(qi + 2)
                for _ in range(2):
                    if nxt_items:
                        nxt_items.pop(0)()
                stC(qi, hd)
            while nxt_items:
                nxt_items.pop(0)()

            p.dma("Yd", SC["attn"].t[hd, :, :], attnh.ap,
                  reads=[attnh], writes=[SC["attn"]])
            if tap_attn is not None:
                p.dma("Yd", tap_attn.t[hd, :, :], attnh.ap,
                      reads=[attnh], writes=[tap_attn])
            if tap_qm is not None:
                p.dma("Yd", tap_qm.t[hd, :, :], qmem[hd].ap,
                      reads=[qmem[hd]], writes=[tap_qm])
            if hd == 0:
                if tap_q2 is not None:
                    p.dma("Yd", tap_q2.ap, q2.ap, reads=[q2], writes=[tap_q2])
                if tap_k2 is not None:
                    p.dma("Yd", tap_k2.ap, k2.ap, reads=[k2], writes=[tap_k2])
                for nmt, bt, shp, dt in (("vth0", vt, [128, 1536], BF16),
                                         ("kpth0", kpt, [128, 1536], BF16)):
                    tpx = tap(nmt, shp, dt)
                    if tpx is not None:
                        p.dma("Yd", tpx.ap, bt.ap, reads=[bt], writes=[tpx])
        p.barrier()



def _p2(p, nc, I, C, hbuf, qmem, SC, rms2ss, tap):
    tap_m = tap("macc", [NDC, 128, T], F32)
    tap_xa = tap("xa", [NDC, 128, T], F32)
    tap_r = tap("r", [NDC, 128, T], F32)
    with ExitStack() as e2:
        w1b = [p.sb(f"p2w1{i}", [128, NDC * 128], BF16, es=e2) for i in range(2)]
        w2b = [p.sb(f"p2w2{i}", [128, 8 * 128], BF16, es=e2) for i in range(2)]
        wob = [p.sb(f"p2wo{i}", [128, NDC * 128], BF16, es=e2) for i in range(2)]
        macc = [p.sb(f"p2ma{m}", [128, T], F32, es=e2) for m in range(NDC)]
        poly = [p.sb(f"p2po{f}", [128, T], BF16, es=e2) for f in range(8)]
        scr = p.sb("p2scr", [128, 512], F32, es=e2)
        sqb = p.sb("p2sqb", [128, 512], BF16, es=e2)
        sg = p.sb("p2sg", [128, 512], F32, es=e2)
        t2b = sg
        xac = p.sb("p2xa", [128, T], F32, es=e2)
        rc = p.sb("p2rc", [128, T], F32, es=e2)
        xcs = [p.sb(f"p2xc{i}", [128, T], F32, es=e2) for i in range(2)]
        PH = [p.ps(f"p2ph{i}", [128, 512], F32, es=e2) for i in range(2)]
        PW = [p.ps(f"p2pw{i}", [128, 512], F32, es=e2) for i in range(2)]
        PA = [p.ps(f"p2pa{i}", [128, 512], F32, es=e2) for i in range(2)]
        RS = p.ps("p2rs", [1, T], F32, es=e2)

        def ldw1(idx, dst):
            p.dma("Gd", dst.t[:].rearrange("p (k c) -> p k c", c=128),
                  I["mw1"].t[:, 128 * idx:128 * (idx + 1)]
                  .rearrange("(k p) c -> p k c", p=128), writes=[dst])

        def ldw2(win, m, dst):
            p.dma("Gd", dst.t[:].rearrange("p (k c) -> p k c", c=128),
                  I["mw2"].t[1024 * win:1024 * (win + 1), 128 * m:128 * (m + 1)]
                  .rearrange("(k p) c -> p k c", p=128), writes=[dst])

        def ldwo(m, dst):
            p.dma("Gd", dst.t[:].rearrange("p (k c) -> p k c", c=128),
                  I["wo"].t[:, 128 * m:128 * (m + 1)]
                  .rearrange("(k p) c -> p k c", p=128), writes=[dst])

        for i in range(NDC):
            p.dma("Yd", hbuf[i].t[:, 0:T], SC["attn"].t[i, :, :],
                  reads=[SC["attn"]], writes=[hbuf[i]])
        ldw1(0, w1b[0])
        for win in range(8):
            for fs8 in range(8):
                idx = 8 * win + fs8
                wc = w1b[idx % 2]
                if idx + 1 < 64:
                    ldw1(idx + 1, w1b[(idx + 1) % 2])
                wv = wc.t[:].rearrange("p (k c) -> p k c", c=128)
                for th in range(2):
                    t0 = 512 * th
                    PHb = PH[th]
                    p.mm_group(PHb, [(PHb.ap, wv[:, k, :], qmem[k].t[:, t0:t0 + 512])
                                     for k in range(NDC)], reads=[wc] + qmem)
                    p.op("S", lambda e, PHb=PHb: e.activation(
                            scr.ap, PHb.ap, AF.Silu),
                         reads=[PHb], writes=[scr])
                    p.op("V", lambda e, fs8=fs8, t0=t0: e.scalar_tensor_tensor(
                            poly[fs8].t[:, t0:t0 + 512], scr.ap,
                            1.0, scr.ap, ALU.add, ALU.mult),
                         reads=[scr], writes=[poly[fs8]])
            if win == 0:
                ldw2(0, 0, w2b[0])
            for m in range(NDC):
                wc = w2b[m % 2]
                nwin, nm = (win, m + 1) if m + 1 < NDC else (win + 1, 0)
                if nwin < 8:
                    ldw2(nwin, nm, w2b[(m + 1) % 2])
                elif win == 7 and m == NDC - 1:
                    ldwo(0, wob[0])
                wv = wc.t[:].rearrange("p (k c) -> p k c", c=128)
                for th in range(2):
                    t0 = 512 * th
                    PWb = PW[th]
                    p.mm_group(PWb, [(PWb.ap, wv[:, k, :],
                                      poly[k].t[:, t0:t0 + 512])
                                     for k in range(8)], reads=[wc] + poly)
                    if win == 0:
                        p.op("V", lambda e, m=m, t0=t0, PWb=PWb: e.tensor_copy(
                                macc[m].t[:, t0:t0 + 512], PWb.ap),
                             reads=[PWb], writes=[macc[m]])
                    else:
                        p.op("V", lambda e, m=m, t0=t0, PWb=PWb: e.tensor_tensor(
                                macc[m].t[:, t0:t0 + 512],
                                macc[m].t[:, t0:t0 + 512], PWb.ap, ALU.add),
                             reads=[macc[m], PWb], writes=[macc[m]])
        p.dma("Yd", xcs[0].ap, I["x_f"].t[0:128, HALO:HALO + T], writes=[xcs[0]])
        for m in range(NDC):
            wc = wob[m % 2]
            if m + 1 < NDC:
                ldwo(m + 1, wob[(m + 1) % 2])
                p.dma("Yd", xcs[(m + 1) % 2].ap,
                      I["x_f"].t[128 * (m + 1):128 * (m + 2), HALO:HALO + T],
                      writes=[xcs[(m + 1) % 2]])
            wv = wc.t[:].rearrange("p (k c) -> p k c", c=128)
            xc = xcs[m % 2]
            for th in range(2):
                t0 = 512 * th
                PAb = PA[th]
                p.mm_group(PAb, [(PAb.ap, wv[:, k, :], hbuf[k].t[:, t0:t0 + 512])
                                 for k in range(NDC)], reads=[wc] + hbuf)
                p.op("S", lambda e, m=m, t0=t0: e.activation(
                        sg.ap, macc[m].t[:, t0:t0 + 512], AF.Sigmoid),
                     reads=[macc[m]], writes=[sg])
                p.op("V", lambda e, PAb=PAb: e.tensor_tensor(t2b.ap, sg.ap, PAb.ap,
                                                             ALU.mult),
                     reads=[sg, PAb], writes=[t2b])
                p.op("V", lambda e, xc=xc, t0=t0: e.tensor_tensor(
                        xac.t[:, t0:t0 + 512], xc.t[:, t0:t0 + 512], t2b.ap,
                        ALU.add),
                     reads=[xc, t2b], writes=[xac])
                p.op("V", lambda e, xc=xc, t0=t0: e.scalar_tensor_tensor(
                        rc.t[:, t0:t0 + 512], xc.t[:, t0:t0 + 512], 2.0, t2b.ap,
                        ALU.mult, ALU.add),
                     reads=[xc, t2b], writes=[rc])
                p.op("S", lambda e, t0=t0: e.activation(
                        sqb.ap, xac.t[:, t0:t0 + 512], AF.Square),
                     reads=[xac], writes=[sqb])
                p.mm(RS.t[:, t0:t0 + 512], C["onesb"].t[:, 0:1],
                     sqb.ap, RS, reads=[C["onesb"], sqb],
                     start=(m == 0), stop=(m == NDC - 1))
            p.dma("Yd", SC["xa"].t[m, :, :], xac.ap, reads=[xac], writes=[SC["xa"]])
            p.dma("Yd", SC["r"].t[m, :, :], rc.ap, reads=[rc], writes=[SC["r"]])
            if tap_m is not None:
                p.dma("Yd", tap_m.t[m, :, :], macc[m].ap,
                      reads=[macc[m]], writes=[tap_m])
            if tap_xa is not None:
                p.dma("Yd", tap_xa.t[m, :, :], xac.ap, reads=[xac], writes=[tap_xa])
            if tap_r is not None:
                p.dma("Yd", tap_r.t[m, :, :], rc.ap, reads=[rc], writes=[tap_r])
        p.op("V", lambda e: e.tensor_copy(rms2ss.ap, RS.ap),
             reads=[RS], writes=[rms2ss])
        p.barrier()



def _p3(p, nc, I, C, SC, rms2ss, tap):
    tap_y = tap("y", [NDC, 128, T], BF16)
    tap_rf = tap("rfin", [NDC, 128, T], F32)
    tap_x1 = tap("x1f", [NDC, 128, T], F32)
    NF3 = FFHP // 128
    with ExitStack() as e3:
        y = [p.sb(f"p3y{i}", [128, T], BF16, es=e3) for i in range(NDC)]
        g = [p.sb(f"p3g{f}", [128, T], BF16, es=e3) for f in range(NF3)]
        w1c = [p.sb(f"p3w1{i}", [128, NDC * 128], BF16, es=e3) for i in range(2)]
        w3c = [p.sb(f"p3w3{i}", [128, NDC * 128], BF16, es=e3) for i in range(2)]
        w2c = [p.sb(f"p3w2{i}", [128, NF3 * 128], BF16, es=e3) for i in range(2)]
        xc = [p.sb(f"p3xc{i}", [128, T], F32, es=e3) for i in range(2)]
        rfin = p.sb("p3rf", [128, T], F32, es=e3)
        scr = p.sb("p3scr", [128, T], F32, es=e3)
        sqb = p.sb("p3sqb", [128, 512], BF16, es=e3)
        rows = p.sb("p3rows", [1, 4 * T], F32, es=e3)
        R2 = rows.t[:, 0:T]; LS = rows.t[:, T:2*T]; LQ = rows.t[:, 2*T:3*T]
        MR = rows.t[:, 3*T:4*T]; R1 = R2
        bco = p.sb("p3bco", [128, T], F32, es=e3)
        mb = p.sb("p3mb", [128, T], F32, es=e3)
        x1b = p.sb("p3x1b", [128, T], BF16, es=e3)
        x1f = rfin
        BC = p.ps("p3bc", [128, 512], F32, es=e3)
        PH = [p.ps(f"p3ph{i}", [128, 512], F32, es=e3) for i in range(2)]
        LR = [p.ps(f"p3lr{i}", [1, 512], F32, es=e3) for i in range(4)]

        def bcast(src_slice, dst):
            for th in range(2):
                p.mm(BC.ap, C["ones"].t[0:1, :], src_slice[:, 512*th:512*(th+1)],
                     BC, reads=[C["ones"], rows])
                p.op("V", lambda e, th=th, dst=dst: e.tensor_copy(
                        dst.t[:, 512*th:512*(th+1)], BC.ap),
                     reads=[BC], writes=[dst])

        p.op("S", lambda e: e.activation(R2, rms2ss.ap, AF.Ln,
                                         bias=C["epsb"].t[0:1, 0:1], scale=1.0 / D),
             reads=[rms2ss, C["epsb"]], writes=[rows])
        p.op("S", lambda e: e.activation(R2, R2, AF.Exp, scale=-0.5),
             reads=[rows], writes=[rows])
        bcast(R2, bco)
        for i in range(NDC):
            xb = xc[i % 2]
            p.dma("Yd", xb.ap, SC["xa"].t[i, :, :], reads=[SC["xa"]], writes=[xb])
            p.op("V", lambda e, i=i, xb=xb: e.scalar_tensor_tensor(
                    y[i].ap, xb.ap, C["n2w"].t[:, i:i+1], bco.ap, ALU.mult, ALU.mult),
                 reads=[xb, C["n2w"], bco], writes=[y[i]])
            if tap_y is not None:
                p.dma("Yd", tap_y.t[i, :, :], y[i].ap, reads=[y[i]], writes=[tap_y])

        def ldw13(fs, b):
            for nm, dst in (("fw1", w1c[b]), ("fw3", w3c[b])):
                p.dma("Gd", dst.t[:].rearrange("p (k c) -> p k c", c=128),
                      I[nm].t[:, 128*fs:128*(fs+1)].rearrange("(k p) c -> p k c", p=128),
                      writes=[dst])
        ldw13(0, 0)
        for fs in range(NF3):
            if fs + 1 < NF3:
                ldw13(fs + 1, (fs + 1) % 2)
            wv1 = w1c[fs % 2].t[:].rearrange("p (k c) -> p k c", c=128)
            wv3 = w3c[fs % 2].t[:].rearrange("p (k c) -> p k c", c=128)
            for th in range(2):
                t0 = 512 * th
                p.mm_group(PH[0], [(PH[0].ap, wv1[:, k, :], y[k].t[:, t0:t0+512])
                                   for k in range(NDC)], reads=[w1c[fs % 2]] + y)
                p.op("S", lambda e, t0=t0: e.activation(scr.t[:, t0:t0+512],
                                                        PH[0].ap, AF.Silu),
                     reads=[PH[0]], writes=[scr])
                p.mm_group(PH[1], [(PH[1].ap, wv3[:, k, :], y[k].t[:, t0:t0+512])
                                   for k in range(NDC)], reads=[w3c[fs % 2]] + y)
                p.op("V", lambda e, fs=fs, t0=t0: e.tensor_tensor(
                        g[fs].t[:, t0:t0+512], scr.t[:, t0:t0+512], PH[1].ap,
                        ALU.mult),
                     reads=[scr, PH[1]], writes=[g[fs]])

        def ldw2(m, b):
            p.dma("Gd", w2c[b].t[:].rearrange("p (k c) -> p k c", c=128),
                  I["fw2"].t[:, 128*m:128*(m+1)].rearrange("(k p) c -> p k c", p=128),
                  writes=[w2c[b]])
        ldw2(0, 0)
        for m in range(NDC):
            if m + 1 < NDC:
                ldw2(m + 1, (m + 1) % 2)
            rb = xc[m % 2]
            p.dma("Yd", rb.ap, SC["r"].t[m, :, :], reads=[SC["r"]], writes=[rb])
            wv2 = w2c[m % 2].t[:].rearrange("p (k c) -> p k c", c=128)
            for th in range(2):
                t0 = 512 * th
                PHb = PH[th]
                p.mm_group(PHb, [(PHb.ap, wv2[:, k, :], g[k].t[:, t0:t0+512])
                                 for k in range(NF3)], reads=[w2c[m % 2]] + g)
                p.op("V", lambda e, t0=t0, rb=rb, PHb=PHb: e.tensor_tensor(
                        rfin.t[:, t0:t0+512], rb.t[:, t0:t0+512], PHb.ap, ALU.add),
                     reads=[rb, PHb], writes=[rfin])
                p.mm(LR[th].ap, C["ones"].t[:, 0:1], rfin.t[:, t0:t0+512],
                     LR[th], reads=[C["ones"], rfin],
                     start=(m == 0), stop=(m == NDC - 1))
                p.op("S", lambda e, t0=t0: e.activation(sqb.ap,
                                                        rfin.t[:, t0:t0+512],
                                                        AF.Square),
                     reads=[rfin], writes=[sqb])
                p.mm(LR[2+th].ap, C["onesb"].t[:, 0:1], sqb.ap,
                     LR[2+th], reads=[C["onesb"], sqb],
                     start=(m == 0), stop=(m == NDC - 1))
            p.dma("Yd", SC["r"].t[m, :, :], rfin.ap, reads=[rfin], writes=[SC["r"]])
            if tap_rf is not None:
                p.dma("Yd", tap_rf.t[m, :, :], rfin.ap, reads=[rfin], writes=[tap_rf])

        for th in range(2):
            p.op("V", lambda e, th=th: e.tensor_copy(LS[:, 512*th:512*(th+1)],
                                                     LR[th].ap),
                 reads=[LR[th]], writes=[rows])
            p.op("V", lambda e, th=th: e.tensor_copy(LQ[:, 512*th:512*(th+1)],
                                                     LR[2+th].ap),
                 reads=[LR[2+th]], writes=[rows])
        p.op("V", lambda e: e.tensor_scalar(MR, LS, 1.0 / D, None, ALU.mult),
             reads=[rows], writes=[rows])
        p.op("V", lambda e: e.tensor_scalar(LQ, LQ, 1.0 / D, None, ALU.mult),
             reads=[rows], writes=[rows])
        p.op("V", lambda e: e.tensor_tensor(R1, MR, MR, ALU.mult),
             reads=[rows], writes=[rows])
        p.op("V", lambda e: e.tensor_tensor(LQ, LQ, R1, ALU.subtract),
             reads=[rows], writes=[rows])
        p.op("S", lambda e: e.activation(R1, LQ, AF.Ln, bias=C["epsb"].t[0:1, 1:2]),
             reads=[rows, C["epsb"]], writes=[rows])
        p.op("S", lambda e: e.activation(R1, R1, AF.Exp, scale=-0.5),
             reads=[rows], writes=[rows])
        bcast(MR, mb)
        bcast(R1, bco)
        for m in range(NDC):
            rb = xc[m % 2]
            p.dma("Yd", rb.ap, SC["r"].t[m, :, :], reads=[SC["r"]], writes=[rb])
            p.op("G", lambda e, rb=rb: e.tensor_tensor(scr.ap, rb.ap, mb.ap,
                                                       ALU.subtract),
                 reads=[rb, mb], writes=[scr])
            p.op("V", lambda e: e.tensor_tensor(x1f.ap, scr.ap, bco.ap, ALU.mult),
                 reads=[scr, bco], writes=[x1f])
            p.op("S", lambda e: e.copy(x1b.ap, x1f.ap), reads=[x1f], writes=[x1b])
            p.dma("Yd", SC["x1f"].t[m, :, :], x1f.ap, reads=[x1f], writes=[SC["x1f"]])
            p.dma("Yd", SC["x1b"].t[m, :, :], x1b.ap, reads=[x1b], writes=[SC["x1b"]])
            if tap_x1 is not None:
                p.dma("Yd", tap_x1.t[m, :, :], x1f.ap, reads=[x1f], writes=[tap_x1])
        p.barrier()



def _p4(p, nc, I, C, SC, ln2s, tap):
    tap_w = tap("gatew", [2, 128, 32], F32)
    tap_r2 = tap("r2", [NDC, 128, T], F32)
    NF4 = DFF // 128
    with ExitStack() as e4:
        x1h = [p.sb(f"p4x{k}", [128, 512], BF16, es=e4) for k in range(NDC)]
        wrb = p.sb("p4wr", [128, NDC * NEXP], BF16, es=e4)
        bdb = p.sb("p4bd", [128, D], BF16, es=e4)
        atile = [p.sb(f"p4at{i}", [128, NDC * 128], BF16, es=e4) for i in range(1)]
        t_sb = p.sb("p4t", [16, 16 * 512], BF16, es=e4)
        gbp = [p.sb(f"p4gb{f}", [128, 1024], BF16, es=e4) for f in range(NF4 // 2)]
        ubp = [p.sb(f"p4ub{f}", [128, 1024], BF16, es=e4) for f in range(NF4 // 2)]
        def gb_ap(fs):
            return gbp[fs // 2].t[:, 512 * (fs % 2):512 * (fs % 2) + 512]
        def ub_ap(fs):
            return ubp[fs // 2].t[:, 512 * (fs % 2):512 * (fs % 2) + 512]
        bgc = p.sb("p4bgc", [16, DFF], BF16, es=e4)
        buc = p.sb("p4buc", [16, DFF], BF16, es=e4)
        adb = [p.sb(f"p4ad{i}", [128, NF4 * R], BF16, es=e4) for i in range(1)]
        wdc = [p.sb(f"p4wd{i}", [128, 22 * 128], BF16, es=e4) for i in range(2)]
        lgs = p.sb("p4lg", [128, 32], F32, es=e4)
        wgt = p.sb("p4wt", [128, 32], F32, es=e4)
        wtr = p.sb("p4wtr", [8, 512], F32, es=e4)
        wrows = p.sb("p4wrow", [1, 512], F32, es=e4)
        wsb = p.sb("p4wsb", [16, 512], F32, es=e4)
        dtt = p.sb("p4dtt", [16, 512], BF16, es=e4)
        dT_all = p.sb("p4dta", [128, 512], BF16, es=e4)
        scr1 = p.sb("p4s1", [128, 512], F32, es=e4)
        scr2 = p.sb("p4s2", [128, 512], F32, es=e4)
        scr3 = None
        heb = p.sb("p4he", [128, 512], BF16, es=e4)
        scr3 = scr1
        xfc = [p.sb("p4xf0", [128, 512], F32, es=e4), scr2]
        r2c = p.sb("p4r2", [128, 512], F32, es=e4)
        col = p.sb("p4col", [128, 4], F32, es=e4)

        p.dma("Gd", wrb.t[:].rearrange("p (k c) -> p k c", c=NEXP),
              I["wr"].ap.rearrange("(k p) c -> p k c", p=128), writes=[wrb])
        p.dma("Gd", bdb.ap, I["bd"].ap, writes=[bdb])

        for hi, t0 in enumerate((0, 512)):
            for k in range(NDC):
                p.dma("Yd", x1h[k].ap, SC["x1b"].t[k, :, t0:t0 + 512],
                      reads=[SC["x1b"]], writes=[x1h[k]])

            with ExitStack() as eR:
                PT4 = p.ps(f"p4pt{hi}", [128, 32], F32, es=eR)
                TRP = p.ps(f"p4trp{hi}", [8, 512], F32, es=eR)
                wrv = wrb.t[:].rearrange("p (k c) -> p k c", c=NEXP)
                for tt in range(4):
                    p.mm_group(PT4, [(PT4.t[:, 8 * tt:8 * (tt + 1)],
                                      x1h[k].t[:, 128 * tt:128 * (tt + 1)],
                                      wrv[:, k, :]) for k in range(NDC)],
                               reads=[wrb] + x1h)
                p.op("V", lambda e: e.tensor_copy(lgs.ap, PT4.ap),
                     reads=[PT4], writes=[lgs])
                for tt in range(4):
                    sl = lgs.t[:, 8 * tt:8 * (tt + 1)]
                    wl = wgt.t[:, 8 * tt:8 * (tt + 1)]
                    p.op("V", lambda e, sl=sl: e.reduce_max(col.t[:, 0:1], sl, axis=AX.X),
                         reads=[lgs], writes=[col])
                    p.op("V", lambda e: e.tensor_scalar_mul(col.t[:, 3:4],
                                                            col.t[:, 0:1], -1.0),
                         reads=[col], writes=[col])
                    p.op("S", lambda e, sl=sl, wl=wl: e.activation(
                            wl, sl, AF.Exp, bias=col.t[:, 3:4],
                            accum_out=col.t[:, 2:3]),
                         reads=[lgs, col], writes=[wgt, col])
                    p.op("V", lambda e: e.reciprocal(col.t[:, 2:3], col.t[:, 2:3]),
                         reads=[col], writes=[col])
                    p.op("V", lambda e, sl=sl: e.tensor_scalar(
                            scr1.t[:, 0:8], sl, col.t[:, 0:1], None, ALU.is_equal),
                         reads=[lgs, col], writes=[scr1])
                    p.op("V", lambda e, sl=sl: e.scalar_tensor_tensor(
                            scr1.t[:, 0:8], scr1.t[:, 0:8], NEG, sl,
                            ALU.mult, ALU.add),
                         reads=[scr1, lgs], writes=[scr1])
                    p.op("V", lambda e: e.reduce_max(col.t[:, 1:2], scr1.t[:, 0:8],
                                                     axis=AX.X),
                         reads=[scr1], writes=[col])
                    p.op("V", lambda e, sl=sl: e.tensor_scalar(
                            scr1.t[:, 0:8], sl, col.t[:, 1:2], None, ALU.is_ge),
                         reads=[lgs, col], writes=[scr1])
                    p.op("V", lambda e, wl=wl: e.scalar_tensor_tensor(
                            wl, wl, col.t[:, 2:3], scr1.t[:, 0:8],
                            ALU.mult, ALU.mult),
                         reads=[wgt, col, scr1], writes=[wgt])
                if tap_w is not None:
                    p.dma("Yd", tap_w.t[hi, :, :], wgt.ap, reads=[wgt], writes=[tap_w])
                for tt in range(4):
                    p.op("T", lambda e, tt=tt: e.transpose(
                            TRP.t[:, 128 * tt:128 * (tt + 1)],
                            wgt.t[:, 8 * tt:8 * (tt + 1)], C["identf"].ap),
                         reads=[wgt, C["identf"]], writes=[TRP])
                p.op("V", lambda e: e.tensor_scalar_mul(wtr.ap, TRP.t[0:8, :],
                                                        SCALING),
                     reads=[TRP], writes=[wtr])
                p.barrier(("T", "V", "S"))

            with ExitStack() as eE:
                PAW = [p.ps(f"p4pa{hi}_{i}", [128, 1024], F32, es=eE)
                       for i in range(2)]
                PUW = p.ps(f"p4pu{hi}", [128, 1024], F32, es=eE)
                PD = p.ps(f"p4pd{hi}", [16, 512], F32, es=eE)
                for ti in range(8):
                    p.dma("Gd", atile[0].t[:].rearrange("p (k c) -> p k c", c=128),
                          I["a_all"].t[:, 128 * ti:128 * (ti + 1)]
                          .rearrange("(k p) c -> p k c", p=128),
                          writes=[atile[0]])
                    av = atile[0].t[:].rearrange("p (k c) -> p k c", c=128)
                    PAb = PAW[ti % 2]
                    p.mm_group(PAb, [(PAb.t[:, 0:512], av[:, k, :], x1h[k].ap)
                                     for k in range(NDC)], reads=[atile[0]] + x1h)
                    base_slot = 0 if ti < 4 else 8
                    uu = ti % 4
                    for half_e, pbase in ((0, 0), (1, 64)):
                        slot = base_slot + 2 * uu + half_e
                        p.op("S", lambda e, slot=slot, pbase=pbase, PAb=PAb:
                             e.activation(t_sb.t[:, 512 * slot:512 * (slot + 1)],
                                          PAb.t[pbase:pbase + 16, 0:512], AF.Copy,
                                          scale=SCALING),
                             reads=[PAb], writes=[t_sb])

                with ExitStack() as eB:
                    wgc = [p.sb(f"p4wg{hi}_{i}", [128, NDC * 128], BF16, es=eB)
                           for i in range(2)]
                    wuc = [p.sb(f"p4wu{hi}_{i}", [128, NDC * 128], BF16, es=eB)
                           for i in range(2)]
                    def ldg(fs, b):
                        for nm, dst in (("wg", wgc[b]), ("wu", wuc[b])):
                            p.dma("Gd", dst.t[:].rearrange("p (k c) -> p k c", c=128),
                                  I[nm].t[:, 128 * fs:128 * (fs + 1)]
                                  .rearrange("(k p) c -> p k c", p=128), writes=[dst])
                    ldg(0, 0)
                    for fs in range(NF4):
                        if fs + 1 < NF4:
                            ldg(fs + 1, (fs + 1) % 2)
                        gv = wgc[fs % 2].t[:].rearrange("p (k c) -> p k c", c=128)
                        uv = wuc[fs % 2].t[:].rearrange("p (k c) -> p k c", c=128)
                        PAb = PAW[fs % 2]
                        p.mm_group(PAb, [(PAb.t[:, 0:512], gv[:, k, :], x1h[k].ap)
                                         for k in range(NDC)], reads=[wgc[fs % 2]] + x1h)
                        p.op("S", lambda e, fs=fs, PAb=PAb: e.copy(
                                gb_ap(fs), PAb.t[:, 0:512]),
                             reads=[PAb], writes=[gbp[fs // 2]])
                        p.mm_group(PUW, [(PUW.t[:, 0:512], uv[:, k, :], x1h[k].ap)
                                         for k in range(NDC)], reads=[wuc[fs % 2]] + x1h)
                        p.op("S", lambda e, fs=fs: e.copy(ub_ap(fs), PUW.t[:, 0:512]),
                             reads=[PUW], writes=[ubp[fs // 2]])
                    p.barrier(("T", "V", "S", "Gd"))

                def ldcombo(ex, b):
                    p.dma("Gd", bgc.ap, I["bg"].t[ex, :, :], writes=[bgc])
                    p.dma("Gd", buc.ap, I["bu"].t[ex, :, :], writes=[buc])
                    p.dma("Gd", adb[b].t[:].rearrange("p (f r) -> p f r", r=R),
                          I["ad"].t[ex, :, :].rearrange("(f p) r -> p f r", p=128),
                          writes=[adb[b]])
                NP4 = NF4 // 2
                with ExitStack() as eX:
                    s1 = [p.sb(f"p4s1{hi}_{i}", [128, 1024], BF16, es=eX)
                          for i in range(2)]
                    s2 = [p.sb(f"p4s2{hi}_{i}", [128, 1024], BF16, es=eX)
                          for i in range(2)]
                    s3 = [p.sb(f"p4s3{hi}_{i}", [128, 1024], BF16, es=eX)
                          for i in range(2)]
                    hebs = [p.sb(f"p4he{hi}_{i}", [128, 1024], BF16, es=eX)
                            for i in range(2)]
                    bbscr = p.sb(f"p4bbs{hi}", [128, 1024], BF16, es=eX)
                    for ex in range(NEXP):
                        ldcombo(ex, 0)
                        ab = adb[0].t[:].rearrange("p (f r) -> p f r", r=R)
                        tg = t_sb.t[:, 512 * ex:512 * (ex + 1)]
                        tu = t_sb.t[:, 512 * (8 + ex):512 * (9 + ex)]

                        def mms_g(j):
                            par = j % 2
                            for h2 in range(2):
                                fs = 2 * j + h2
                                p.mm(PAW[par].t[:, 512 * h2:512 * (h2 + 1)],
                                     bgc.t[:, 128 * fs:128 * (fs + 1)],
                                     tg, PAW[par], reads=[bgc, t_sb])

                        def mms_u(j):
                            for h2 in range(2):
                                fs = 2 * j + h2
                                p.mm(PUW.t[:, 512 * h2:512 * (h2 + 1)],
                                     buc.t[:, 128 * fs:128 * (fs + 1)],
                                     tu, PUW, reads=[buc, t_sb])

                        def elem(j):
                            par = j % 2
                            p.op("V", lambda e, j=j, par=par: e.tensor_tensor(
                                    s1[par].ap, gbp[j].ap, PAW[par].ap, ALU.add),
                                 reads=[gbp[j], PAW[par]], writes=[s1[par]])
                            p.op("S", lambda e, par=par: e.activation(
                                    s2[par].ap, s1[par].ap, AF.Silu),
                                 reads=[s1[par]], writes=[s2[par]])
                            p.op("V", lambda e, j=j, par=par: e.tensor_tensor(
                                    s3[par].ap, ubp[j].ap, PUW.ap, ALU.add),
                                 reads=[ubp[j], PUW], writes=[s3[par]])
                            p.op("V", lambda e, par=par: e.tensor_tensor(
                                    hebs[par].ap, s2[par].ap, s3[par].ap, ALU.mult),
                                 reads=[s2[par], s3[par]], writes=[hebs[par]])
                            for h2 in range(2):
                                fs = 2 * j + h2
                                p.mm(PD.ap, ab[:, fs, :],
                                     hebs[par].t[:, 512 * h2:512 * (h2 + 1)], PD,
                                     reads=[adb[0], hebs[par]],
                                     start=(fs == 0), stop=(fs == NF4 - 1))

                        def bb_make(j):
                            p.op("S", lambda e, j=j: e.activation(
                                    bbscr.ap, gbp[j].ap, AF.Silu),
                                 reads=[gbp[j]], writes=[bbscr])
                            p.op("V", lambda e, j=j: e.tensor_tensor(
                                    gbp[j].ap, bbscr.ap, ubp[j].ap, ALU.mult),
                                 reads=[bbscr, ubp[j]], writes=[gbp[j]])
                        mms_g(0)
                        mms_u(0)
                        for j in range(1, NP4):
                            mms_g(j)
                            elem(j - 1)
                            mms_u(j)
                            if ex == NEXP - 1 and j >= 2:
                                bb_make(j - 2)
                        elem(NP4 - 1)
                        if ex == NEXP - 1:
                            bb_make(NP4 - 2)
                            bb_make(NP4 - 1)
                        for tt in range(4):
                            p.dma("Yd", wrows.t[:, 128 * tt:128 * (tt + 1)],
                                  wtr.t[ex:ex + 1, 128 * tt:128 * (tt + 1)],
                                  reads=[wtr], writes=[wrows])
                        p.mm(PUW.t[0:16, 0:512], C["ones"].t[0:1, 0:16],
                             wrows.ap, PUW, reads=[C["ones"], wrows])
                        p.op("V", lambda e: e.tensor_copy(wsb.ap, PUW.t[0:16, 0:512]),
                             reads=[PUW], writes=[wsb])
                        p.op("V", lambda e: e.tensor_tensor(dtt.ap, PD.ap, wsb.ap,
                                                            ALU.mult),
                             reads=[PD, wsb], writes=[dtt])
                        p.dma("Yd", dT_all.t[16 * ex:16 * (ex + 1), :], dtt.ap,
                              reads=[dtt], writes=[dT_all])

                p.barrier(("T", "V", "S", "G"))

            with ExitStack() as eF:
                PF = [p.ps(f"p4pf{hi}_{i}", [128, 512], F32, es=eF) for i in range(2)]
                LR2 = [p.ps(f"p4lr{hi}_{i}", [1, 512], F32, es=eF) for i in range(2)]
                def ldwd(m, b):
                    p.dma("Gd", wdc[b].t[:].rearrange("p (k c) -> p k c", c=128),
                          I["wd"].t[2816 * b:2816 * (b + 1), 128 * m:128 * (m + 1)]
                          .rearrange("(k p) c -> p k c", p=128), writes=[wdc[b]])
                ldwd(0, 0)
                ldwd(0, 1)
                p.dma("Yd", xfc[0].ap, SC["x1f"].t[0, :, t0:t0 + 512],
                      reads=[SC["x1f"]], writes=[xfc[0]])
                for m in range(NDC):
                    if m + 1 < NDC:
                        p.dma("Yd", xfc[(m + 1) % 2].ap,
                              SC["x1f"].t[m + 1, :, t0:t0 + 512],
                              reads=[SC["x1f"]], writes=[xfc[(m + 1) % 2]])
                    dvA = wdc[0].t[:].rearrange("p (k c) -> p k c", c=128)
                    dvB = wdc[1].t[:].rearrange("p (k c) -> p k c", c=128)
                    PFb = PF[m % 2]
                    p.mm_group(PFb, [(PFb.ap, dvA[:, k, :], gb_ap(k))
                                     for k in range(22)],
                               reads=[wdc[0]] + gbp[0:11], start=True, stop=False)
                    if m + 1 < NDC:
                        ldwd(m + 1, 0)
                    mmsB = [(PFb.ap, dvB[:, k, :], gb_ap(22 + k))
                            for k in range(22)]
                    mmsB.append((PFb.ap, bdb.t[:, 128 * m:128 * (m + 1)],
                                 dT_all.ap))
                    p.mm_group(PFb, mmsB, reads=[wdc[1], bdb, dT_all] + gbp[11:],
                               start=False, stop=True)
                    if m + 1 < NDC:
                        ldwd(m + 1, 1)
                    xb = xfc[m % 2]
                    p.op("V", lambda e, xb=xb, PFb=PFb: e.tensor_tensor(
                            r2c.ap, xb.ap, PFb.ap, ALU.add),
                         reads=[xb, PFb], writes=[r2c])
                    p.mm(LR2[0].ap, C["ones"].t[:, 0:1], r2c.ap, LR2[0],
                         reads=[C["ones"], r2c], start=(m == 0), stop=(m == NDC - 1))
                    p.op("S", lambda e: e.activation(scr1.ap, r2c.ap, AF.Square),
                         reads=[r2c], writes=[scr1])
                    p.mm(LR2[1].ap, C["ones"].t[:, 0:1], scr1.ap, LR2[1],
                         reads=[C["ones"], scr1], start=(m == 0), stop=(m == NDC - 1))
                    p.dma("Yd", SC["r2"].t[m, :, t0:t0 + 512], r2c.ap,
                          reads=[r2c], writes=[SC["r2"]])
                    if tap_r2 is not None:
                        p.dma("Yd", tap_r2.t[m, :, t0:t0 + 512], r2c.ap,
                              reads=[r2c], writes=[tap_r2])
                p.op("V", lambda e, t0=t0: e.tensor_copy(
                        ln2s.t[:, t0:t0 + 512], LR2[0].ap),
                     reads=[LR2[0]], writes=[ln2s])
                p.op("V", lambda e, t0=t0: e.tensor_copy(
                        ln2s.t[:, T + t0:T + t0 + 512], LR2[1].ap),
                     reads=[LR2[1]], writes=[ln2s])
                p.barrier(("T", "V", "S"))
        p.barrier()



def _p5(p, nc, I, C, SC, ln2s, out, tap):
    with ExitStack() as e5:
        rows = p.sb("p5rows", [1, 3 * T], F32, es=e5)
        MR = rows.t[:, 0:T]; VR = rows.t[:, T:2*T]; RS = rows.t[:, 2*T:3*T]
        mb = p.sb("p5mb", [128, T], F32, es=e5)
        bco = p.sb("p5bco", [128, T], F32, es=e5)
        rc = [p.sb(f"p5rc{i}", [128, T], F32, es=e5) for i in range(2)]
        oc = p.sb("p5oc", [128, T], F32, es=e5)
        scr = p.sb("p5scr", [128, T], F32, es=e5)
        BC = p.ps("p5bc", [128, 512], F32, es=e5)

        p.op("V", lambda e: e.tensor_scalar(MR, ln2s.t[:, 0:T], 1.0 / D, None,
                                            ALU.mult),
             reads=[ln2s], writes=[rows])
        p.op("V", lambda e: e.tensor_scalar(VR, ln2s.t[:, T:2*T], 1.0 / D, None,
                                            ALU.mult),
             reads=[ln2s], writes=[rows])
        p.op("V", lambda e: e.tensor_tensor(RS, MR, MR, ALU.mult),
             reads=[rows], writes=[rows])
        p.op("V", lambda e: e.tensor_tensor(VR, VR, RS, ALU.subtract),
             reads=[rows], writes=[rows])
        p.op("S", lambda e: e.activation(RS, VR, AF.Ln, bias=C["epsb"].t[0:1, 1:2]),
             reads=[rows, C["epsb"]], writes=[rows])
        p.op("S", lambda e: e.activation(RS, RS, AF.Exp, scale=-0.5),
             reads=[rows], writes=[rows])
        for src_sl, dst in ((MR, mb), (RS, bco)):
            for th in range(2):
                p.mm(BC.ap, C["ones"].t[0:1, :], src_sl[:, 512*th:512*(th+1)],
                     BC, reads=[C["ones"], rows])
                p.op("V", lambda e, th=th, dst=dst: e.tensor_copy(
                        dst.t[:, 512*th:512*(th+1)], BC.ap),
                     reads=[BC], writes=[dst])
        for m in range(NDC):
            rb = rc[m % 2]
            p.dma("Yd", rb.ap, SC["r2"].t[m, :, :], reads=[SC["r2"]], writes=[rb])
            p.op("V", lambda e, rb=rb: e.tensor_tensor(scr.ap, rb.ap, mb.ap,
                                                       ALU.subtract),
                 reads=[rb, mb], writes=[scr])
            p.op("V", lambda e: e.tensor_tensor(oc.ap, scr.ap, bco.ap, ALU.mult),
                 reads=[scr, bco], writes=[oc])
            p.dma("Yd", out.t[m, :, :], oc.ap, reads=[oc], writes=[out])
        p.barrier()



def kernel(**inputs):
    shared, per_core = prep_inputs(inputs)
    nc, _ = build(phases=("P0", "P1", "P2", "P3", "P4", "P5"))
    in_maps = [dict(shared, **pc) for pc in per_core]
    res = run_bass_kernel_spmd(nc, in_maps, core_ids=list(range(NCORES)))
    outs = [np.asarray(res.results[c]["out"]).transpose(2, 0, 1).reshape(T, D)
            for c in range(NCORES)]
    return np.concatenate(outs, axis=0).reshape(1, S, D).astype(np.float32)
